# version 02
# baseline (speedup 1.0000x reference)
import contextlib
import numpy as np
import concourse.bass as bass
import concourse.mybir as mybir
from concourse.bass import IndirectOffsetOnAxis
from concourse.bass_utils import run_bass_kernel_spmd

F32 = mybir.dt.float32
I32 = mybir.dt.int32
AF = mybir.ActivationFunctionType
ALU = mybir.AluOpType
AX = mybir.AxisListType

T = 2048
D = 1024
NT = 16
LN32 = float(np.log(32.0))
EPS = 1e-5
DEBUG = True
BUILD_UPTO = 6

PERM = [0, 2, 4, 6, 8, 10, 12, 14, 1, 3, 5, 7, 9, 11, 13, 15]
IPERM = [PERM.index(h) for h in range(16)]


class Sem:
    __slots__ = ("h", "n")

    def __init__(self, nc, name):
        self.h = nc.alloc_semaphore(name)
        self.n = 0

    def inc(self, ins, k=1):
        ins.then_inc(self.h, k)
        self.n += k
        return self.n

    def wait(self, eng, val=None):
        eng.wait_ge(self.h, self.n if val is None else val)


def consts_arrays():
    c = {}
    c["IDENT"] = np.eye(128, dtype=np.float32)
    c["LINC64"] = (np.triu(np.ones((128, 128), np.float32)) / np.float32(64.0))
    c["STRICTLOW"] = np.triu(np.ones((128, 128), np.float32), 1)
    c["TRIL"] = np.triu(np.ones((128, 128), np.float32))
    sp = np.zeros((16, 128), np.float32)
    for h in range(16):
        sp[h, (h % 2) * 64:(h % 2) * 64 + 64] = 1.0
    c["SELPAR"] = sp
    dc = np.zeros((16, 8), np.float32)
    for h in range(16):
        dc[h, h // 2] = 1.0
    c["DIAGC"] = dc
    r16 = np.zeros((16, 256), np.float32)
    for h in range(16):
        r16[h, h::16] = 1.0
    c["REPL16"] = r16
    c["ONES16x128"] = np.ones((16, 128), np.float32)
    c["ONESROW"] = np.ones((1, 512), np.float32)
    c["ONESCOL"] = np.ones((128, 64), np.float32)
    c["STRICTLOW8"] = np.triu(np.ones((8, 8), np.float32), 1)
    c["DIAG8"] = np.eye(8, dtype=np.float32)
    c["SLOTF"] = np.add.outer(np.arange(128, dtype=np.float32),
                              128.0 * np.arange(16, dtype=np.float32))
    c["R2049"] = (2049.0 * np.arange(8, dtype=np.float32)).reshape(8, 1)
    return c


def mm_accum(t, psum, pairs):
    ins = None
    n = len(pairs)
    for i, (lhs, rhs) in enumerate(pairs):
        ins = t.matmul(psum, lhs, rhs, start=(i == 0), stop=(i == n - 1))
    return ins


class B:
    ENG = {"PE": "tensor", "SA": "scalar", "DV": "vector", "GP": "gpsimd",
           "SY": "sync", "CC": "gpsimd"}

    def __init__(self, nc, block, ap, sb, pb, debug):
        self.nc, self.block, self.ap, self.sb, self.pb = nc, block, ap, sb, pb
        self.debug = debug
        self.PE = Sem(nc, "sPE")
        self.SA = Sem(nc, "sSA")
        self.DV = Sem(nc, "sDV")
        self.GP = Sem(nc, "sGP")
        self.SY = Sem(nc, "sSY")
        self.CC = Sem(nc, "sCC")
        self.all = dict(PE=self.PE, SA=self.SA, DV=self.DV, GP=self.GP,
                        SY=self.SY, CC=self.CC)

    CVALS = {}

    def cv(self, val, p=128):
        key = float(val)
        if key not in B.CVALS:
            raise KeyError(f"const {val} not in pool")
        return self.sb["cpool"][0:p, B.CVALS[key]:B.CVALS[key] + 1]

    def snap(self):
        return {k: s.n for k, s in self.all.items()}

    def wait_snap(self, eng, snap, skip=()):
        for k, v in snap.items():
            if v and k not in skip:
                self.all[k].wait(eng, v)

    def sched(self, steps):
        base = self.snap()
        cnt = dict(base)
        after = []
        eng_of = []
        for e, inc, fn, deps in steps:
            cnt[e] += inc
            after.append(dict(cnt))
            eng_of.append(e)
        by_sec = {}
        for idx, (e, inc, fn, deps) in enumerate(steps):
            by_sec.setdefault(self.ENG[e], []).append((idx, e, inc, fn, deps))
        for sec, items in by_sec.items():
            def make(items=items, sec=sec):
                def emitter(eng):
                    first = True
                    for idx, e, inc, fn, deps in items:
                        if first:
                            for k, v in base.items():
                                if v:
                                    self.all[k].wait(eng, v)
                            first = False
                        for d in deps:
                            de = eng_of[d]
                            if self.ENG[de] == sec and de not in ("GP", "SY", "CC"):
                                continue
                            self.all[de].wait(eng, after[d][de])
                        ins = fn(eng)
                        if inc:
                            ins.then_inc(self.all[e].h, inc)
                        if sec == "vector":
                            eng.drain()
                return emitter
            getattr(self.block, sec)(make())
        for e, inc, fn, deps in steps:
            self.all[e].n += inc

    def serial(self, steps):
        base = self.snap()
        cnt = dict(base)
        plan = []
        for e, inc, fn in steps:
            plan.append((e, inc, fn, dict(cnt)))
            cnt[e] += inc
        by_sec = {}
        for idx, (e, inc, fn, pre) in enumerate(plan):
            by_sec.setdefault(self.ENG[e], []).append((idx, e, inc, fn, pre))
        for sec, items in by_sec.items():
            def make(items=items, sec=sec):
                def emitter(eng):
                    first = True
                    for idx, e, inc, fn, pre in items:
                        if first:
                            for k, v in base.items():
                                if v:
                                    self.all[k].wait(eng, v)
                            first = False
                        for k, v in pre.items():
                            if v > base[k]:
                                if self.ENG[k] != sec or k in ("GP", "SY", "CC"):
                                    self.all[k].wait(eng, v)
                        ins = fn(eng)
                        if inc:
                            ins.then_inc(self.all[e].h, inc)
                        if sec == "vector":
                            eng.drain()
                return emitter
            getattr(self.block, sec)(make())
        for e, inc, fn in steps:
            self.all[e].n += inc


for _i, _v in enumerate([1024.0 * EPS, LN32, -0.5, -1.0, 1.0, 0.125, 1.0 / 32.0, -1.0 / 1024.0, 0.0, EPS, -10.0, 1e-20, -1e9, 0.125, 2048.0]):
    B.CVALS[float(_v)] = _i


def build_program(debug=DEBUG, upto=BUILD_UPTO):
    nc = bass.Bass("TRN2", target_bir_lowering=False)
    ap = {}

    def din(name, shape, dtype=F32):
        ap[name] = nc.dram_tensor(name, list(shape), dtype, kind="ExternalInput").ap()

    def dout(name, shape, dtype=F32):
        ap[name] = nc.dram_tensor(name, list(shape), dtype, kind="ExternalOutput").ap()

    def dint(name, shape, addr_space=None):
        kw = {"addr_space": addr_space} if addr_space else {}
        ap[name] = nc.dram_tensor(name, list(shape), F32, kind="Internal", **kw).ap()

    din("x_in", (T, D))
    for w in ["WqT0", "WkT0", "WvT0", "WbT0", "WaT0"]:
        din(w, (D, D))
    din("b0c", (128, 5 * 8))
    din("b0r", (1, 3 * D))
    din("gwT0", (D, 8))
    din("up_w", (8, D, D))
    din("up_bc", (128, 8 * 8))
    din("down_w", (8, D, D))
    din("down_br", (1, 8 * D))
    din("sh_up_w", (D, D))
    din("sh_up_bc", (128, 8))
    din("sh_down_w", (D, D))
    din("sh_down_br", (1, D))
    din("band", (1, T))
    for w in ["WqT1", "WkT1", "WvT1", "WoT1"]:
        din(w, (D, D))
    din("b1c", (128, 4 * 8))
    din("b1r", (1, 4 * D))
    din("g1r", (1, D))
    din("be1r", (1, D))
    din("gwT1", (D, 8))
    din("lngc", (128, 8))
    din("lnbc", (128, 8))
    din("wi_e", (D, 4096))
    din("wo_e", (4096, D))
    din("metaP", (128, 8))
    din("oh8", (8, 8))

    CA = consts_arrays()
    for k, v in CA.items():
        ap["c_" + k] = nc.inline_tensor(np.ascontiguousarray(v), "c_" + k).ap()

    dout("x4_out", (T, D))
    if debug:
        dout("d_x1", (T, D))
        dout("d_x2", (T, D))
        dout("d_x3", (T, D))
        dout("d_xln0T", (NT, 8, 128, 128))
        dout("d_beta", (T, D))
        dout("d_qspill", (NT, 128, D))
        dout("d_delta", (NT, 128, 512))
        dout("d_lmsum", (128, NT * 16))
        dout("d_bmean", (128, NT * 16))
        dout("d_curve", (128, NT * 16))
        dout("d_icurve", (128, NT * 16))
        dout("d_state", (128, 512))
        dout("d_sinh", (128, 512))
        dout("d_cumdec", (16, 16))
        dout("d_stats", (128, 6 * NT))
        dout("d_LG", (128, 128))
        dout("d_OH", (128, 128))
        dout("d_RK", (128, 128))
        dout("d_wcol", (128, 16))
        dout("d_crow", (1, 1032))
        dout("d_ixf", (128, 16))
        dout("d_wc2", (128, 16))
        dout("d_tkf", (128, 16))
        dout("d_dT", (128, 8 * T))
        dout("d_rs", (T, D))

    dint("xln0T_h", (NT, 8, 128, 128))
    dint("beta_h", (T, D))
    dint("x1_h", (T, D))
    dint("qspill_h", (NT, 128, D))
    dint("delta_h", (NT, 128, 512))
    dint("xln1T_h", (NT, 8, 128, 128))
    dint("yacc_h", (T, D))
    dint("x2_h", (T, D))
    dint("x3_h", (T, D))
    dint("ctxT_h", (16, 64, T))
    dint("q1_h", (8, 128, T))
    dint("g_h", (32, 128, T))
    dint("cc_st_in", (128 * 512,))
    dint("cc_st_out", (2 * 128 * 512,))
    dint("cc_k_in", (8 * 128 * T,))
    dint("cc_k_out", (2 * 8 * 128 * T,))
    dint("ccvin", (T * D,))
    dint("ccvout", (2 * T * D,))
    dint("cc_disp_in", ((T + 1) * 1032,))
    ap["cc_disp_out"] = nc.dram_tensor("cc_disp_out", [8 * (T + 1) * 1032], F32,
                                       kind="Internal", addr_space="Shared").ap()
    dint("cc_rs_in", ((8 * T + 1) * D,))
    dint("cc_rs_out", (T * D,))

    sb = {}

    def salloc(name, shape, dtype=F32):
        sb[name] = nc.alloc_sbuf_tensor(name, list(shape), dtype).ap()
        return sb[name]

    for k, v in CA.items():
        salloc("k" + k, v.shape)
    salloc("metaPs", (128, 8))
    salloc("oh8s", (8, 8))
    salloc("b0cS", (128, 40))
    salloc("b0cs", (128, 16))
    salloc("stats", (128, 6 * NT))
    for nm in ["rstd2", "nmr2", "rstd3", "nmr3", "sc3p", "nb3p"]:
        salloc(nm, (128, NT))
    salloc("lmsum", (128, NT * 16))
    salloc("bmean", (128, NT * 16))
    salloc("curve", (128, NT * 16))
    salloc("icurve", (128, NT * 16))
    salloc("totb", (16, 16))
    salloc("curvec", (128, NT * 16))
    salloc("asumC", (16, 16))
    salloc("cumdec", (16, 16))
    salloc("stateAll", (128, 512))
    salloc("comb", (128, NT * 8))
    salloc("combT", (8, T))
    salloc("cmcol", (128, 16))
    salloc("Sinh", (128, 512))
    salloc("scr", (128, D))
    salloc("scr2", (128, 64))
    salloc("cpool", (128, 16))

    pbig = nc.alloc_psum_tensor("pbig", [128, 4096], F32).ap()
    pb = [pbig[:, 512 * i:512 * (i + 1)] for i in range(8)]
    sb["PBIG"] = pbig

    with nc.Block() as block:
        b = B(nc, block, ap, sb, pb, debug)
        emit_preamble(b)
        emit_p0(b)
        emit_p1(b)
        emit_p2(b)
        if upto >= 4:
            emit_p3(b)
            emit_p4(b)
        if upto >= 5:
            emit_p5(b)
        if upto >= 6:
            emit_p6(b)
        emit_final(b, upto)
    return nc


def emit_preamble(b):
    nc, block, ap, sb = b.nc, b.block, b.ap, b.sb
    names = list(consts_arrays().keys())

    @block.sync
    def _(s):
        for k in names:
            b.SY.inc(s.dma_start(out=sb["k" + k], in_=ap["c_" + k][:, :]), 16)
        b.SY.inc(s.dma_start(out=sb["metaPs"], in_=ap["metaP"][:, :]), 16)
        b.SY.inc(s.dma_start(out=sb["oh8s"], in_=ap["oh8"][:, :]), 16)
        b.SY.inc(s.dma_start(out=sb["b0cS"], in_=ap["b0c"][:, :]), 16)

    @block.vector
    def _(v):
        b.SY.wait(v)
        v.memset(sb["stateAll"], 0.0)
        v.memset(sb["Sinh"], 0.0)
        for val, idx in B.CVALS.items():
            v.memset(sb["cpool"][:, idx:idx + 1], float(val))
        b.DV.inc(v.tensor_scalar(sb["b0cs"], sb["b0cS"][:, 0:16], 0.125, None, ALU.mult))


def emit_p0(b):
    nc, block, ap, sb, pb = b.nc, b.block, b.ap, b.sb, b.pb
    es = contextlib.ExitStack()
    xb = [es.enter_context(nc.sbuf_tensor(f"p0x{i}", [128, D], F32)).ap() for i in range(2)]
    xlnb = [es.enter_context(nc.sbuf_tensor(f"p0l{i}", [128, D], F32)).ap() for i in range(2)]
    xtb = [es.enter_context(nc.sbuf_tensor(f"p0t{i}", [128, D], F32)).ap() for i in range(2)]
    S0 = b.snap()
    st = sb["stats"]

    @block.sync
    def _(s):
        b.wait_snap(s, S0)
        for n in range(NT):
            if n >= 2:
                s.wait_ge(b.SA.h, S0["SA"] + 12 * (n - 2) + 4)
            b.SY.inc(s.dma_start(out=xb[n % 2],
                                 in_=ap["x_in"][128 * n:128 * (n + 1), :]), 16)

    @block.vector
    def _(v):
        b.wait_snap(v, S0)
        for n in range(NT):
            c0 = 6 * n
            v.wait_ge(b.SY.h, S0["SY"] + 16 * (n + 1))
            b.DV.inc(v.reduce_sum(st[:, c0:c0 + 1], xb[n % 2], axis=AX.X))
            v.drain()
            v.wait_ge(b.SA.h, S0["SA"] + 12 * n + 2)
            b.DV.inc(v.tensor_tensor(st[:, c0 + 3:c0 + 4], st[:, c0 + 1:c0 + 2],
                                     st[:, c0 + 2:c0 + 3], ALU.subtract))
            v.drain()
            v.wait_ge(b.SA.h, S0["SA"] + 12 * n + 3)
            b.DV.inc(v.scalar_tensor_tensor(st[:, c0 + 5:c0 + 6], st[:, c0:c0 + 1],
                                            b.cv(-1.0 / 1024.0), st[:, c0 + 4:c0 + 5],
                                            ALU.mult, ALU.mult))
            v.drain()

    @block.scalar
    def _(sc):
        b.wait_snap(sc, S0)
        for n in range(NT):
            c0 = 6 * n
            sc.wait_ge(b.SY.h, S0["SY"] + 16 * (n + 1))
            b.SA.inc(sc.activation(out=sb["scr"], in_=xb[n % 2], func=AF.Square,
                                   accum_out=st[:, c0 + 1:c0 + 2]))
            sc.wait_ge(b.DV.h, S0["DV"] + 3 * n + 1)
            b.SA.inc(sc.activation(out=st[:, c0 + 2:c0 + 3], in_=st[:, c0:c0 + 1],
                                   func=AF.Square, scale=b.cv(1.0 / 32.0)))
            sc.wait_ge(b.DV.h, S0["DV"] + 3 * n + 2)
            sc.activation(out=st[:, c0 + 4:c0 + 5], in_=st[:, c0 + 3:c0 + 4], func=AF.Ln,
                          bias=b.cv(1024.0 * EPS))
            b.SA.inc(sc.activation(out=st[:, c0 + 4:c0 + 5], in_=st[:, c0 + 4:c0 + 5],
                                   func=AF.Exp, scale=b.cv(-0.5), bias=b.cv(LN32)))
            sc.wait_ge(b.DV.h, S0["DV"] + 3 * n + 3)
            if n >= 2:
                sc.wait_ge(b.PE.h, S0["PE"] + 8 * (n - 1))
            b.SA.inc(sc.activation(out=xlnb[n % 2], in_=xb[n % 2], func=AF.Identity,
                                   scale=st[:, c0 + 4:c0 + 5],
                                   bias=st[:, c0 + 5:c0 + 6]))
            if n >= 2:
                sc.wait_ge(b.GP.h, S0["GP"] + 128 * (n - 1))
            for c in range(8):
                sc.wait_ge(b.PE.h, S0["PE"] + 8 * n + c + 1)
                b.SA.inc(sc.copy(out=xtb[n % 2][:, 128 * c:128 * (c + 1)],
                                 in_=pb[c % 2][:, 0:128]))

    @block.tensor
    def _(t):
        b.wait_snap(t, S0)
        for n in range(NT):
            t.wait_ge(b.SA.h, S0["SA"] + 12 * n + 4)
            for c in range(8):
                gi = 8 * n + c
                if gi >= 2:
                    pn, pc = (gi - 2) // 8, (gi - 2) % 8
                    t.wait_ge(b.SA.h, S0["SA"] + 12 * pn + 4 + pc + 1)
                b.PE.inc(t.transpose(pb[c % 2][:, 0:128],
                                     xlnb[n % 2][:, 128 * c:128 * (c + 1)],
                                     sb["kIDENT"]))

    @block.gpsimd
    def _(g):
        b.wait_snap(g, S0)
        for n in range(NT):
            g.wait_ge(b.SA.h, S0["SA"] + 12 * (n + 1))
            for c in range(8):
                b.GP.inc(g.dma_start(out=ap["xln0T_h"][n, c, :, :],
                                     in_=xtb[n % 2][:, 128 * c:128 * (c + 1)]), 16)
    es.close()


def emit_p1(b):
    nc, block, ap, sb, pb = b.nc, b.block, b.ap, b.sb, b.pb
    es = contextlib.ExitStack()
    Wa = es.enter_context(nc.sbuf_tensor("p1Wa", [128, 8 * D], F32)).ap()
    Wb = es.enter_context(nc.sbuf_tensor("p1Wb", [128, 8 * D], F32)).ap()
    xTb = [es.enter_context(nc.sbuf_tensor(f"p1xT{i}", [128, D], F32)).ap() for i in range(2)]
    acl = es.enter_context(nc.sbuf_tensor("p1acl", [128, D], F32)).ap()
    ebf = es.enter_context(nc.sbuf_tensor("p1ebf", [128, D], F32)).ap()
    spb = es.enter_context(nc.sbuf_tensor("p1spb", [128, D], F32)).ap()
    ab2 = es.enter_context(nc.sbuf_tensor("p1ab2", [128, D], F32)).ap()
    eb2 = es.enter_context(nc.sbuf_tensor("p1eb2", [128, D], F32)).ap()
    bbf = es.enter_context(nc.sbuf_tensor("p1bbf", [128, D], F32)).ap()
    b0one = es.enter_context(nc.sbuf_tensor("p1b0r", [1, 2 * D], F32)).ap()
    S0 = b.snap()

    @block.sync
    def _(s):
        b.wait_snap(s, S0)
        b.SY.inc(s.dma_start(out=Wa, in_=ap["WaT0"].rearrange("(c p) o -> p c o", p=128)), 16)
        b.SY.inc(s.dma_start(out=Wb, in_=ap["WbT0"].rearrange("(c p) o -> p c o", p=128)), 16)
        b.SY.inc(s.dma_start(out=b0one, in_=ap["b0r"][0:1, D:3 * D]), 16)
        for n in range(NT):
            if n >= 2:
                s.wait_ge(b.PE.h, S0["PE"] + 4 * (n - 1))
            for c in range(8):
                b.SY.inc(s.dma_start(out=xTb[n % 2][:, 128 * c:128 * (c + 1)],
                                     in_=ap["xln0T_h"][n, c, :, :]), 16)

    @block.tensor
    def _(t):
        b.wait_snap(t, S0)
        for n in range(NT):
            t.wait_ge(b.SY.h, S0["SY"] + 48 + 128 * (n + 1))
            for wi_, (Wm, brow) in enumerate(((Wa, 1), (Wb, 0))):
                if n >= 1:
                    t.wait_ge(b.DV.h, S0["DV"] + 4 * (n - 1) + (1 if wi_ == 0 else 3))
                for og in range(2):
                    pairs = [(xTb[n % 2][:, 128 * c:128 * (c + 1)],
                              Wm[:, D * c + 512 * og: D * c + 512 * og + 512])
                             for c in range(8)]
                    pairs.append((sb["kONESROW"][0:1, 0:128],
                                  b0one[0:1, D * brow + 512 * og: D * brow + 512 * og + 512]))
                    b.PE.inc(mm_accum(t, pb[2 * wi_ + og], pairs))

    @block.vector
    def _(v):
        b.wait_snap(v, S0)
        for n in range(NT):
            v.wait_ge(b.PE.h, S0["PE"] + 4 * n + 2)
            v.tensor_scalar(acl[:, 0:512], pb[0], 10.0, -10.0, ALU.min, ALU.max)
            b.DV.inc(v.tensor_scalar(acl[:, 512:1024], pb[1], 10.0, -10.0,
                                     ALU.min, ALU.max))
            v.wait_ge(b.SA.h, S0["SA"] + 2 * n + 1)
            b.DV.inc(v.tensor_reduce(sb["lmsum"][:, 16 * n:16 * (n + 1)],
                                     spb.rearrange("p (a d) -> p a d", d=64),
                                     axis=AX.X, op=ALU.add))
            v.wait_ge(b.SA.h, S0["SA"] + 2 * n + 2)
            if n >= 1:
                v.wait_ge(b.GP.h, S0["GP"] + 16 * n)
            v.scalar_tensor_tensor(bbf[:, 0:512], pb[2], b.cv(0.0), eb2[:, 0:512],
                                   ALU.max, ALU.add)
            b.DV.inc(v.scalar_tensor_tensor(bbf[:, 512:1024], pb[3], b.cv(0.0),
                                            eb2[:, 512:1024], ALU.max, ALU.add))
            v.drain()
            v.tensor_reduce(sb["scr2"][:, 0:16], bbf.rearrange("p (a d) -> p a d", d=64),
                            axis=AX.X, op=ALU.add)
            v.drain()
            b.DV.inc(v.tensor_scalar(sb["bmean"][:, 16 * n:16 * (n + 1)],
                                     sb["scr2"][:, 0:16], 1.0 / 64.0, None, ALU.mult))
            v.drain()

    @block.scalar
    def _(sc):
        b.wait_snap(sc, S0)
        for n in range(NT):
            sc.wait_ge(b.DV.h, S0["DV"] + 4 * n + 1)
            sc.activation(out=ebf, in_=acl, func=AF.Exp, scale=b.cv(-1.0))
            b.SA.inc(sc.activation(out=spb, in_=ebf, func=AF.Ln, bias=b.cv(1.0)))
            sc.wait_ge(b.PE.h, S0["PE"] + 4 * n + 4)
            sc.activation(out=ab2[:, 0:512], in_=pb[2], func=AF.Abs)
            sc.activation(out=ab2[:, 512:1024], in_=pb[3], func=AF.Abs)
            sc.activation(out=eb2, in_=ab2, func=AF.Exp, scale=b.cv(-1.0))
            b.SA.inc(sc.activation(out=eb2, in_=eb2, func=AF.Ln, bias=b.cv(1.0)))

    @block.gpsimd
    def _(g):
        b.wait_snap(g, S0)
        for n in range(NT):
            g.wait_ge(b.DV.h, S0["DV"] + 4 * n + 4)
            b.GP.inc(g.dma_start(out=ap["beta_h"][128 * n:128 * (n + 1), :], in_=bbf), 16)

    steps = []
    for n in range(NT):
        def mmf(t, n=n):
            return t.matmul(pb[0][:, 0:16], sb["kLINC64"],
                            sb["lmsum"][:, 16 * n:16 * (n + 1)], start=True, stop=True)
        steps.append(("PE", 1, mmf))
        steps.append(("SA", 1, lambda sc, n=n: sc.activation(
            out=sb["curve"][:, 16 * n:16 * (n + 1)], in_=pb[0][:, 0:16],
            func=AF.Exp, scale=b.cv(-1.0))))
        steps.append(("SA", 1, lambda sc, n=n: sc.activation(
            out=sb["icurve"][:, 16 * n:16 * (n + 1)], in_=pb[0][:, 0:16],
            func=AF.Exp)))
        steps.append(("SA", 1, lambda sc: sc.copy(out=sb["scr2"][:, 16:32],
                                                  in_=pb[0][:, 0:16])))
        steps.append(("PE", 1, lambda t: t.transpose(pb[1][0:16, 0:128],
                                                     sb["scr2"][:, 16:32], sb["kIDENT"])))
        steps.append(("SA", 1, lambda sc, n=n: sc.activation(
            out=sb["asumC"][:, n:n + 1], in_=pb[1][0:16, 127:128],
            func=AF.Exp, scale=b.cv(-1.0, 16))))
        steps.append(("SA", 1, lambda sc, n=n: sc.copy(
            out=sb["totb"][:, n:n + 1], in_=pb[1][0:16, 127:128])))
    steps.append(("PE", 1, lambda t: t.transpose(pb[2][0:16, 0:16], sb["totb"],
                                                 sb["kIDENT"][0:16, 0:16])))
    steps.append(("SA", 1, lambda sc: sc.copy(out=sb["scr2"][0:16, 32:48],
                                              in_=pb[2][0:16, 0:16])))
    steps.append(("PE", 1, lambda t: t.matmul(pb[3][0:16, 0:16],
                                              sb["kSTRICTLOW"][0:16, 0:16],
                                              sb["scr2"][0:16, 32:48],
                                              start=True, stop=True)))
    steps.append(("SA", 1, lambda sc: sc.copy(out=sb["scr2"][0:16, 48:64],
                                              in_=pb[3][0:16, 0:16])))
    steps.append(("PE", 1, lambda t: t.transpose(pb[2][0:16, 0:16],
                                                 sb["scr2"][0:16, 48:64],
                                                 sb["kIDENT"][0:16, 0:16])))
    steps.append(("SA", 1, lambda sc: sc.activation(out=sb["cumdec"], in_=pb[2][0:16, 0:16],
                                                    func=AF.Exp, scale=b.cv(-1.0, 16))))
    steps.append(("DV", 1, lambda v: v.tensor_tensor(
        sb["scr"][0:16, 0:256].rearrange("h (n hp) -> h n hp", hp=16),
        sb["kREPL16"].rearrange("h (n hp) -> h n hp", hp=16),
        sb["cumdec"].to_broadcast([16, 16, 16]), ALU.mult)))
    steps.append(("PE", 1, lambda t: t.matmul(pb[4][:, 0:256], sb["kONES16x128"],
                                              sb["scr"][0:16, 0:256], start=True, stop=True)))
    steps.append(("DV", 1, lambda v: v.tensor_tensor(sb["curvec"], sb["curve"],
                                                     pb[4][:, 0:256], ALU.mult)))
    b.serial(steps)
    es.close()


def emit_p2(b):
    nc, block, ap, sb, pb = b.nc, b.block, b.ap, b.sb, b.pb
    es = contextlib.ExitStack()
    Wq = es.enter_context(nc.sbuf_tensor("p2Wq", [128, 8 * D], F32)).ap()
    Wk = es.enter_context(nc.sbuf_tensor("p2Wk", [128, 8 * D], F32)).ap()
    Wv = es.enter_context(nc.sbuf_tensor("p2Wv", [128, 8 * D], F32)).ap()
    xTb = [es.enter_context(nc.sbuf_tensor(f"p2xT{i}", [128, D], F32)).ap() for i in range(2)]
    bb = [es.enter_context(nc.sbuf_tensor(f"p2bb{i}", [128, D], F32)).ap() for i in range(2)]
    xb2 = [es.enter_context(nc.sbuf_tensor(f"p2xb{i}", [128, D], F32)).ap() for i in range(2)]
    qTb = es.enter_context(nc.sbuf_tensor("p2qT", [128, D], F32)).ap()
    kTb = es.enter_context(nc.sbuf_tensor("p2kT", [128, D], F32)).ap()
    ktb = es.enter_context(nc.sbuf_tensor("p2kt", [128, D], F32)).ap()
    vb = es.enter_context(nc.sbuf_tensor("p2vb", [128, D], F32)).ap()
    vbb = es.enter_context(nc.sbuf_tensor("p2vbb", [128, D], F32)).ap()
    kbb = es.enter_context(nc.sbuf_tensor("p2kbb", [128, D], F32)).ap()
    PTb = es.enter_context(nc.sbuf_tensor("p2PT", [128, 2048], F32)).ap()
    ycb = es.enter_context(nc.sbuf_tensor("p2ycb", [128, 512], F32)).ap()
    dbuf = es.enter_context(nc.sbuf_tensor("p2db", [128, 512], F32)).ap()
    x1b = es.enter_context(nc.sbuf_tensor("p2x1", [128, D], F32)).ap()
    bvr = es.enter_context(nc.sbuf_tensor("p2bvr", [1, D], F32)).ap()
    S0 = b.snap()
    PEC, SAC, DVC = 29, 27, 4

    @block.sync
    def _(s):
        b.wait_snap(s, S0)
        b.SY.inc(s.dma_start(out=Wq, in_=ap["WqT0"].rearrange("(c p) o -> p c o", p=128)), 16)
        b.SY.inc(s.dma_start(out=Wk, in_=ap["WkT0"].rearrange("(c p) o -> p c o", p=128)), 16)
        b.SY.inc(s.dma_start(out=Wv, in_=ap["WvT0"].rearrange("(c p) o -> p c o", p=128)), 16)
        b.SY.inc(s.dma_start(out=bvr, in_=ap["b0r"][0:1, 0:D]), 16)
        for n in range(NT):
            if n >= 2:
                s.wait_ge(b.PE.h, S0["PE"] + PEC * (n - 1))
                s.wait_ge(b.DV.h, S0["DV"] + DVC * (n - 1))
            for c in range(8):
                b.SY.inc(s.dma_start(out=xTb[n % 2][:, 128 * c:128 * (c + 1)],
                                     in_=ap["xln0T_h"][n, c, :, :]), 16)
            b.SY.inc(s.dma_start(out=bb[n % 2], in_=ap["beta_h"][128 * n:128 * (n + 1), :]), 16)
            b.SY.inc(s.dma_start(out=xb2[n % 2], in_=ap["x_in"][128 * n:128 * (n + 1), :]), 16)

    @block.tensor
    def _(t):
        b.wait_snap(t, S0)
        t.wait_ge(b.SY.h, S0["SY"] + 64)
        for n in range(NT):
            t.wait_ge(b.SY.h, S0["SY"] + 64 + 160 * (n + 1))
            if n >= 1:
                t.wait_ge(b.DV.h, S0["DV"] + DVC * n)
                t.wait_ge(b.SA.h, S0["SA"] + SAC * n)
            for wmi, Wm in enumerate((Wq, Wk)):
                for co in range(8):
                    g = 8 * wmi + co
                    if g >= 2:
                        t.wait_ge(b.SA.h, S0["SA"] + SAC * n + g - 1)
                    pairs = [(Wm[:, D * c + 128 * co: D * c + 128 * co + 128],
                              xTb[n % 2][:, 128 * c:128 * (c + 1)])
                             for c in range(8)]
                    b.PE.inc(mm_accum(t, pb[g % 2][:, 0:128], pairs))
            for og in range(2):
                g = 16 + og
                t.wait_ge(b.SA.h, S0["SA"] + SAC * n + g - 1)
                pairs = [(xTb[n % 2][:, 128 * c:128 * (c + 1)],
                          Wv[:, D * c + 512 * og: D * c + 512 * og + 512])
                         for c in range(8)]
                pairs.append((sb["kONESROW"][0:1, 0:128],
                              bvr[0:1, 512 * og: 512 * og + 512]))
                b.PE.inc(mm_accum(t, pb[g % 2], pairs))
            for i in range(8):
                if i >= 2:
                    t.wait_ge(b.SA.h, S0["SA"] + SAC * n + 18 + i - 1)
                b.PE.inc(t.transpose(pb[2 + i % 2][:, 0:128],
                                     kTb[:, 128 * i:128 * (i + 1)], sb["kIDENT"]))
            t.wait_ge(b.SA.h, S0["SA"] + SAC * n + 26)
            ins = None
            for j in range(16):
                h = PERM[j]
                bank, reg = divmod(j, 4)
                half = slice(0, 64) if h % 2 == 0 else slice(64, 128)
                c = h // 2
                ins = t.matmul(pb[bank][:, 128 * reg:128 * reg + 128],
                               kTb[half, 128 * c:128 * (c + 1)],
                               qTb[half, 128 * c:128 * (c + 1)],
                               start=True, stop=True)
            b.PE.inc(ins)
            t.wait_ge(b.DV.h, S0["DV"] + DVC * n + 3)
            ins = None
            for j in range(16):
                h = PERM[j]
                bank = pb[0] if h % 2 == 0 else pb[1]
                col = 64 * (h // 2)
                ins = t.matmul(bank[:, col:col + 64], PTb[:, 128 * j:128 * (j + 1)],
                               vbb[:, 64 * h:64 * h + 64],
                               start=True, stop=True)
            b.PE.inc(ins)
            t.wait_ge(b.DV.h, S0["DV"] + DVC * n + 2)
            ins = None
            for h in range(16):
                rows = slice(0, 64) if h % 2 == 0 else slice(64, 128)
                col = 64 * (h // 2)
                ins = t.matmul(pb[4][rows, col:col + 64], kbb[:, 64 * h:64 * h + 64],
                               vb[:, 64 * h:64 * h + 64], start=True, stop=True)
            b.PE.inc(ins)

    @block.scalar
    def _(sc):
        b.wait_snap(sc, S0)
        for n in range(NT):
            for co in range(8):
                sc.wait_ge(b.PE.h, S0["PE"] + PEC * n + co + 1)
                if n >= 1 and co == 0:
                    sc.wait_ge(b.PE.h, S0["PE"] + PEC * n)
                    sc.wait_ge(b.GP.h, S0["GP"] + 48 * n)
                b.SA.inc(sc.activation(out=qTb[:, 128 * co:128 * (co + 1)],
                                       in_=pb[co % 2][:, 0:128], func=AF.Identity,
                                       scale=b.cv(0.125), bias=sb["b0cs"][:, co:co + 1]))
            for co in range(8):
                sc.wait_ge(b.PE.h, S0["PE"] + PEC * n + 8 + co + 1)
                b.SA.inc(sc.activation(out=kTb[:, 128 * co:128 * (co + 1)],
                                       in_=pb[co % 2][:, 0:128], func=AF.Identity,
                                       scale=b.cv(0.125), bias=sb["b0cs"][:, 8 + co:8 + co + 1]))
            for og in range(2):
                sc.wait_ge(b.PE.h, S0["PE"] + PEC * n + 16 + og + 1)
                b.SA.inc(sc.copy(out=vb[:, 512 * og:512 * og + 512], in_=pb[(16 + og) % 2]))
            for i in range(8):
                sc.wait_ge(b.PE.h, S0["PE"] + PEC * n + 18 + i + 1)
                if n >= 1 and i == 0:
                    sc.wait_ge(b.DV.h, S0["DV"] + DVC * (n - 1) + 2)
                b.SA.inc(sc.copy(out=ktb[:, 128 * i:128 * (i + 1)],
                                 in_=pb[2 + i % 2][:, 0:128]))
            sc.wait_ge(b.PE.h, S0["PE"] + PEC * n + 29)
            if n >= 1:
                sc.wait_ge(b.GP.h, S0["GP"] + 48 * n)
            b.SA.inc(sc.copy(out=dbuf, in_=pb[4]))

    @block.vector
    def _(v):
        b.wait_snap(v, S0)
        for n in range(NT):
            v.wait_ge(b.SA.h, S0["SA"] + SAC * n + 18)
            if n >= 1:
                v.wait_ge(b.PE.h, S0["PE"] + PEC * (n - 1) + 29)
            b.DV.inc(v.tensor_tensor(vbb, vb, bb[n % 2], ALU.mult))
            v.wait_ge(b.SA.h, S0["SA"] + SAC * n + 26)
            b.DV.inc(v.tensor_tensor(
                kbb.rearrange("p (a d) -> p a d", d=64),
                ktb.rearrange("p (a d) -> p a d", d=64),
                sb["bmean"][:, 16 * n:16 * (n + 1)].to_broadcast([128, 16, 64]),
                ALU.mult))
            v.wait_ge(b.PE.h, S0["PE"] + PEC * n + 27)
            ins = None
            for j in range(16):
                h = PERM[j]
                bank, reg = divmod(j, 4)
                ins = v.scalar_tensor_tensor(
                    PTb[:, 128 * j:128 * (j + 1)],
                    pb[bank][:, 128 * reg:128 * reg + 128],
                    sb["icurve"][:, 16 * n + h:16 * n + h + 1],
                    sb["kTRIL"], ALU.mult, ALU.mult)
            b.DV.inc(ins)
            v.wait_ge(b.PE.h, S0["PE"] + PEC * n + 28)
            if n >= 1:
                v.wait_ge(b.GP.h, S0["GP"] + 48 * n)
            ins = None
            for par in range(2):
                yv = pb[par].rearrange("p (a d) -> p a d", d=64)
                cb = sb["curve"][:, 16 * n:16 * (n + 1)] \
                    .rearrange("p (a two) -> p a two", two=2)[:, :, par] \
                    .to_broadcast([128, 8, 64])
                xv = xb2[n % 2].rearrange("p (a two d) -> p two a d", two=2, d=64)[:, par]
                ov = x1b.rearrange("p (a two d) -> p two a d", two=2, d=64)[:, par]
                v.tensor_tensor(ycb.rearrange("p (a d) -> p a d", d=64), yv, cb,
                                ALU.mult)
                v.drain()
                ins = v.tensor_tensor(ov, ycb.rearrange("p (a d) -> p a d", d=64),
                                      xv, ALU.add)
                v.drain()
            b.DV.inc(ins)

    @block.gpsimd
    def _(g):
        b.wait_snap(g, S0)
        for n in range(NT):
            g.wait_ge(b.DV.h, S0["DV"] + DVC * (n + 1))
            g.wait_ge(b.SA.h, S0["SA"] + SAC * (n + 1))
            b.GP.inc(g.dma_start(out=ap["qspill_h"][n, :, :], in_=qTb), 16)
            b.GP.inc(g.dma_start(out=ap["delta_h"][n, :, :], in_=dbuf), 16)
            b.GP.inc(g.dma_start(out=ap["x1_h"][128 * n:128 * (n + 1), :], in_=x1b), 16)

    es.close()
    es = contextlib.ExitStack()
    qc = es.enter_context(nc.sbuf_tensor("p2qc", [128, D], F32)).ap()
    dc2 = es.enter_context(nc.sbuf_tensor("p2dc2", [128, 512], F32)).ap()
    mcb = es.enter_context(nc.sbuf_tensor("p2mc", [128, D], F32)).ap()
    st2 = es.enter_context(nc.sbuf_tensor("p2st2", [128, 512], F32)).ap()
    sA = es.enter_context(nc.sbuf_tensor("p2sA", [128, 512], F32)).ap()
    sBv = es.enter_context(nc.sbuf_tensor("p2sB", [128, 512], F32)).ap()

    def chain_pass(state_src, curve_name):
        steps = []
        for n in range(NT):
            steps.append(("SY", 16, lambda s, n=n: s.dma_start(
                out=qc, in_=ap["qspill_h"][n, :, :])))
            steps.append(("SY", 16, lambda s, n=n: s.dma_start(
                out=dc2, in_=ap["delta_h"][n, :, :])))

            def mems(t, n=n):
                ins = None
                for h in range(16):
                    half = slice(0, 64) if h % 2 == 0 else slice(64, 128)
                    bank = pb[0] if h % 2 == 0 else pb[1]
                    col = 64 * (h // 2)
                    ins = t.matmul(bank[:, col:col + 64],
                                   qc[half, 128 * (h // 2):128 * (h // 2) + 128],
                                   state_src[half, col:col + 64],
                                   start=True, stop=True)
                return ins
            steps.append(("PE", 1, mems))

            def mcf0(v, n=n):
                cv = sb[curve_name][:, 16 * n:16 * (n + 1)] \
                    .rearrange("p (a two) -> p a two", two=2)[:, :, 0] \
                    .to_broadcast([128, 8, 64])
                return v.tensor_tensor(mcb[:, 0:512].rearrange("p (a d) -> p a d", d=64),
                                       pb[0].rearrange("p (a d) -> p a d", d=64),
                                       cv, ALU.mult)
            steps.append(("DV", 1, mcf0))

            def mcf1(v, n=n):
                cv = sb[curve_name][:, 16 * n:16 * (n + 1)] \
                    .rearrange("p (a two) -> p a two", two=2)[:, :, 1] \
                    .to_broadcast([128, 8, 64])
                return v.tensor_tensor(mcb[:, 512:1024].rearrange("p (a d) -> p a d", d=64),
                                       pb[1].rearrange("p (a d) -> p a d", d=64),
                                       cv, ALU.mult)
            steps.append(("DV", 1, mcf1))

            if state_src is sb["stateAll"]:
                def asd(v, n=n):
                    return v.tensor_scalar(sb["scr2"][0:16, 0:8], sb["kDIAGC"],
                                           sb["asumC"][:, n:n + 1], None, ALU.mult)
                steps.append(("DV", 1, asd))
                steps.append(("PE", 1, lambda t: t.matmul(
                    pb[2][:, 0:8], sb["kSELPAR"], sb["scr2"][0:16, 0:8],
                    start=True, stop=True)))
                steps.append(("DV", 1, lambda v: v.tensor_tensor(
                    st2.rearrange("p (a d) -> p a d", d=64),
                    sb["stateAll"].rearrange("p (a d) -> p a d", d=64),
                    pb[2][:, 0:8].to_broadcast([128, 8, 64]), ALU.mult)))
                steps.append(("DV", 1, lambda v: v.tensor_tensor(
                    sb["stateAll"], st2, dc2, ALU.add)))

            def acc0(g, n=n):
                dst = ap["x1_h"][128 * n:128 * (n + 1), :] \
                    .rearrange("t (a two d) -> t two a d", two=2, d=64)[:, 0]
                return g.dma_start(out=dst,
                                   in_=mcb[:, 0:512].rearrange("p (a d) -> p a d", d=64),
                                   accum_op=ALU.add)
            steps.append(("GP", 16, acc0))

            def acc1(g, n=n):
                dst = ap["x1_h"][128 * n:128 * (n + 1), :] \
                    .rearrange("t (a two d) -> t two a d", two=2, d=64)[:, 1]
                return g.dma_start(out=dst,
                                   in_=mcb[:, 512:1024].rearrange("p (a d) -> p a d", d=64),
                                   accum_op=ALU.add)
            steps.append(("GP", 16, acc1))
        return steps

    b.serial(chain_pass(sb["stateAll"], "curve"))

    steps = [
        ("GP", 16, lambda g: g.dma_start(
            out=ap["cc_st_in"].rearrange("(p f) -> p f", p=128), in_=sb["stateAll"])),
        ("CC", 1, lambda g: g.collective_compute(
            "AllGather", ALU.bypass,
            replica_groups=[[0, 1], [2, 3], [4, 5], [6, 7]],
            ins=[ap["cc_st_in"][:]], outs=[ap["cc_st_out"][:]])),
        ("SY", 16, lambda s: s.dma_start(
            out=sA, in_=ap["cc_st_out"][0:65536].rearrange("(p f) -> p f", p=128))),
        ("SY", 16, lambda s: s.dma_start(
            out=sBv, in_=ap["cc_st_out"][65536:131072].rearrange("(p f) -> p f", p=128))),
        ("DV", 1, lambda v: v.tensor_scalar(st2, sA, sb["metaPs"][:, 2:3], None, ALU.mult)),
        ("DV", 1, lambda v: v.scalar_tensor_tensor(sb["Sinh"], sBv, sb["metaPs"][:, 3:4],
                                                   st2, ALU.mult, ALU.add)),
    ]
    b.serial(steps)
    b.serial(chain_pass(sb["Sinh"], "curvec"))
    es.close()


def emit_p3(b):
    nc, block, ap, sb, pb = b.nc, b.block, b.ap, b.sb, b.pb
    es = contextlib.ExitStack()
    xb = [es.enter_context(nc.sbuf_tensor(f"p3x{i}", [128, D], F32)).ap() for i in range(2)]
    xlnb = [es.enter_context(nc.sbuf_tensor(f"p3l{i}", [128, D], F32)).ap() for i in range(2)]
    xtb = [es.enter_context(nc.sbuf_tensor(f"p3t{i}", [128, D], F32)).ap() for i in range(2)]
    gws = es.enter_context(nc.sbuf_tensor("p3gw", [128, 64], F32)).ap()
    lg = es.enter_context(nc.sbuf_tensor("p3lg", [128, 128], F32)).ap()
    g16 = es.enter_context(nc.sbuf_tensor("p3g16", [128, 128], F32)).ap()
    E8 = es.enter_context(nc.sbuf_tensor("p3E8", [128, 128], F32)).ap()
    E2 = es.enter_context(nc.sbuf_tensor("p3E2", [128, 128], F32)).ap()
    MS = es.enter_context(nc.sbuf_tensor("p3MS", [128, 128], F32)).ap()
    M2B = es.enter_context(nc.sbuf_tensor("p3M2B", [128, 128], F32)).ap()
    S0 = b.snap()
    st = sb["stats"]
    DVC, SAC, PEC = 3, 13, 9

    @block.sync
    def _(s):
        b.wait_snap(s, S0)
        b.SY.inc(s.dma_start(out=gws.rearrange("p (c e) -> p c e", e=8),
                             in_=ap["gwT0"].rearrange("(c p) e -> p c e", p=128)), 16)
        for n in range(NT):
            if n >= 2:
                s.wait_ge(b.SA.h, S0["SA"] + SAC * (n - 2) + 4)
            b.SY.inc(s.dma_start(out=xb[n % 2],
                                 in_=ap["x1_h"][128 * n:128 * (n + 1), :]), 16)

    @block.vector
    def _(v):
        b.wait_snap(v, S0)
        for n in range(NT):
            c0 = 6 * n
            v.wait_ge(b.SY.h, S0["SY"] + 16 + 16 * (n + 1))
            b.DV.inc(v.reduce_sum(st[:, c0:c0 + 1], xb[n % 2], axis=AX.X))
            v.drain()
            v.wait_ge(b.SA.h, S0["SA"] + SAC * n + 2)
            b.DV.inc(v.tensor_tensor(st[:, c0 + 3:c0 + 4], st[:, c0 + 1:c0 + 2],
                                     st[:, c0 + 2:c0 + 3], ALU.subtract))
            v.drain()
            v.wait_ge(b.SA.h, S0["SA"] + SAC * n + 3)
            b.DV.inc(v.scalar_tensor_tensor(st[:, c0 + 5:c0 + 6], st[:, c0:c0 + 1],
                                            b.cv(-1.0 / 1024.0), st[:, c0 + 4:c0 + 5],
                                            ALU.mult, ALU.mult))
            v.drain()

    @block.scalar
    def _(sc):
        b.wait_snap(sc, S0)
        for n in range(NT):
            c0 = 6 * n
            sc.wait_ge(b.SY.h, S0["SY"] + 16 + 16 * (n + 1))
            b.SA.inc(sc.activation(out=sb["scr"], in_=xb[n % 2], func=AF.Square,
                                   accum_out=st[:, c0 + 1:c0 + 2]))
            sc.wait_ge(b.DV.h, S0["DV"] + DVC * n + 1)
            b.SA.inc(sc.activation(out=st[:, c0 + 2:c0 + 3], in_=st[:, c0:c0 + 1],
                                   func=AF.Square, scale=b.cv(1.0 / 32.0)))
            sc.wait_ge(b.DV.h, S0["DV"] + DVC * n + 2)
            sc.activation(out=st[:, c0 + 4:c0 + 5], in_=st[:, c0 + 3:c0 + 4], func=AF.Ln,
                          bias=b.cv(1024.0 * EPS))
            b.SA.inc(sc.activation(out=st[:, c0 + 4:c0 + 5], in_=st[:, c0 + 4:c0 + 5],
                                   func=AF.Exp, scale=b.cv(-0.5), bias=b.cv(LN32)))
            sc.wait_ge(b.DV.h, S0["DV"] + DVC * n + 3)
            if n >= 2:
                sc.wait_ge(b.PE.h, S0["PE"] + PEC * (n - 1))
            b.SA.inc(sc.activation(out=xlnb[n % 2], in_=xb[n % 2], func=AF.Identity,
                                   scale=st[:, c0 + 4:c0 + 5],
                                   bias=st[:, c0 + 5:c0 + 6]))
            if n >= 2:
                sc.wait_ge(b.GP.h, S0["GP"] + 128 * (n - 1))
            for c in range(8):
                sc.wait_ge(b.PE.h, S0["PE"] + PEC * n + c + 1)
                b.SA.inc(sc.copy(out=xtb[n % 2][:, 128 * c:128 * (c + 1)],
                                 in_=pb[c % 2][:, 0:128]))
            sc.wait_ge(b.PE.h, S0["PE"] + PEC * n + 9)
            b.SA.inc(sc.copy(out=lg[:, 8 * n:8 * n + 8], in_=pb[2][:, 0:8]))

    @block.tensor
    def _(t):
        b.wait_snap(t, S0)
        for n in range(NT):
            t.wait_ge(b.SA.h, S0["SA"] + SAC * n + 4)
            for c in range(8):
                Tg = 8 * n + c
                if Tg >= 2:
                    pn, pc = (Tg - 2) // 8, (Tg - 2) % 8
                    t.wait_ge(b.SA.h, S0["SA"] + SAC * pn + 4 + pc + 1)
                b.PE.inc(t.transpose(pb[c % 2][:, 0:128],
                                     xlnb[n % 2][:, 128 * c:128 * (c + 1)],
                                     sb["kIDENT"]))
            t.wait_ge(b.SA.h, S0["SA"] + SAC * n + 12)
            if n >= 1:
                t.wait_ge(b.SA.h, S0["SA"] + SAC * (n - 1) + 13)
            pairs = [(xtb[n % 2][:, 128 * c:128 * (c + 1)], gws[:, 8 * c:8 * c + 8])
                     for c in range(8)]
            b.PE.inc(mm_accum(t, pb[2][:, 0:8], pairs))

    @block.gpsimd
    def _(g):
        b.wait_snap(g, S0)
        for n in range(NT):
            g.wait_ge(b.SA.h, S0["SA"] + SAC * n + 12)
            for c in range(8):
                b.GP.inc(g.dma_start(out=ap["xln1T_h"][n, c, :, :],
                                     in_=xtb[n % 2][:, 128 * c:128 * (c + 1)]), 16)

    L3 = lg.rearrange("p (n e) -> p n e", e=8)
    steps = [
        ("DV", 1, lambda v: v.tensor_reduce(g16[:, 0:16], L3, axis=AX.X, op=ALU.max)),
        ("DV", 1, lambda v: v.tensor_tensor(
            MS.rearrange("p (n e) -> p n e", e=8), L3,
            g16[:, 0:16].to_broadcast([128, 16, 8]), ALU.subtract)),
        ("SA", 1, lambda sc: sc.activation(out=E8, in_=MS, func=AF.Exp)),
        ("DV", 1, lambda v: v.tensor_reduce(g16[:, 16:32],
                                            E8.rearrange("p (n e) -> p n e", e=8),
                                            axis=AX.X, op=ALU.add)),
        ("DV", 1, lambda v: v.tensor_reduce(g16[:, 32:48],
                                            E8.rearrange("p (n e) -> p n e", e=8),
                                            axis=AX.X, op=ALU.max)),
        ("DV", 1, lambda v: v.tensor_tensor(
            MS.rearrange("p (n e) -> p n e", e=8),
            E8.rearrange("p (n e) -> p n e", e=8),
            g16[:, 32:48].to_broadcast([128, 16, 8]), ALU.is_ge)),
        ("DV", 1, lambda v: v.scalar_tensor_tensor(E2, MS, b.cv(-10.0), E8,
                                                   ALU.mult, ALU.add)),
        ("DV", 1, lambda v: v.tensor_reduce(g16[:, 48:64],
                                            E2.rearrange("p (n e) -> p n e", e=8),
                                            axis=AX.X, op=ALU.max)),
        ("DV", 1, lambda v: v.tensor_tensor(
            M2B.rearrange("p (n e) -> p n e", e=8),
            E2.rearrange("p (n e) -> p n e", e=8),
            g16[:, 48:64].to_broadcast([128, 16, 8]), ALU.is_ge)),
        ("DV", 1, lambda v: v.tensor_tensor(g16[:, 64:80], g16[:, 32:48],
                                            g16[:, 48:64], ALU.add)),
        ("DV", 1, lambda v: v.scalar_tensor_tensor(g16[:, 80:96], g16[:, 16:32],
                                                   b.cv(1e-20), g16[:, 64:80],
                                                   ALU.mult, ALU.add)),
        ("DV", 1, lambda v: v.reciprocal(g16[:, 96:112], g16[:, 80:96])),
        ("DV", 1, lambda v: v.tensor_tensor(lg, MS, M2B, ALU.add)),
        ("DV", 1, lambda v: v.tensor_tensor(MS, lg, E8, ALU.mult)),
        ("DV", 1, lambda v: v.tensor_tensor(
            sb["comb"].rearrange("p (n e) -> p n e", e=8),
            MS.rearrange("p (n e) -> p n e", e=8),
            g16[:, 96:112].to_broadcast([128, 16, 8]), ALU.mult)),
    ]
    for n in range(NT):
        steps.append(("PE", 1, lambda t, n=n: t.transpose(
            pb[3][0:8, 0:128], sb["comb"][:, 8 * n:8 * n + 8], sb["kIDENT"])))
        steps.append(("SA", 1, lambda sc, n=n: sc.copy(
            out=sb["combT"][:, 128 * n:128 * (n + 1)], in_=pb[3][0:8, 0:128])))
    b.serial(steps)
    es.close()


def emit_p4(b):
    nc, block, ap, sb, pb = b.nc, b.block, b.ap, b.sb, b.pb
    es = contextlib.ExitStack()
    Wu = es.enter_context(nc.sbuf_tensor("p4Wu", [128, 8 * D], F32)).ap()
    Wd = es.enter_context(nc.sbuf_tensor("p4Wd", [128, 8 * D], F32)).ap()
    ubc = es.enter_context(nc.sbuf_tensor("p4ubc", [128, 8], F32)).ap()
    db8 = es.enter_context(nc.sbuf_tensor("p4db8", [8, D], F32)).ap()
    sdbr = es.enter_context(nc.sbuf_tensor("p4sdbr", [1, D], F32)).ap()
    xs = [es.enter_context(nc.sbuf_tensor(f"p4xs{i}", [128, 4096], F32)).ap() for i in range(2)]
    hb = es.enter_context(nc.sbuf_tensor("p4hb", [128, 4096], F32)).ap()
    yt = es.enter_context(nc.sbuf_tensor("p4yt", [128, 512], F32)).ap()
    S0 = b.snap()
    NE = 9
    NS = 4
    PEC, SAC = 16, 16

    def eidx(e, s):
        return e * NS + s

    @block.sync
    def _(s_):
        b.wait_snap(s_, S0)
        for e in range(NE):
            if e >= 1:
                s_.wait_ge(b.PE.h, S0["PE"] + PEC * NS * e)
                s_.wait_ge(b.SA.h, S0["SA"] + SAC * NS * e)
            if e < 8:
                b.SY.inc(s_.dma_start(out=Wu, in_=ap["up_w"][e].rearrange(
                    "(c p) h -> p c h", p=128)), 16)
                b.SY.inc(s_.dma_start(out=Wd, in_=ap["down_w"][e].rearrange(
                    "(c p) h -> p c h", p=128)), 16)
                b.SY.inc(s_.dma_start(out=ubc[:, 0:8],
                                      in_=ap["up_bc"][:, 8 * e:8 * e + 8]), 16)
            else:
                b.SY.inc(s_.dma_start(out=Wu, in_=ap["sh_up_w"].rearrange(
                    "(c p) h -> p c h", p=128)), 16)
                b.SY.inc(s_.dma_start(out=Wd, in_=ap["sh_down_w"].rearrange(
                    "(c p) h -> p c h", p=128)), 16)
                b.SY.inc(s_.dma_start(out=ubc[:, 0:8], in_=ap["sh_up_bc"][:, :]), 16)
                b.SY.inc(s_.dma_start(out=db8, in_=ap["down_br"].rearrange(
                    "one (e o) -> (one e) o", e=8)), 16)
                b.SY.inc(s_.dma_start(out=sdbr, in_=ap["sh_down_br"][:, :]), 16)
            for s in range(NS):
                i = eidx(e, s)
                if i >= 2:
                    s_.wait_ge(b.PE.h, S0["PE"] + PEC * (i - 1))
                for c in range(8):
                    b.SY.inc(s_.dma_start(
                        out=xs[s % 2][:, 512 * c:512 * (c + 1)],
                        in_=ap["xln1T_h"][4 * s:4 * s + 4, c].rearrange("n p t -> p n t")), 16)

    @block.tensor
    def _(t):
        b.wait_snap(t, S0)
        for e in range(NE):
            nw = 3 * min(e + 1, 8) + (5 if e == 8 else 0)
            for s in range(NS):
                i = eidx(e, s)
                t.wait_ge(b.SY.h, S0["SY"] + 16 * (nw + 8 * (i + 1)))
                for hc in range(8):
                    g = PEC * i + hc
                    if g >= 2:
                        pn, pc = (g - 2) // PEC, (g - 2) % PEC
                        t.wait_ge(b.SA.h, S0["SA"] + PEC * pn + pc + 1)
                    pairs = [(Wu[:, D * c + 128 * hc: D * c + 128 * hc + 128],
                              xs[s % 2][:, 512 * c:512 * (c + 1)])
                             for c in range(8)]
                    b.PE.inc(mm_accum(t, pb[hc % 2], pairs))
                for g2 in range(8):
                    tt, og = divmod(g2, 2)
                    g = PEC * i + 8 + g2
                    if g >= 2:
                        pn, pc = (g - 2) // PEC, (g - 2) % PEC
                        t.wait_ge(b.SA.h, S0["SA"] + PEC * pn + pc + 1)
                    t.wait_ge(b.SA.h, S0["SA"] + PEC * i + 8)
                    pairs = [(hb[:, 512 * hc + 128 * tt: 512 * hc + 128 * tt + 128],
                              Wd[:, D * hc + 512 * og: D * hc + 512 * og + 512])
                             for hc in range(8)]
                    if e == 8:
                        pairs.append((sb["kONESROW"][0:1, 0:128],
                                      sdbr[0:1, 512 * og:512 * og + 512]))
                        pairs.append((sb["combT"][0:8, 128 * (4 * s + tt):
                                                  128 * (4 * s + tt) + 128],
                                      db8[:, 512 * og:512 * og + 512]))
                    b.PE.inc(mm_accum(t, pb[2 + g2 % 2], pairs))

    @block.scalar
    def _(sc):
        b.wait_snap(sc, S0)
        for e in range(NE):
            for s in range(NS):
                i = eidx(e, s)
                for hc in range(8):
                    sc.wait_ge(b.PE.h, S0["PE"] + PEC * i + hc + 1)
                    b.SA.inc(sc.activation(out=hb[:, 512 * hc:512 * (hc + 1)],
                                           in_=pb[hc % 2], func=AF.Silu,
                                           bias=ubc[:, hc:hc + 1]))
                for g2 in range(8):
                    tt, og = divmod(g2, 2)
                    sc.wait_ge(b.PE.h, S0["PE"] + PEC * i + 8 + g2 + 1)
                    if i >= 1:
                        sc.wait_ge(b.GP.h, S0["GP"] + 8 * 16 * i)
                    n_ = 4 * s + tt
                    if e < 8:
                        b.SA.inc(sc.activation(out=yt, in_=pb[2 + g2 % 2],
                                               func=AF.Identity,
                                               scale=sb["comb"][:, 8 * n_ + e:8 * n_ + e + 1]))
                    else:
                        b.SA.inc(sc.copy(out=yt, in_=pb[2 + g2 % 2]))

    @block.gpsimd
    def _(g):
        b.wait_snap(g, S0)
        for e in range(NE):
            for s in range(NS):
                i = eidx(e, s)
                for g2 in range(8):
                    tt, og = divmod(g2, 2)
                    g.wait_ge(b.SA.h, S0["SA"] + PEC * i + 8 + g2 + 1)
                    n_ = 4 * s + tt
                    dst = ap["yacc_h"][128 * n_:128 * (n_ + 1), 512 * og:512 * (og + 1)]
                    b.GP.inc(g.dma_start(out=dst, in_=yt,
                                         accum_op=(ALU.add if e > 0 else ALU.bypass)), 16)

    es.close()
    es = contextlib.ExitStack()
    xa = es.enter_context(nc.sbuf_tensor("p4xa", [128, D], F32)).ap()
    ya = es.enter_context(nc.sbuf_tensor("p4ya", [128, D], F32)).ap()
    x2b = es.enter_context(nc.sbuf_tensor("p4x2", [128, D], F32)).ap()
    steps = []
    for n in range(NT):
        steps.append(("SY", 16, lambda s, n=n: s.dma_start(
            out=xa, in_=ap["x1_h"][128 * n:128 * (n + 1), :])))
        steps.append(("SY", 16, lambda s, n=n: s.dma_start(
            out=ya, in_=ap["yacc_h"][128 * n:128 * (n + 1), :])))
        steps.append(("DV", 1, lambda v: v.tensor_tensor(x2b, xa, ya, ALU.add)))
        steps.append(("GP", 16, lambda g, n=n: g.dma_start(
            out=ap["x2_h"][128 * n:128 * (n + 1), :], in_=x2b)))
        steps.append(("DV", 1, lambda v, n=n: v.reduce_sum(
            sb["scr2"][:, 0:1], x2b, axis=AX.X)))
        steps.append(("SA", 1, lambda sc, n=n: sc.activation(
            out=sb["scr"], in_=x2b, func=AF.Square, accum_out=sb["scr2"][:, 1:2])))
        steps.append(("SA", 1, lambda sc: sc.activation(
            out=sb["scr2"][:, 2:3], in_=sb["scr2"][:, 0:1], func=AF.Square,
            scale=b.cv(1.0 / 32.0))))
        steps.append(("DV", 1, lambda v: v.tensor_tensor(
            sb["scr2"][:, 3:4], sb["scr2"][:, 1:2], sb["scr2"][:, 2:3], ALU.subtract)))
        steps.append(("SA", 1, lambda sc: sc.activation(
            out=sb["scr2"][:, 4:5], in_=sb["scr2"][:, 3:4], func=AF.Ln,
            bias=b.cv(1024.0 * EPS))))
        steps.append(("SA", 1, lambda sc, n=n: sc.activation(
            out=sb["rstd2"][:, n:n + 1], in_=sb["scr2"][:, 4:5],
            func=AF.Exp, scale=b.cv(-0.5), bias=b.cv(LN32))))
        steps.append(("DV", 1, lambda v, n=n: v.scalar_tensor_tensor(
            sb["nmr2"][:, n:n + 1], sb["scr2"][:, 0:1], b.cv(-1.0 / 1024.0),
            sb["rstd2"][:, n:n + 1], ALU.mult, ALU.mult)))
    b.serial(steps)
    es.close()


def emit_p5(b):
    nc, block, ap, sb, pb = b.nc, b.block, b.ap, b.sb, b.pb
    esw = contextlib.ExitStack()
    gmat = esw.enter_context(nc.sbuf_tensor("p5gmat", [128, D], F32)).ap()
    bmat = esw.enter_context(nc.sbuf_tensor("p5bmat", [128, D], F32)).ap()

    es = contextlib.ExitStack()
    Wq = es.enter_context(nc.sbuf_tensor("p5Wq", [128, 8 * D], F32)).ap()
    Wk = es.enter_context(nc.sbuf_tensor("p5Wk", [128, 8 * D], F32)).ap()
    Wv = es.enter_context(nc.sbuf_tensor("p5Wv", [128, 8 * D], F32)).ap()
    xb = [es.enter_context(nc.sbuf_tensor(f"p5x{i}", [128, D], F32)).ap() for i in range(2)]
    xlnb = [es.enter_context(nc.sbuf_tensor(f"p5l{i}", [128, D], F32)).ap() for i in range(2)]
    xtb = [es.enter_context(nc.sbuf_tensor(f"p5t{i}", [128, D], F32)).ap() for i in range(2)]
    qob = es.enter_context(nc.sbuf_tensor("p5qo", [128, D], F32)).ap()
    kob = es.enter_context(nc.sbuf_tensor("p5ko", [128, D], F32)).ap()
    vob = es.enter_context(nc.sbuf_tensor("p5vo", [128, D], F32)).ap()
    b1cs = es.enter_context(nc.sbuf_tensor("p5b1c", [128, 32], F32)).ap()
    b1vr = es.enter_context(nc.sbuf_tensor("p5b1vr", [1, D], F32)).ap()

    steps = []
    steps.append(("SY", 16, lambda s: s.dma_start(out=b1cs, in_=ap["b1c"][:, :])))
    steps.append(("SY", 16, lambda s: s.dma_start(out=b1vr, in_=ap["b1r"][0:1, 2 * D:3 * D])))
    steps.append(("SY", 16, lambda s: s.dma_start(out=Wq, in_=ap["WqT1"].rearrange(
        "(c p) o -> p c o", p=128))))
    steps.append(("SY", 16, lambda s: s.dma_start(out=Wk, in_=ap["WkT1"].rearrange(
        "(c p) o -> p c o", p=128))))
    steps.append(("SY", 16, lambda s: s.dma_start(out=Wv, in_=ap["WvT1"].rearrange(
        "(c p) o -> p c o", p=128))))
    b.serial(steps)

    S0 = b.snap()
    SAC = 27
    PEC = 26
    GPC = 18 * 16

    @block.sync
    def _(s):
        b.wait_snap(s, S0)
        for n in range(NT):
            if n >= 2:
                s.wait_ge(b.SA.h, S0["SA"] + SAC * (n - 2) + 1)
            b.SY.inc(s.dma_start(out=xb[n % 2],
                                 in_=ap["x2_h"][128 * n:128 * (n + 1), :]), 16)

    @block.scalar
    def _(sc):
        b.wait_snap(sc, S0)
        for n in range(NT):
            sc.wait_ge(b.SY.h, S0["SY"] + 16 * (n + 1))
            if n >= 2:
                sc.wait_ge(b.PE.h, S0["PE"] + PEC * (n - 1))
            b.SA.inc(sc.activation(out=xlnb[n % 2], in_=xb[n % 2], func=AF.Identity,
                                   scale=sb["rstd2"][:, n:n + 1],
                                   bias=sb["nmr2"][:, n:n + 1]))
            for c in range(8):
                sc.wait_ge(b.PE.h, S0["PE"] + PEC * n + c + 1)
                b.SA.inc(sc.copy(out=xtb[n % 2][:, 128 * c:128 * (c + 1)],
                                 in_=pb[c % 2][:, 0:128]))
            for co in range(8):
                sc.wait_ge(b.PE.h, S0["PE"] + PEC * n + 8 + co + 1)
                if n >= 1 and co == 0:
                    sc.wait_ge(b.GP.h, S0["GP"] + GPC * n)
                b.SA.inc(sc.activation(out=qob[:, 128 * co:128 * (co + 1)],
                                       in_=pb[2 + co % 2][:, 0:128], func=AF.Identity,
                                       bias=b1cs[:, co:co + 1]))
            for co in range(8):
                sc.wait_ge(b.PE.h, S0["PE"] + PEC * n + 16 + co + 1)
                b.SA.inc(sc.activation(out=kob[:, 128 * co:128 * (co + 1)],
                                       in_=pb[4 + co % 2][:, 0:128], func=AF.Identity,
                                       bias=b1cs[:, 8 + co:8 + co + 1]))
            for og in range(2):
                sc.wait_ge(b.PE.h, S0["PE"] + PEC * n + 24 + og + 1)
                b.SA.inc(sc.copy(out=vob[:, 512 * og:512 * og + 512], in_=pb[6 + og]))

    @block.tensor
    def _(t):
        b.wait_snap(t, S0)
        for n in range(NT):
            t.wait_ge(b.SA.h, S0["SA"] + SAC * n + 1)
            for c in range(8):
                Tg = 8 * n + c
                if Tg >= 2:
                    pn, pc = (Tg - 2) // 8, (Tg - 2) % 8
                    t.wait_ge(b.SA.h, S0["SA"] + SAC * pn + 1 + pc + 1)
                b.PE.inc(t.transpose(pb[c % 2][:, 0:128],
                                     xlnb[n % 2][:, 128 * c:128 * (c + 1)],
                                     sb["kIDENT"]))
            t.wait_ge(b.SA.h, S0["SA"] + SAC * n + 9)
            for co in range(8):
                Tg = 8 * n + co
                if Tg >= 2:
                    pn, pc = (Tg - 2) // 8, (Tg - 2) % 8
                    t.wait_ge(b.SA.h, S0["SA"] + SAC * pn + 9 + pc + 1)
                pairs = [(Wq[:, D * c + 128 * co: D * c + 128 * co + 128],
                          xtb[n % 2][:, 128 * c:128 * (c + 1)]) for c in range(8)]
                b.PE.inc(mm_accum(t, pb[2 + co % 2][:, 0:128], pairs))
            for co in range(8):
                Tg = 8 * n + co
                if Tg >= 2:
                    pn, pc = (Tg - 2) // 8, (Tg - 2) % 8
                    t.wait_ge(b.SA.h, S0["SA"] + SAC * pn + 17 + pc + 1)
                pairs = [(Wk[:, D * c + 128 * co: D * c + 128 * co + 128],
                          xtb[n % 2][:, 128 * c:128 * (c + 1)]) for c in range(8)]
                b.PE.inc(mm_accum(t, pb[4 + co % 2][:, 0:128], pairs))
            for og in range(2):
                if n >= 1:
                    t.wait_ge(b.SA.h, S0["SA"] + SAC * (n - 1) + 25 + og + 1)
                pairs = [(xtb[n % 2][:, 128 * c:128 * (c + 1)],
                          Wv[:, D * c + 512 * og: D * c + 512 * og + 512])
                         for c in range(8)]
                pairs.append((sb["kONESROW"][0:1, 0:128],
                              b1vr[0:1, 512 * og: 512 * og + 512]))
                b.PE.inc(mm_accum(t, pb[6 + og], pairs))

    @block.gpsimd
    def _(g):
        b.wait_snap(g, S0)
        kin = ap["cc_k_in"].rearrange("(c p t) -> c p t", c=8, p=128)
        vin = ap["ccvin"].rearrange("(t d) -> t d", d=D)
        for n in range(NT):
            g.wait_ge(b.SA.h, S0["SA"] + SAC * (n + 1))
            for co in range(8):
                b.GP.inc(g.dma_start(out=ap["q1_h"][co, :, 128 * n:128 * (n + 1)],
                                     in_=qob[:, 128 * co:128 * (co + 1)]), 16)
            for co in range(8):
                b.GP.inc(g.dma_start(out=kin[co, :, 128 * n:128 * (n + 1)],
                                     in_=kob[:, 128 * co:128 * (co + 1)]), 16)
            for og in range(2):
                b.GP.inc(g.dma_start(
                    out=vin[128 * n:128 * (n + 1), 512 * og:512 * (og + 1)],
                    in_=vob[:, 512 * og:512 * og + 512]), 16)
    es.close()

    es2 = contextlib.ExitStack()
    bandb = es2.enter_context(nc.sbuf_tensor("p5band", [1, 2 * T], F32)).ap()
    bscr = es2.enter_context(nc.sbuf_tensor("p5bscr", [1, 2 * T], F32)).ap()
    grows = es2.enter_context(nc.sbuf_tensor("p5grows", [1, 2 * D], F32)).ap()
    steps = []
    steps.append(("CC", 1, lambda g: g.collective_compute(
        "AllGather", ALU.bypass, replica_groups=[[0, 1], [2, 3], [4, 5], [6, 7]],
        ins=[ap["cc_k_in"][:]], outs=[ap["cc_k_out"][:]])))
    steps.append(("CC", 1, lambda g: g.collective_compute(
        "AllGather", ALU.bypass, replica_groups=[[0, 1], [2, 3], [4, 5], [6, 7]],
        ins=[ap["ccvin"][:]], outs=[ap["ccvout"][:]])))
    steps.append(("SY", 16, lambda s: s.dma_start(out=bandb[0:1, 0:T], in_=ap["band"][:, :])))
    steps.append(("SY", 16, lambda s: s.dma_start(out=grows[0:1, 0:D], in_=ap["g1r"][:, :])))
    steps.append(("SY", 16, lambda s: s.dma_start(out=grows[0:1, D:2 * D], in_=ap["be1r"][:, :])))
    for og in range(2):
        steps.append(("PE", 1, lambda t, og=og: t.matmul(
            pb[0], sb["kONESROW"][0:1, 0:128], grows[0:1, 512 * og:512 * og + 512],
            start=True, stop=True)))
        steps.append(("SA", 1, lambda sc, og=og: sc.copy(
            out=gmat[:, 512 * og:512 * og + 512], in_=pb[0])))
        steps.append(("PE", 1, lambda t, og=og: t.matmul(
            pb[0], sb["kONESROW"][0:1, 0:128],
            grows[0:1, D + 512 * og:D + 512 * og + 512], start=True, stop=True)))
        steps.append(("SA", 1, lambda sc, og=og: sc.copy(
            out=bmat[:, 512 * og:512 * og + 512], in_=pb[0])))

    def band_mask(v):
        B0 = bandb[0:1, 0:T]
        s1 = bscr[0:1, T:T + 1]
        tmp = bscr[0:1, 0:T]
        msk = bandb[0:1, T:2 * T]
        v.tensor_copy(tmp, B0)
        v.drain()
        for it in range(3):
            v.reduce_max(s1, tmp, axis=AX.X)
            v.drain()
            v.tensor_scalar(msk, tmp, s1, None, ALU.is_ge)
            v.drain()
            v.scalar_tensor_tensor(tmp, msk, b.cv(-1e9, 1), tmp, ALU.mult, ALU.add)
            v.drain()
        ins = v.tensor_scalar(msk, tmp, -1e8, None, ALU.is_gt)
        return ins
    steps.append(("DV", 1, band_mask))
    steps.append(("GP", 16, lambda g: g.dma_start(
        out=ap["cc_rs_out"][0:T], in_=bandb[0:1, T:2 * T])))
    steps.append(("SY", 16, lambda s: s.dma_start(
        out=sb["cmcol"], in_=ap["cc_rs_out"][0:T].rearrange("(t p) -> p t", p=128),
        allow_slow_non_contiguous=True)))
    b.serial(steps)
    es2.close()

    es = contextlib.ExitStack()
    kThp = es.enter_context(nc.sbuf_tensor("p5kT", [128, 2 * T], F32)).ap()
    qThp = es.enter_context(nc.sbuf_tensor("p5qT", [128, T], F32)).ap()
    vvt = es.enter_context(nc.sbuf_tensor("p5vvt", [128, 32 * 130], F32)).ap()
    PT = es.enter_context(nc.sbuf_tensor("p5PT", [128, 3 * 1024], F32)).ap()
    rd64 = es.enter_context(nc.sbuf_tensor("p5rd", [128, 512], F32)).ap()
    rdsb = es.enter_context(nc.sbuf_tensor("p5rdsb", [128, 512], F32)).ap()
    ctxe = es.enter_context(nc.sbuf_tensor("p5ctxe", [128, 512], F32)).ap()
    kof = ap["cc_k_out"].rearrange("(bl c p t) -> bl c p t", bl=2, c=8, p=128)
    vof = ap["ccvout"].rearrange("(bl t d) -> bl t d", bl=2, d=D)

    for hp in range(8):
        steps = []
        for half in range(2):
            steps.append(("SY", 16, (lambda s, hp=hp, half=half: s.dma_start(
                out=kThp[:, T * half:T * (half + 1)], in_=kof[half, hp])), []))
        steps.append(("SY", 16, (lambda s, hp=hp: s.dma_start(
            out=qThp, in_=ap["q1_h"][hp])), []))
        i_ones = len(steps)
        steps.append(("DV", 1, (lambda v: v.memset(
            vvt.rearrange("p (j two x) -> p j two x", two=2, x=65)[:, :, :, 64], 1.0)), []))
        for j in range(32):
            def vload(s, hp=hp, j=j):
                dst = vvt[:, 130 * j:130 * (j + 1)] \
                    .rearrange("p (two x) -> p two x", x=65)[:, :, 0:64]
                return s.dma_start(out=dst,
                                   in_=vof[j // 16, 128 * (j % 16):128 * (j % 16) + 128,
                                           128 * hp:128 * hp + 128]
                                   .rearrange("t (two d) -> t two d", two=2))
            steps.append(("SY", 16, vload, [i_ones]))
            if j < 16:
                def vmask(v, j=j):
                    return v.tensor_scalar(vvt[:, 130 * j:130 * (j + 1)],
                                           vvt[:, 130 * j:130 * (j + 1)],
                                           sb["cmcol"][:, j:j + 1], None, ALU.mult)
                steps.append(("DV", 1, vmask, [len(steps) - 1, i_ones]))
        b.sched(steps)

        for qg in range(4):
            steps = []
            idx_exp = {}
            idx_ctx = {}
            for j in range(32):
                bk = 2 * (j % 3)
                def stmm(t, hp=hp, qg=qg, j=j, bk=bk):
                    t.matmul(pb[bk], kThp[0:64, 128 * j:128 * (j + 1)],
                             qThp[0:64, 512 * qg:512 * (qg + 1)],
                             start=True, stop=True)
                    return t.matmul(pb[bk + 1], kThp[64:128, 128 * j:128 * (j + 1)],
                                    qThp[64:128, 512 * qg:512 * (qg + 1)],
                                    start=True, stop=True)
                deps = [idx_exp[j - 3]] if j >= 3 else []
                steps.append(("PE", 1, stmm, deps))
                i_st = len(steps) - 1

                def expf(sc, j=j, bk=bk):
                    return sc.activation(out=PT[:, 1024 * (j % 3):1024 * (j % 3) + 1024],
                                         in_=sb["PBIG"][:, 512 * bk:512 * bk + 1024],
                                         func=AF.Exp, scale=b.cv(0.125))
                deps = [i_st]
                if j >= 3:
                    deps.append(idx_ctx[j - 3])
                steps.append(("SA", 1, expf, deps))
                idx_exp[j] = len(steps) - 1

                def ctxf(t, j=j):
                    c0 = 1024 * (j % 3)
                    t.matmul(pb[6][0:65, :], vvt[:, 130 * j:130 * j + 65],
                             PT[:, c0:c0 + 512], start=(j == 0), stop=(j == 31))
                    return t.matmul(pb[7][0:65, :], vvt[:, 130 * j + 65:130 * (j + 1)],
                                    PT[:, c0 + 512:c0 + 1024],
                                    start=(j == 0), stop=(j == 31))
                steps.append(("PE", 1, ctxf, [idx_exp[j]]))
                idx_ctx[j] = len(steps) - 1
            prev = {}
            for h2 in range(2):
                deps = [idx_ctx[31]]
                if h2 == 1:
                    deps.append(prev["rdm"])
                steps.append(("DV", 1, (lambda v, h2=h2: v.reciprocal(
                    rd64[64:65, 0:512], pb[6 + h2][64:65, 0:512])), deps))
                i_rec = len(steps) - 1
                deps2 = [i_rec]
                if h2 == 1:
                    deps2.append(prev["rde"])
                steps.append(("PE", 1, (lambda t: t.matmul(
                    pb[2][0:64, 0:512], sb["kONESCOL"][64:65, 0:64],
                    rd64[64:65, 0:512], start=True, stop=True)), deps2))
                i_rdm = len(steps) - 1
                deps = [i_rdm]
                if h2 == 1:
                    deps.append(prev["cn"])
                steps.append(("SA", 1, (lambda sc: sc.copy(
                    out=rdsb[0:64, 0:512], in_=pb[2][0:64, 0:512])), deps))
                i_rde = len(steps) - 1
                deps = [i_rde]
                if h2 == 1:
                    deps.append(prev["store"])
                steps.append(("DV", 1, (lambda v, h2=h2: v.tensor_tensor(
                    ctxe[0:64, 0:512], pb[6 + h2][0:64, 0:512],
                    rdsb[0:64, 0:512], ALU.mult)), deps))
                i_cn = len(steps) - 1
                steps.append(("GP", 16, (lambda g, hp=hp, qg=qg, h2=h2: g.dma_start(
                    out=ap["ctxT_h"][2 * hp + h2, :, 512 * qg:512 * (qg + 1)],
                    in_=ctxe[0:64, 0:512])), [i_cn]))
                prev = {"rdm": i_rdm, "cn": i_cn, "rde": i_rde, "store": len(steps) - 1}
            b.sched(steps)
    es.close()

    es = contextlib.ExitStack()
    Wo = es.enter_context(nc.sbuf_tensor("p5Wo", [128, 8 * D], F32)).ap()
    cth = es.enter_context(nc.sbuf_tensor("p5cth", [128, 8 * T], F32)).ap()
    bor = es.enter_context(nc.sbuf_tensor("p5bor", [1, D], F32)).ap()
    x2c = es.enter_context(nc.sbuf_tensor("p5x2c", [128, D], F32)).ap()
    yb = es.enter_context(nc.sbuf_tensor("p5yb", [128, D], F32)).ap()
    tb = es.enter_context(nc.sbuf_tensor("p5tb", [128, D], F32)).ap()
    x3b = es.enter_context(nc.sbuf_tensor("p5x3b", [128, D], F32)).ap()
    steps = [("SY", 16, lambda s: s.dma_start(
        out=Wo, in_=ap["WoT1"].rearrange("(c p) o -> p c o", p=128))),
        ("SY", 16, lambda s: s.dma_start(
            out=bor, in_=ap["b1r"][0:1, 3 * D:4 * D]))]
    for hp in range(8):
        for h2 in range(2):
            def ldc(s, hp=hp, h2=h2):
                return s.dma_start(out=cth[64 * h2:64 * (h2 + 1),
                                           2048 * hp:2048 * (hp + 1)],
                                   in_=ap["ctxT_h"][2 * hp + h2])
            steps.append(("SY", 16, ldc))
    b.serial(steps)

    steps = []
    prev = {}
    for n in range(NT):
        i_mm = []
        for og in range(2):
            def womm(t, n=n, og=og):
                pairs = [(cth[:, 2048 * c + 128 * n: 2048 * c + 128 * n + 128],
                          Wo[:, D * c + 512 * og: D * c + 512 * og + 512])
                         for c in range(8)]
                pairs.append((sb["kONESROW"][0:1, 0:128],
                              bor[0:1, 512 * og:512 * og + 512]))
                return mm_accum(t, pb[og], pairs)
            deps = [prev[f"yadd{og}"]] if n >= 1 else []
            steps.append(("PE", 1, womm, deps))
            i_mm.append(len(steps) - 1)
        deps = [prev["x3add"]] if n >= 1 else []
        steps.append(("SY", 16, (lambda s, n=n: s.dma_start(
            out=x2c, in_=ap["x2_h"][128 * n:128 * (n + 1), :])), deps))
        i_x2 = len(steps) - 1
        deps = [i_x2] + ([prev["tread"]] if n >= 1 else [])
        steps.append(("SA", 1, (lambda sc, n=n: sc.activation(
            out=yb, in_=x2c, func=AF.Identity, scale=sb["rstd2"][:, n:n + 1],
            bias=sb["nmr2"][:, n:n + 1])), deps))
        i_ln = len(steps) - 1
        steps.append(("DV", 1, (lambda v: v.tensor_tensor(
            yb[:, 0:512], yb[:, 0:512], pb[0], ALU.add)), [i_mm[0], i_ln]))
        yadd0 = len(steps) - 1
        steps.append(("DV", 1, (lambda v: v.tensor_tensor(
            yb[:, 512:1024], yb[:, 512:1024], pb[1], ALU.add)), [i_mm[1], i_ln]))
        yadd1 = len(steps) - 1
        i_y = yadd1
        steps.append(("DV", 1, (lambda v: v.reduce_sum(
            sb["scr2"][:, 0:1], yb, axis=AX.X)), [i_y]))
        i_rs = len(steps) - 1
        steps.append(("SA", 1, (lambda sc: sc.activation(
            out=sb["scr"], in_=yb, func=AF.Square, accum_out=sb["scr2"][:, 1:2])), [i_y]))
        i_sq = len(steps) - 1
        steps.append(("SA", 1, (lambda sc: sc.activation(
            out=sb["scr2"][:, 2:3], in_=sb["scr2"][:, 0:1], func=AF.Square,
            scale=b.cv(1.0 / 32.0))), [i_rs]))
        i_t1 = len(steps) - 1
        steps.append(("DV", 1, (lambda v: v.tensor_tensor(
            sb["scr2"][:, 3:4], sb["scr2"][:, 1:2], sb["scr2"][:, 2:3],
            ALU.subtract)), [i_sq, i_t1]))
        i_t2 = len(steps) - 1
        steps.append(("SA", 1, (lambda sc: sc.activation(
            out=sb["scr2"][:, 4:5], in_=sb["scr2"][:, 3:4], func=AF.Ln,
            bias=b.cv(1024.0 * EPS))), [i_t2]))
        steps.append(("SA", 1, (lambda sc: sc.activation(
            out=sb["scr2"][:, 4:5], in_=sb["scr2"][:, 4:5], func=AF.Exp,
            scale=b.cv(-0.5), bias=b.cv(LN32))), [len(steps) - 1]))
        i_r = len(steps) - 1
        steps.append(("DV", 1, (lambda v: v.scalar_tensor_tensor(
            sb["scr2"][:, 5:6], sb["scr2"][:, 0:1], b.cv(-1.0 / 1024.0),
            sb["scr2"][:, 4:5], ALU.mult, ALU.mult)), [i_r]))
        i_nmr = len(steps) - 1
        steps.append(("SA", 1, (lambda sc: sc.activation(
            out=tb, in_=yb, func=AF.Identity, scale=sb["scr2"][:, 4:5],
            bias=sb["scr2"][:, 5:6])), [i_nmr]))
        i_t = len(steps) - 1
        prev["tread"] = i_t
        steps.append(("DV", 1, (lambda v: v.tensor_tensor(tb, tb, gmat, ALU.mult)),
                      [i_t]))
        steps.append(("DV", 1, (lambda v: v.tensor_tensor(tb, tb, bmat, ALU.add)),
                      [len(steps) - 1]))
        steps.append(("DV", 1, (lambda v: v.tensor_tensor(x3b, tb, x2c, ALU.add)),
                      [len(steps) - 1]))
        i_x3 = len(steps) - 1
        prev["x3add"] = i_x3
        prev["yadd0"] = yadd0
        prev["yadd1"] = yadd1
        steps.append(("GP", 16, (lambda g, n=n: g.dma_start(
            out=ap["x3_h"][128 * n:128 * (n + 1), :], in_=x3b)), [i_x3]))
        i_st3 = len(steps) - 1
        steps.append(("DV", 1, (lambda v: v.reduce_sum(
            sb["scr2"][:, 8:9], x3b, axis=AX.X)), [i_x3]))
        i_rs3 = len(steps) - 1
        steps.append(("SA", 1, (lambda sc: sc.activation(
            out=sb["scr"], in_=x3b, func=AF.Square, accum_out=sb["scr2"][:, 9:10])),
            [i_x3]))
        i_sq3 = len(steps) - 1
        steps.append(("SA", 1, (lambda sc: sc.activation(
            out=sb["scr2"][:, 10:11], in_=sb["scr2"][:, 8:9], func=AF.Square,
            scale=b.cv(1.0 / 32.0))), [i_rs3]))
        steps.append(("DV", 1, (lambda v: v.tensor_tensor(
            sb["scr2"][:, 11:12], sb["scr2"][:, 9:10], sb["scr2"][:, 10:11],
            ALU.subtract)), [i_sq3, len(steps) - 1]))
        i_t23 = len(steps) - 1
        steps.append(("SA", 1, (lambda sc: sc.activation(
            out=sb["scr2"][:, 12:13], in_=sb["scr2"][:, 11:12], func=AF.Ln,
            bias=b.cv(1024.0 * EPS))), [i_t23]))
        steps.append(("SA", 1, (lambda sc, n=n: sc.activation(
            out=sb["rstd3"][:, n:n + 1], in_=sb["scr2"][:, 12:13], func=AF.Exp,
            scale=b.cv(-0.5), bias=b.cv(LN32))), [len(steps) - 1]))
        i_r3 = len(steps) - 1
        steps.append(("DV", 1, (lambda v, n=n: v.scalar_tensor_tensor(
            sb["nmr3"][:, n:n + 1], sb["scr2"][:, 8:9], b.cv(-1.0 / 1024.0),
            sb["rstd3"][:, n:n + 1], ALU.mult, ALU.mult)), [i_r3]))
        i_nm3 = len(steps) - 1
        steps.append(("DV", 1, (lambda v: v.tensor_scalar(
            sb["scr2"][:, 13:14], sb["scr2"][:, 11:12], 1024.0 * EPS, None, ALU.add)),
            [i_t23]))
        steps.append(("DV", 1, (lambda v: v.reciprocal(
            sb["scr2"][:, 14:15], sb["scr2"][:, 13:14])), [len(steps) - 1]))
        steps.append(("DV", 1, (lambda v: v.tensor_tensor(
            sb["scr2"][:, 15:16], sb["scr2"][:, 11:12], sb["scr2"][:, 14:15],
            ALU.mult)), [len(steps) - 1]))
        steps.append(("DV", 1, (lambda v: v.tensor_scalar(
            sb["scr2"][:, 16:17], sb["scr2"][:, 15:16], EPS, None, ALU.add)),
            [len(steps) - 1]))
        steps.append(("SA", 1, (lambda sc: sc.activation(
            out=sb["scr2"][:, 17:18], in_=sb["scr2"][:, 16:17], func=AF.Ln)),
            [len(steps) - 1]))
        steps.append(("SA", 1, (lambda sc: sc.activation(
            out=sb["scr2"][:, 17:18], in_=sb["scr2"][:, 17:18], func=AF.Exp,
            scale=b.cv(-0.5))), [len(steps) - 1]))
        i_s3 = len(steps) - 1
        steps.append(("DV", 1, (lambda v, n=n: v.tensor_tensor(
            sb["sc3p"][:, n:n + 1], sb["rstd3"][:, n:n + 1], sb["scr2"][:, 17:18],
            ALU.mult)), [i_s3, i_r3]))
        steps.append(("DV", 1, (lambda v, n=n: v.tensor_tensor(
            sb["nb3p"][:, n:n + 1], sb["nmr3"][:, n:n + 1], sb["scr2"][:, 17:18],
            ALU.mult)), [i_s3, i_nm3]))
    b.sched(steps)
    es.close()
    esw.close()


def emit_p6(b):
    nc, block, ap, sb, pb = b.nc, b.block, b.ap, b.sb, b.pb
    es0 = contextlib.ExitStack()
    lngS = es0.enter_context(nc.sbuf_tensor("p6lng", [128, 8], F32)).ap()
    lnbS = es0.enter_context(nc.sbuf_tensor("p6lnb", [128, 8], F32)).ap()
    wc2 = es0.enter_context(nc.sbuf_tensor("p6wc2", [128, 16], F32)).ap()
    sI = es0.enter_context(nc.sbuf_tensor("p6sI", [128, 16], I32)).ap()
    tkf = es0.enter_context(nc.sbuf_tensor("p6tkf", [128, 16], F32)).ap()
    dspin = ap["cc_disp_in"].rearrange("(r c) -> r c", c=1032)
    dout8 = ap["cc_disp_out"].rearrange("(r c) -> r c", c=1032)
    rsin = ap["cc_rs_in"].rearrange("(r c) -> r c", c=D)

    esz = contextlib.ExitStack()
    zrow2 = esz.enter_context(nc.sbuf_tensor("p6zr", [128, D], F32)).ap()
    steps = [("DV", 1, (lambda v: v.memset(zrow2, 0.0)), [])]
    for z in range(129):
        r0 = 128 * z
        rows = min(128, 8 * T + 1 - r0)
        steps.append(("GP", 16, (lambda g, r0=r0, rows=rows: g.dma_start(
            out=rsin[r0:r0 + rows, :], in_=zrow2[0:rows, :])), [0]))
    b.sched(steps)

    es = contextlib.ExitStack()
    zb = [es.enter_context(nc.sbuf_tensor(f"p6z{i}", [128, D], F32)).ap() for i in range(2)]
    ztb = [es.enter_context(nc.sbuf_tensor(f"p6t{i}", [128, D], F32)).ap() for i in range(2)]
    x3c = [es.enter_context(nc.sbuf_tensor(f"p6x{i}", [128, D], F32)).ap() for i in range(2)]
    gws1 = es.enter_context(nc.sbuf_tensor("p6gw", [128, 64], F32)).ap()
    blg = es.enter_context(nc.sbuf_tensor("p6blg", [1, 8], F32)).ap()
    LG = es.enter_context(nc.sbuf_tensor("p6LG", [128, 128], F32)).ap()
    OH = es.enter_context(nc.sbuf_tensor("p6OH", [128, 128], F32)).ap()
    RK = es.enter_context(nc.sbuf_tensor("p6RK", [128, 128], F32)).ap()
    g16 = es.enter_context(nc.sbuf_tensor("p6g16", [128, 64], F32)).ap()
    wcol = es.enter_context(nc.sbuf_tensor("p6w", [128, 16], F32)).ap()
    bas = [es.enter_context(nc.sbuf_tensor(f"p6b{i}", [1, 8], F32)).ap() for i in range(2)]
    rowb = es.enter_context(nc.sbuf_tensor("p6row", [8, 2], F32)).ap()
    send = es.enter_context(nc.sbuf_tensor("p6send", [128, 1032], F32)).ap()
    crow = es.enter_context(nc.sbuf_tensor("p6crow", [1, 1032], F32)).ap()
    slotf = es.enter_context(nc.sbuf_tensor("p6sf", [128, 1], F32)).ap()
    sloti = es.enter_context(nc.sbuf_tensor("p6si", [128, 16], I32)).ap()

    b.serial([
        ("SY", 16, lambda s: s.dma_start(
            out=gws1.rearrange("p (c e) -> p c e", e=8),
            in_=ap["gwT1"].rearrange("(c p) e -> p c e", p=128))),
        ("SY", 16, lambda s: s.dma_start(out=lngS, in_=ap["lngc"][:, :])),
        ("SY", 16, lambda s: s.dma_start(out=lnbS, in_=ap["lnbc"][:, :])),
        ("PE", 1, lambda t: mm_accum(
            t, pb[0][0:1, 0:8],
            [(lnbS[:, c:c + 1], gws1[:, 8 * c:8 * c + 8]) for c in range(8)])),
        ("SA", 1, lambda sc: sc.copy(out=blg, in_=pb[0][0:1, 0:8])),
        ("DV", 1, lambda v: v.tensor_tensor(
            gws1.rearrange("p (c e) -> p c e", e=8),
            gws1.rearrange("p (c e) -> p c e", e=8),
            lngS.to_broadcast([128, 8, 8]), ALU.mult)),
        ("DV", 1, lambda v: v.memset(bas[0], 0.0)),
        ("DV", 1, lambda v: v.memset(send[:, 1026:1032], 0.0)),
        ("DV", 1, lambda v: v.memset(crow, 0.0)),
    ])

    S0 = b.snap()
    SAC, PEC = 10, 9

    @block.sync
    def _(s):
        b.wait_snap(s, S0)
        for n in range(NT):
            if n >= 2:
                s.wait_ge(b.SA.h, S0["SA"] + SAC * (n - 2) + 1)
            b.SY.inc(s.dma_start(out=x3c[n % 2],
                                 in_=ap["x3_h"][128 * n:128 * (n + 1), :]), 16)

    @block.scalar
    def _(sc):
        b.wait_snap(sc, S0)
        for n in range(NT):
            sc.wait_ge(b.SY.h, S0["SY"] + 16 * (n + 1))
            if n >= 2:
                sc.wait_ge(b.PE.h, S0["PE"] + PEC * (n - 1))
            b.SA.inc(sc.activation(out=zb[n % 2], in_=x3c[n % 2], func=AF.Identity,
                                   scale=sb["sc3p"][:, n:n + 1],
                                   bias=sb["nb3p"][:, n:n + 1]))
            for c in range(8):
                sc.wait_ge(b.PE.h, S0["PE"] + PEC * n + c + 1)
                b.SA.inc(sc.copy(out=ztb[n % 2][:, 128 * c:128 * (c + 1)],
                                 in_=pb[c % 2][:, 0:128]))
            sc.wait_ge(b.PE.h, S0["PE"] + PEC * n + 9)
            b.SA.inc(sc.copy(out=LG[:, 8 * n:8 * n + 8], in_=pb[2][:, 0:8]))

    @block.tensor
    def _(t):
        b.wait_snap(t, S0)
        for n in range(NT):
            t.wait_ge(b.SA.h, S0["SA"] + SAC * n + 1)
            for c in range(8):
                Tg = 8 * n + c
                if Tg >= 2:
                    pn, pc = (Tg - 2) // 8, (Tg - 2) % 8
                    t.wait_ge(b.SA.h, S0["SA"] + SAC * pn + 1 + pc + 1)
                b.PE.inc(t.transpose(pb[c % 2][:, 0:128],
                                     zb[n % 2][:, 128 * c:128 * (c + 1)],
                                     sb["kIDENT"]))
            t.wait_ge(b.SA.h, S0["SA"] + SAC * n + 9)
            if n >= 1:
                t.wait_ge(b.SA.h, S0["SA"] + SAC * (n - 1) + 10)
            pairs = [(ztb[n % 2][:, 128 * c:128 * (c + 1)], gws1[:, 8 * c:8 * c + 8])
                     for c in range(8)]
            pairs.append((sb["kONESROW"][0:1, 0:128], blg[0:1, 0:8]))
            b.PE.inc(mm_accum(t, pb[2][:, 0:8], pairs))

    L3 = LG.rearrange("p (n e) -> p n e", e=8)
    steps = [
        ("DV", 1, lambda v: v.tensor_reduce(g16[:, 0:16], L3, axis=AX.X, op=ALU.max)),
        ("DV", 1, lambda v: v.tensor_tensor(
            OH.rearrange("p (n e) -> p n e", e=8), L3,
            g16[:, 0:16].to_broadcast([128, 16, 8]), ALU.is_ge)),
        ("DV", 1, lambda v: v.tensor_tensor(
            RK.rearrange("p (n e) -> p n e", e=8), L3,
            g16[:, 0:16].to_broadcast([128, 16, 8]), ALU.subtract)),
        ("SA", 1, lambda sc: sc.activation(out=RK, in_=RK, func=AF.Exp)),
        ("DV", 1, lambda v: v.tensor_reduce(g16[:, 16:32],
                                            RK.rearrange("p (n e) -> p n e", e=8),
                                            axis=AX.X, op=ALU.add)),
        ("DV", 1, lambda v: v.reciprocal(wcol, g16[:, 16:32])),
    ]
    for n in range(NT):
        pa = n % 2
        def cummm(t, n=n, pa=pa):
            t.matmul(pb[4][:, 0:8], sb["kSTRICTLOW"], OH[:, 8 * n:8 * n + 8],
                     start=True, stop=False)
            return t.matmul(pb[4][:, 0:8], sb["kONESROW"][0:1, 0:128],
                            bas[pa][0:1, 0:8], start=False, stop=True)
        steps.append(("PE", 1, cummm))
        steps.append(("SA", 1, (lambda sc, n=n: sc.copy(
            out=RK[:, 8 * n:8 * n + 8], in_=pb[4][:, 0:8]))))
        def totmm(t, n=n):
            return t.matmul(pb[5][0:1, 0:8], sb["kONESCOL"][:, 0:1],
                            OH[:, 8 * n:8 * n + 8], start=True, stop=True)
        steps.append(("PE", 1, totmm))
        steps.append(("DV", 1, (lambda v, pa=pa: v.scalar_tensor_tensor(
            bas[1 - pa][0:1, 0:8], pb[5][0:1, 0:8], b.cv(1.0, 1),
            bas[pa][0:1, 0:8], ALU.mult, ALU.add))))
    basf = bas[NT % 2]
    steps += [
        ("PE", 1, (lambda t: t.transpose(pb[6][0:8, 0:1], basf[0:1, 0:8],
                                         sb["kIDENT"][0:1, 0:1]))),
        ("SA", 1, (lambda sc: sc.copy(out=rowb[:, 0:1], in_=pb[6][0:8, 0:1]))),
        ("PE", 1, (lambda t: t.matmul(pb[6][0:8, 8:9], sb["kSTRICTLOW8"],
                                      rowb[:, 0:1], start=True, stop=True))),
        ("SA", 1, (lambda sc: sc.copy(out=rowb[:, 1:2], in_=pb[6][0:8, 8:9]))),
        ("PE", 1, (lambda t: t.transpose(pb[6][0:1, 16:24], rowb[:, 1:2],
                                         sb["kIDENT"][0:8, 0:8]))),
        ("SA", 1, (lambda sc: sc.copy(out=crow[0:1, 8:16], in_=pb[6][0:1, 16:24]))),
        ("PE", 1, (lambda t: t.matmul(pb[6][:, 24:32], sb["kONESROW"][0:1, 0:128],
                                      crow[0:1, 8:16], start=True, stop=True))),
        ("SA", 1, (lambda sc: sc.copy(out=g16[:, 32:40], in_=pb[6][:, 24:32]))),
        ("DV", 1, (lambda v: v.tensor_copy(crow[0:1, 0:8], basf[0:1, 0:8]))),
        ("DV", 1, (lambda v: v.memset(crow[0:1, 1025:1026], 16384.0))),
        ("GP", 16, (lambda g: g.dma_start(out=dspin[T:T + 1, :], in_=crow))),
    ]
    for n in range(NT):
        steps.append(("DV", 1, (lambda v, n=n: v.tensor_tensor(
            g16[:, 40:48], RK[:, 8 * n:8 * n + 8], g16[:, 32:40], ALU.add))))
        steps.append(("DV", 1, (lambda v, n=n: v.tensor_tensor(
            g16[:, 48:56], g16[:, 40:48], OH[:, 8 * n:8 * n + 8], ALU.mult))))
        steps.append(("DV", 1, (lambda v: v.reduce_sum(
            slotf, g16[:, 48:56], axis=AX.X))))
        steps.append(("DV", 1, (lambda v, n=n: v.tensor_copy(
            sloti[:, n:n + 1], slotf))))
        steps.append(("SY", 16, (lambda s, n=n: s.dma_start(
            out=send[:, 0:1024], in_=ap["x3_h"][128 * n:128 * (n + 1), :]))))
        steps.append(("SA", 1, (lambda sc, n=n: sc.activation(
            out=send[:, 0:1024], in_=send[:, 0:1024], func=AF.Identity,
            scale=sb["sc3p"][:, n:n + 1], bias=sb["nb3p"][:, n:n + 1]))))
        steps.append(("DV", 1, (lambda v, n=n: v.tensor_copy(
            send[:, 1024:1025], wcol[:, n:n + 1]))))
        steps.append(("DV", 1, (lambda v, n=n: v.tensor_scalar(
            send[:, 1025:1026], sb["kSLOTF"][:, n:n + 1], sb["metaPs"][:, 0:1],
            None, ALU.add))))
        steps.append(("GP", 16, (lambda g, n=n: g.indirect_dma_start(
            dspin, IndirectOffsetOnAxis(ap=sloti[:, n:n + 1], axis=0),
            send, None))))
    if b.debug:
        for nm, src_ap in [("d_LG", LG), ("d_OH", OH), ("d_RK", RK),
                           ("d_wcol", wcol), ("d_crow", crow)]:
            steps.append(("GP", 16, (lambda g, nm=nm, src_ap=src_ap: g.dma_start(
                out=ap[nm][:, :], in_=src_ap))))
    b.serial(steps)
    es.close()

    b.serial([("CC", 1, lambda g: g.collective_compute(
        "AllGather", ALU.bypass, replica_groups=[list(range(8))],
        ins=[ap["cc_disp_in"][:]], outs=[ap["cc_disp_out"][:]]))])

    es = contextlib.ExitStack()
    cnts = es.enter_context(nc.sbuf_tensor("p6cn", [8, 32], F32)).ap()
    colv = es.enter_context(nc.sbuf_tensor("p6cv", [8, 16], F32)).ap()
    bmats = es.enter_context(nc.sbuf_tensor("p6bm", [128, 24], F32)).ap()
    ixf = [es.enter_context(nc.sbuf_tensor(f"p6ix{i}", [128, 16], F32)).ap()
           for i in range(2)]
    gidx = es.enter_context(nc.sbuf_tensor("p6gx", [128, 16], I32)).ap()
    msk = es.enter_context(nc.sbuf_tensor("p6mk", [128, 16], F32)).ap()

    steps = [
        ("SY", 16, lambda s: s.dma_start(
            out=cnts[:, 0:16],
            in_=ap["cc_disp_out"].rearrange("(r x) -> r x", x=2049 * 1032)
            [:, 2048 * 1032:2048 * 1032 + 16])),
        ("DV", 1, lambda v: v.tensor_tensor(cnts[:, 16:24], cnts[:, 0:8],
                                            sb["oh8s"], ALU.mult)),
        ("DV", 1, lambda v: v.reduce_sum(colv[:, 0:1], cnts[:, 16:24], axis=AX.X)),
        ("DV", 1, lambda v: v.tensor_tensor(cnts[:, 16:24], cnts[:, 8:16],
                                            sb["oh8s"], ALU.mult)),
        ("DV", 1, lambda v: v.reduce_sum(colv[:, 1:2], cnts[:, 16:24], axis=AX.X)),
        ("PE", 1, (lambda t: t.matmul(pb[0][0:8, 0:1], sb["kSTRICTLOW8"],
                                      colv[:, 0:1], start=True, stop=True))),
        ("SA", 1, (lambda sc: sc.copy(out=colv[:, 2:3], in_=pb[0][0:8, 0:1]))),
        ("DV", 1, (lambda v: v.tensor_tensor(colv[:, 3:4], colv[:, 2:3],
                                             colv[:, 0:1], ALU.add))),
        ("DV", 1, (lambda v: v.tensor_tensor(colv[:, 4:5], sb["kR2049"],
                                             colv[:, 1:2], ALU.add))),
        ("DV", 1, (lambda v: v.tensor_tensor(colv[:, 5:6], colv[:, 4:5],
                                             colv[:, 2:3], ALU.subtract))),
        ("SY", 16, (lambda s: s.dma_start(out=ap["cc_rs_out"][0:8],
                                          in_=colv[:, 5:6]))),
        ("SY", 16, (lambda s: s.dma_start(out=colv[0:7, 6:7],
                                          in_=ap["cc_rs_out"][1:8],
                                          allow_slow_non_contiguous=True))),
        ("DV", 1, (lambda v: v.tensor_tensor(colv[0:7, 7:8], colv[0:7, 6:7],
                                             colv[0:7, 5:6], ALU.subtract))),
        ("DV", 1, (lambda v: v.tensor_scalar(cnts[:, 24:32], sb["kDIAG8"],
                                             colv[:, 3:4], None, ALU.mult))),
        ("PE", 1, (lambda t: t.matmul(pb[0][:, 0:8], sb["kONES16x128"][0:8, :],
                                      cnts[:, 24:32], start=True, stop=True))),
        ("SA", 1, (lambda sc: sc.copy(out=bmats[:, 0:8], in_=pb[0][:, 0:8]))),
        ("DV", 1, (lambda v: v.tensor_scalar(cnts[:, 24:32], sb["kDIAG8"],
                                             colv[0:8, 7:8], None, ALU.mult))),
        ("PE", 1, (lambda t: t.matmul(pb[0][:, 8:16], sb["kONES16x128"][0:8, :],
                                      cnts[:, 24:32], start=True, stop=True))),
        ("SA", 1, (lambda sc: sc.copy(out=bmats[:, 8:16], in_=pb[0][:, 8:16]))),
        ("DV", 1, (lambda v: v.tensor_scalar(cnts[:, 24:32], sb["kDIAG8"],
                                             colv[:, 5:6], None, ALU.mult))),
        ("PE", 1, (lambda t: t.matmul(pb[0][:, 16:24], sb["kONES16x128"][0:8, :],
                                      cnts[:, 24:32], start=True, stop=True))),
        ("SA", 1, (lambda sc: sc.copy(out=bmats[:, 16:24], in_=pb[0][:, 16:24]))),
        ("DV", 1, (lambda v: v.tensor_scalar(ixf[0], sb["kSLOTF"],
                                             bmats[:, 16:17], None, ALU.add))),
    ]
    cur = 0
    for r in range(7):
        steps.append(("DV", 1, (lambda v, r=r: v.tensor_scalar(
            msk, sb["kSLOTF"], bmats[:, r:r + 1], None, ALU.is_ge))))
        steps.append(("DV", 1, (lambda v, r=r: v.tensor_scalar(
            msk, msk, bmats[:, 8 + r:8 + r + 1], None, ALU.mult))))
        steps.append(("DV", 1, (lambda v, cur=cur: v.tensor_tensor(
            ixf[1 - cur], ixf[cur], msk, ALU.add))))
        cur = 1 - cur
    steps += [
        ("DV", 1, (lambda v: v.tensor_scalar(
            msk, sb["kSLOTF"], bmats[:, 7:8], None, ALU.is_ge))),
        ("DV", 1, (lambda v, cur=cur: v.tensor_tensor(
            ixf[1 - cur], msk, ixf[cur], ALU.mult))),
        ("DV", 1, (lambda v, cur=cur: v.tensor_tensor(
            ixf[1 - cur], ixf[cur], ixf[1 - cur], ALU.subtract))),
        ("DV", 1, (lambda v, cur=cur: v.scalar_tensor_tensor(
            ixf[cur], msk, b.cv(2048.0), ixf[1 - cur], ALU.mult, ALU.add))),
        ("DV", 1, (lambda v, cur=cur: v.tensor_copy(gidx, ixf[cur]))),
    ]
    if b.debug:
        steps.append(("GP", 16, (lambda g, cur=cur: g.dma_start(
            out=ap["d_ixf"][:, :], in_=ixf[cur]))))
    b.serial(steps)

    dT = es.enter_context(nc.sbuf_tensor("p6dT", [128, 8 * T], F32)).ap()
    dsp = es.enter_context(nc.sbuf_tensor("p6dsp", [128, 1032], F32)).ap()
    steps = []
    prev_g = None
    for n in range(NT):
        deps = [prev_g] if prev_g is not None else []
        steps.append(("GP", 16, (lambda g, n=n: g.indirect_dma_start(
            dsp, None, dout8,
            IndirectOffsetOnAxis(ap=gidx[:, n:n + 1], axis=0))), deps))
        i_ga = len(steps) - 1
        steps.append(("DV", 1, (lambda v, n=n: v.tensor_copy(
            wc2[:, n:n + 1], dsp[:, 1024:1025])), [i_ga]))
        steps.append(("DV", 1, (lambda v, n=n: v.tensor_copy(
            sI[:, n:n + 1], dsp[:, 1025:1026])), [i_ga]))
        if b.debug:
            steps.append(("DV", 1, (lambda v, n=n: v.tensor_copy(
                tkf[:, n:n + 1], dsp[:, 1025:1026])), [i_ga]))
        for c in range(8):
            deps = [i_ga]
            if c >= 2:
                deps.append(len(steps) - 3)
            steps.append(("PE", 1, (lambda t, c=c: t.transpose(
                pb[c % 2][:, 0:128], dsp[:, 128 * c:128 * (c + 1)],
                sb["kIDENT"])), deps))
            steps.append(("SA", 1, (lambda sc, n=n, c=c: sc.activation(
                out=dT[:, 2048 * c + 128 * n: 2048 * c + 128 * n + 128],
                in_=pb[c % 2][:, 0:128], func=AF.Identity,
                scale=lngS[:, c:c + 1], bias=lnbS[:, c:c + 1])),
                [len(steps) - 1]))
        prev_g = len(steps) - 1
    b.sched(steps)
    if b.debug:
        b.serial([
            ("GP", 16, lambda g: g.dma_start(out=ap["d_wc2"][:, :], in_=wc2)),
            ("GP", 16, lambda g: g.dma_start(out=ap["d_tkf"][:, :], in_=tkf)),
            ("GP", 16, lambda g: g.dma_start(out=ap["d_dT"][:, :], in_=dT)),
        ])

    wib = [es.enter_context(nc.sbuf_tensor(f"p6wi{i}", [128, 1024], F32)).ap()
           for i in range(2)]
    gb = es.enter_context(nc.sbuf_tensor("p6gb", [128, 2048], F32)).ap()
    steps = []
    last_store = {}
    last_evac = {}
    last_mm = {}
    for io in range(32):
        deps = [last_store[io - 2]] if io >= 2 else []
        def wload(s, io=io):
            return s.dma_start(
                out=wib[io % 2].rearrange("p (a q) -> p a q", q=128),
                in_=ap["wi_e"].rearrange("(c p) (a q) -> p c a q", p=128, q=128)
                [:, :, io, :])
        steps.append(("SY", 16, wload, deps))
        i_w = len(steps) - 1
        for tg in range(4):
            def gmm(t, io=io, tg=tg):
                pairs = [(wib[io % 2][:, 128 * c:128 * (c + 1)],
                          dT[:, 2048 * c + 512 * tg: 2048 * c + 512 * tg + 512])
                         for c in range(8)]
                return mm_accum(t, pb[4 + tg], pairs)
            deps = [i_w]
            if (io - 1, tg) in last_evac:
                deps.append(last_evac[(io - 1, tg)])
            steps.append(("PE", 1, gmm, deps))
            last_mm[(io, tg)] = len(steps) - 1
            deps = [len(steps) - 1]
            if (io - 1, tg) in last_store:
                deps.append(last_store[(io - 1, tg)])
            steps.append(("SA", 1, (lambda sc, tg=tg: sc.activation(
                out=gb[:, 512 * tg:512 * (tg + 1)], in_=pb[4 + tg],
                func=AF.Gelu)), deps))
            last_evac[(io, tg)] = len(steps) - 1
            steps.append(("GP", 16, (lambda g, io=io, tg=tg: g.dma_start(
                out=ap["g_h"][io, :, 512 * tg:512 * (tg + 1)],
                in_=gb[:, 512 * tg:512 * (tg + 1)])), [len(steps) - 1]))
            last_store[(io, tg)] = len(steps) - 1
            last_store[io] = len(steps) - 1
    b.sched(steps)
    es.close()

    es = contextlib.ExitStack()
    Wo2 = es.enter_context(nc.sbuf_tensor("p6Wo", [128, 32 * D], F32)).ap()
    gcb = [es.enter_context(nc.sbuf_tensor(f"p6gc{i}", [128, 4096], F32)).ap()
           for i in range(2)]
    ot = es.enter_context(nc.sbuf_tensor("p6ot", [128, D], F32)).ap()
    b.serial([("SY", 16, lambda s: s.dma_start(
        out=Wo2.rearrange("p (c o) -> p c o", o=D),
        in_=ap["wo_e"].rearrange("(c p) o -> p c o", p=128)))])
    steps = []
    prev = {}
    for n in range(NT):
        deps = [prev["sc", n - 2]] if n >= 2 else []
        steps.append(("SY", 16, (lambda s, n=n: s.dma_start(
            out=gcb[n % 2].rearrange("p (c q) -> p c q", q=128),
            in_=ap["g_h"][:, :, 128 * n:128 * (n + 1)]
            .rearrange("c p q -> p c q"))), deps))
        i_ld = len(steps) - 1
        i_mm = []
        for og in range(2):
            def omm(t, n=n, og=og):
                pairs = [(gcb[n % 2][:, 128 * c:128 * (c + 1)],
                          Wo2[:, D * c + 512 * og: D * c + 512 * og + 512])
                         for c in range(32)]
                return mm_accum(t, pb[og], pairs)
            deps = [i_ld]
            if n >= 1:
                deps.append(prev["ev", n - 1])
            steps.append(("PE", 1, omm, deps))
            i_mm.append(len(steps) - 1)
        def oev(sc, n=n):
            sc.activation(out=ot[:, 0:512], in_=pb[0], func=AF.Identity,
                          scale=wc2[:, n:n + 1])
            return sc.activation(out=ot[:, 512:1024], in_=pb[1],
                                 func=AF.Identity, scale=wc2[:, n:n + 1])
        deps = list(i_mm)
        if n >= 1:
            deps.append(prev["sc", n - 1])
        steps.append(("SA", 2, oev, deps))
        prev["ev", n] = len(steps) - 1
        steps.append(("GP", 16, (lambda g, n=n: g.indirect_dma_start(
            rsin, IndirectOffsetOnAxis(ap=sI[:, n:n + 1], axis=0),
            ot, None)), [len(steps) - 1]))
        prev["sc", n] = len(steps) - 1
    b.sched(steps)
    es.close()

    b.serial([("CC", 1, lambda g: g.collective_compute(
        "ReduceScatter", ALU.add, replica_groups=[list(range(8))],
        ins=[ap["cc_rs_in"][0:8 * T * D]], outs=[ap["cc_rs_out"][:]]))])

    esf = contextlib.ExitStack()
    xa = [esf.enter_context(nc.sbuf_tensor(f"p6xa{i}", [128, D], F32)).ap()
          for i in range(2)]
    ra = [esf.enter_context(nc.sbuf_tensor(f"p6ra{i}", [128, D], F32)).ap()
          for i in range(2)]
    za = [esf.enter_context(nc.sbuf_tensor(f"p6za{i}", [128, D], F32)).ap()
          for i in range(2)]
    rso = ap["cc_rs_out"].rearrange("(t d) -> t d", d=D)
    steps = []
    prev = {}
    for n in range(NT):
        deps = [prev[n - 2]] if n >= 2 else []
        steps.append(("SY", 16, (lambda s, n=n: s.dma_start(
            out=xa[n % 2], in_=ap["x3_h"][128 * n:128 * (n + 1), :])), deps))
        i_x = len(steps) - 1
        steps.append(("SY", 16, (lambda s, n=n: s.dma_start(
            out=ra[n % 2], in_=rso[128 * n:128 * (n + 1), :])), deps))
        i_r = len(steps) - 1
        steps.append(("SA", 1, (lambda sc, n=n: sc.activation(
            out=za[n % 2], in_=xa[n % 2], func=AF.Identity,
            scale=sb["rstd3"][:, n:n + 1], bias=sb["nmr3"][:, n:n + 1])), [i_x]))
        steps.append(("DV", 1, (lambda v, n=n: v.tensor_tensor(
            za[n % 2], za[n % 2], xa[n % 2], ALU.add)), [len(steps) - 1]))
        steps.append(("DV", 1, (lambda v, n=n: v.tensor_tensor(
            za[n % 2], za[n % 2], ra[n % 2], ALU.add)), [len(steps) - 1, i_r]))
        steps.append(("GP", 16, (lambda g, n=n: g.dma_start(
            out=ap["x4_out"][128 * n:128 * (n + 1), :], in_=za[n % 2])),
            [len(steps) - 1]))
        prev[n] = len(steps) - 1
    b.sched(steps)
    esf.close()
    esz.close()
    es0.close()


def emit_final(b, upto):
    nc, block, ap = b.nc, b.block, b.ap
    S0 = b.snap()
    src = {2: "x1_h", 4: "x2_h", 5: "x3_h"}.get(upto, "x1_h")

    @block.gpsimd
    def _(g):
        b.wait_snap(g, S0)
        if upto < 6:
            b.GP.inc(g.dma_start(out=ap["x4_out"][:, :], in_=ap[src][:, :]), 16)
        if b.debug:
            b.GP.inc(g.dma_start(out=ap["d_x1"][:, :], in_=ap["x1_h"][:, :]), 16)
            b.GP.inc(g.dma_start(out=ap["d_xln0T"][:, :, :, :], in_=ap["xln0T_h"][:, :, :, :]), 16)
            b.GP.inc(g.dma_start(out=ap["d_beta"][:, :], in_=ap["beta_h"][:, :]), 16)
            b.GP.inc(g.dma_start(out=ap["d_qspill"][:, :, :], in_=ap["qspill_h"][:, :, :]), 16)
            b.GP.inc(g.dma_start(out=ap["d_delta"][:, :, :], in_=ap["delta_h"][:, :, :]), 16)
            for nm, key in [("lmsum", "d_lmsum"), ("bmean", "d_bmean"),
                            ("curve", "d_curve"), ("icurve", "d_icurve"),
                            ("stateAll", "d_state"), ("Sinh", "d_sinh"),
                            ("cumdec", "d_cumdec"), ("stats", "d_stats")]:
                b.GP.inc(g.dma_start(out=ap[key][:, :] if ap[key].shape[0] != NT else ap[key],
                                     in_=b.sb[nm]), 16)
            if upto >= 4:
                b.GP.inc(g.dma_start(out=ap["d_x2"][:, :], in_=ap["x2_h"][:, :]), 16)
            if upto >= 5:
                b.GP.inc(g.dma_start(out=ap["d_x3"][:, :], in_=ap["x3_h"][:, :]), 16)
            if upto >= 6:
                b.GP.inc(g.dma_start(out=ap["d_rs"][:, :],
                                     in_=ap["cc_rs_out"].rearrange("(t d) -> t d", d=D)), 16)
        g.wait_ge(b.GP.h, b.GP.n)


def prep_inputs(inputs, core):
    z = {k: np.ascontiguousarray(np.asarray(v, dtype=np.float32)) for k, v in inputs.items()}
    d = {}
    x = z["L0_attn_x"].reshape(8, T, D)
    d["x_in"] = x[core]
    d["WqT0"] = z["L0_attn_Wq"].T
    d["WkT0"] = z["L0_attn_Wk"].T
    d["WvT0"] = z["L0_attn_Wv"].T
    d["WbT0"] = z["L0_attn_Wbeta"].T
    d["WaT0"] = z["L0_attn_Walpha"].T
    b0 = np.stack([z["L0_attn_bq"], z["L0_attn_bk"], z["L0_attn_bv"],
                   z["L0_attn_bbeta"], z["L0_attn_balpha"]], 0)
    d["b0r"] = b0[2:5].reshape(1, 3 * D)
    d["b0c"] = b0.reshape(5, 8, 128).transpose(2, 0, 1).reshape(128, 40)
    d["gwT0"] = z["L0_ffn_gate_w"].T
    d["up_w"] = z["L0_ffn_up_w"]
    d["up_bc"] = z["L0_ffn_up_b"].reshape(8, 8, 128).transpose(2, 0, 1).reshape(128, 64)
    d["down_w"] = z["L0_ffn_down_w"]
    d["down_br"] = z["L0_ffn_down_b"].reshape(1, 8 * D)
    d["sh_up_w"] = z["L0_ffn_sh_up_w"]
    d["sh_up_bc"] = z["L0_ffn_sh_up_b"].reshape(8, 128).T.copy()
    d["sh_down_w"] = z["L0_ffn_sh_down_w"]
    d["sh_down_br"] = z["L0_ffn_sh_down_b"].reshape(1, D)
    d["band"] = z["L1_attn_band_importance"]
    d["WqT1"] = z["L1_attn_Wq"].T
    d["WkT1"] = z["L1_attn_Wk"].T
    d["WvT1"] = z["L1_attn_Wv"].T
    d["WoT1"] = z["L1_attn_Wo"].T
    b1 = np.stack([z["L1_attn_bq"], z["L1_attn_bk"], z["L1_attn_bv"], z["L1_attn_bo"]], 0)
    d["b1r"] = b1.reshape(1, 4 * D)
    d["b1c"] = b1.reshape(4, 8, 128).transpose(2, 0, 1).reshape(128, 32)
    d["g1r"] = z["L1_attn_gamma"].reshape(1, D)
    d["be1r"] = z["L1_attn_beta"].reshape(1, D)
    d["gwT1"] = z["L1_ffn_gate_weight"].T
    d["lngc"] = z["L1_ffn_ln_gamma"].reshape(8, 128).T.copy()
    d["lnbc"] = z["L1_ffn_ln_beta"].reshape(8, 128).T.copy()
    d["wi_e"] = z["L1_ffn_wi"][core]
    d["wo_e"] = z["L1_ffn_wo"][core]
    meta = np.zeros((128, 8), np.float32)
    meta[:, 0] = core * T
    meta[:, 1] = core % 2
    meta[:, 2] = 1.0 if (core % 2 == 1) else 0.0
    meta[:, 3] = 0.0
    d["metaP"] = meta
    oh = np.zeros((8, 8), np.float32)
    oh[:, core] = 1.0
    d["oh8"] = oh
    return d


def kernel(**inputs):
    nc = build_program(debug=False, upto=BUILD_UPTO)
    in_maps = [prep_inputs(inputs, c) for c in range(8)]
    res = run_bass_kernel_spmd(nc, in_maps, core_ids=list(range(8)))
    out = np.concatenate([res.results[c]["x4_out"] for c in range(8)], axis=0)
    return out.reshape(4, 4096, 1024)


# version 03
# speedup vs baseline: 1.0864x; 1.0864x over previous
import contextlib
import numpy as np
import concourse.bass as bass
import concourse.mybir as mybir
from concourse.bass import IndirectOffsetOnAxis
from concourse.bass_utils import run_bass_kernel_spmd

F32 = mybir.dt.float32
I32 = mybir.dt.int32
AF = mybir.ActivationFunctionType
ALU = mybir.AluOpType
AX = mybir.AxisListType

T = 2048
D = 1024
NT = 16
LN32 = float(np.log(32.0))
EPS = 1e-5
DEBUG = True
BUILD_UPTO = 6

PERM = [0, 2, 4, 6, 8, 10, 12, 14, 1, 3, 5, 7, 9, 11, 13, 15]
IPERM = [PERM.index(h) for h in range(16)]


class Sem:
    __slots__ = ("h", "n")

    def __init__(self, nc, name):
        self.h = nc.alloc_semaphore(name)
        self.n = 0

    def inc(self, ins, k=1):
        ins.then_inc(self.h, k)
        self.n += k
        return self.n

    def wait(self, eng, val=None):
        eng.wait_ge(self.h, self.n if val is None else val)


def consts_arrays():
    c = {}
    c["IDENT"] = np.eye(128, dtype=np.float32)
    c["LINC64"] = (np.triu(np.ones((128, 128), np.float32)) / np.float32(64.0))
    c["STRICTLOW"] = np.triu(np.ones((128, 128), np.float32), 1)
    c["TRIL"] = np.triu(np.ones((128, 128), np.float32))
    sp = np.zeros((16, 128), np.float32)
    for h in range(16):
        sp[h, (h % 2) * 64:(h % 2) * 64 + 64] = 1.0
    c["SELPAR"] = sp
    dc = np.zeros((16, 8), np.float32)
    for h in range(16):
        dc[h, h // 2] = 1.0
    c["DIAGC"] = dc
    r16 = np.zeros((16, 256), np.float32)
    for h in range(16):
        r16[h, h::16] = 1.0
    c["REPL16"] = r16
    c["ONES16x128"] = np.ones((16, 128), np.float32)
    c["ONESROW"] = np.ones((1, 512), np.float32)
    c["ONESCOL"] = np.ones((128, 64), np.float32)
    c["STRICTLOW8"] = np.triu(np.ones((8, 8), np.float32), 1)
    c["DIAG8"] = np.eye(8, dtype=np.float32)
    c["SLOTF"] = np.add.outer(np.arange(128, dtype=np.float32),
                              128.0 * np.arange(16, dtype=np.float32))
    c["R2049"] = (2049.0 * np.arange(8, dtype=np.float32)).reshape(8, 1)
    return c


def mm_accum(t, psum, pairs):
    ins = None
    n = len(pairs)
    for i, (lhs, rhs) in enumerate(pairs):
        ins = t.matmul(psum, lhs, rhs, start=(i == 0), stop=(i == n - 1))
    return ins


class B:
    ENG = {"PE": "tensor", "SA": "scalar", "DV": "vector", "GP": "gpsimd",
           "SY": "sync", "CC": "gpsimd"}

    def __init__(self, nc, block, ap, sb, pb, debug):
        self.nc, self.block, self.ap, self.sb, self.pb = nc, block, ap, sb, pb
        self.debug = debug
        self.PE = Sem(nc, "sPE")
        self.SA = Sem(nc, "sSA")
        self.DV = Sem(nc, "sDV")
        self.GP = Sem(nc, "sGP")
        self.SY = Sem(nc, "sSY")
        self.CC = Sem(nc, "sCC")
        self.all = dict(PE=self.PE, SA=self.SA, DV=self.DV, GP=self.GP,
                        SY=self.SY, CC=self.CC)

    CVALS = {}

    def cv(self, val, p=128):
        key = float(val)
        if key not in B.CVALS:
            raise KeyError(f"const {val} not in pool")
        return self.sb["cpool"][0:p, B.CVALS[key]:B.CVALS[key] + 1]

    def snap(self):
        return {k: s.n for k, s in self.all.items()}

    def wait_snap(self, eng, snap, skip=()):
        for k, v in snap.items():
            if v and k not in skip:
                self.all[k].wait(eng, v)

    def sched(self, steps):
        base = self.snap()
        cnt = dict(base)
        after = []
        eng_of = []
        for e, inc, fn, deps in steps:
            cnt[e] += inc
            after.append(dict(cnt))
            eng_of.append(e)
        by_sec = {}
        for idx, (e, inc, fn, deps) in enumerate(steps):
            by_sec.setdefault(self.ENG[e], []).append((idx, e, inc, fn, deps))
        for sec, items in by_sec.items():
            def make(items=items, sec=sec):
                def emitter(eng):
                    first = True
                    for idx, e, inc, fn, deps in items:
                        if first:
                            for k, v in base.items():
                                if v:
                                    self.all[k].wait(eng, v)
                            first = False
                        for d in deps:
                            de = eng_of[d]
                            if self.ENG[de] == sec and de not in ("GP", "SY", "CC"):
                                continue
                            self.all[de].wait(eng, after[d][de])
                        ins = fn(eng)
                        if inc:
                            ins.then_inc(self.all[e].h, inc)
                        if sec == "vector":
                            eng.drain()
                return emitter
            getattr(self.block, sec)(make())
        for e, inc, fn, deps in steps:
            self.all[e].n += inc

    def serial(self, steps):
        base = self.snap()
        cnt = dict(base)
        plan = []
        for e, inc, fn in steps:
            plan.append((e, inc, fn, dict(cnt)))
            cnt[e] += inc
        by_sec = {}
        for idx, (e, inc, fn, pre) in enumerate(plan):
            by_sec.setdefault(self.ENG[e], []).append((idx, e, inc, fn, pre))
        for sec, items in by_sec.items():
            def make(items=items, sec=sec):
                def emitter(eng):
                    first = True
                    for idx, e, inc, fn, pre in items:
                        if first:
                            for k, v in base.items():
                                if v:
                                    self.all[k].wait(eng, v)
                            first = False
                        for k, v in pre.items():
                            if v > base[k]:
                                if self.ENG[k] != sec or k in ("GP", "SY", "CC"):
                                    self.all[k].wait(eng, v)
                        ins = fn(eng)
                        if inc:
                            ins.then_inc(self.all[e].h, inc)
                        if sec == "vector":
                            eng.drain()
                return emitter
            getattr(self.block, sec)(make())
        for e, inc, fn in steps:
            self.all[e].n += inc


for _i, _v in enumerate([1024.0 * EPS, LN32, -0.5, -1.0, 1.0, 0.125, 1.0 / 32.0, -1.0 / 1024.0, 0.0, EPS, -10.0, 1e-20, -1e9, 0.125, 2048.0]):
    B.CVALS[float(_v)] = _i


def build_program(debug=DEBUG, upto=BUILD_UPTO):
    nc = bass.Bass("TRN2", target_bir_lowering=False)
    ap = {}

    def din(name, shape, dtype=F32):
        ap[name] = nc.dram_tensor(name, list(shape), dtype, kind="ExternalInput").ap()

    def dout(name, shape, dtype=F32):
        ap[name] = nc.dram_tensor(name, list(shape), dtype, kind="ExternalOutput").ap()

    def dint(name, shape, addr_space=None):
        kw = {"addr_space": addr_space} if addr_space else {}
        ap[name] = nc.dram_tensor(name, list(shape), F32, kind="Internal", **kw).ap()

    din("x_in", (T, D))
    for w in ["WqT0", "WkT0", "WvT0", "WbT0", "WaT0"]:
        din(w, (D, D))
    din("b0c", (128, 5 * 8))
    din("b0r", (1, 3 * D))
    din("gwT0", (D, 8))
    din("up_w", (8, D, D))
    din("up_bc", (128, 8 * 8))
    din("down_w", (8, D, D))
    din("down_br", (1, 8 * D))
    din("sh_up_w", (D, D))
    din("sh_up_bc", (128, 8))
    din("sh_down_w", (D, D))
    din("sh_down_br", (1, D))
    din("band", (1, T))
    for w in ["WqT1", "WkT1", "WvT1", "WoT1"]:
        din(w, (D, D))
    din("b1c", (128, 4 * 8))
    din("b1r", (1, 4 * D))
    din("g1r", (1, D))
    din("be1r", (1, D))
    din("gwT1", (D, 8))
    din("lngc", (128, 8))
    din("lnbc", (128, 8))
    din("wi_e", (D, 4096))
    din("wo_e", (4096, D))
    din("metaP", (128, 8))
    din("oh8", (8, 8))

    CA = consts_arrays()
    for k, v in CA.items():
        ap["c_" + k] = nc.inline_tensor(np.ascontiguousarray(v), "c_" + k).ap()

    dout("x4_out", (T, D))
    if debug:
        dout("d_x1", (T, D))
        dout("d_x2", (T, D))
        dout("d_x3", (T, D))
        dout("d_xln0T", (NT, 8, 128, 128))
        dout("d_beta", (T, D))
        dout("d_qspill", (NT, 128, D))
        dout("d_delta", (NT, 128, 512))
        dout("d_lmsum", (128, NT * 16))
        dout("d_bmean", (128, NT * 16))
        dout("d_curve", (128, NT * 16))
        dout("d_icurve", (128, NT * 16))
        dout("d_state", (128, 512))
        dout("d_sinh", (128, 512))
        dout("d_cumdec", (16, 16))
        dout("d_stats", (128, 6 * NT))
        dout("d_LG", (128, 128))
        dout("d_OH", (128, 128))
        dout("d_RK", (128, 128))
        dout("d_wcol", (128, 16))
        dout("d_crow", (1, 1032))
        dout("d_ixf", (128, 16))
        dout("d_wc2", (128, 16))
        dout("d_tkf", (128, 16))
        dout("d_dT", (128, 8 * T))
        dout("d_rs", (T, D))

    dint("xln0T_h", (NT, 8, 128, 128))
    dint("beta_h", (T, D))
    dint("x1_h", (T, D))
    dint("qspill_h", (NT, 128, D))
    dint("delta_h", (NT, 128, 512))
    dint("xln1T_h", (NT, 8, 128, 128))
    dint("yacc_h", (T, D))
    dint("x2_h", (T, D))
    dint("x3_h", (T, D))
    dint("ctxT_h", (16, 64, T))
    dint("q1_h", (8, 128, T))
    dint("g_h", (32, 128, T))
    dint("cc_st_in", (128 * 512,))
    dint("cc_st_out", (2 * 128 * 512,))
    dint("cc_k_in", (8 * 128 * T,))
    dint("cc_k_out", (2 * 8 * 128 * T,))
    dint("ccvin", (T * D,))
    dint("ccvout", (2 * T * D,))
    dint("cc_disp_in", ((T + 1) * 1032,))
    ap["cc_disp_out"] = nc.dram_tensor("cc_disp_out", [8 * (T + 1) * 1032], F32,
                                       kind="Internal", addr_space="Shared").ap()
    dint("cc_rs_in", ((8 * T + 1) * D,))
    dint("cc_rs_out", (T * D,))

    sb = {}

    def salloc(name, shape, dtype=F32):
        sb[name] = nc.alloc_sbuf_tensor(name, list(shape), dtype).ap()
        return sb[name]

    for k, v in CA.items():
        salloc("k" + k, v.shape)
    salloc("metaPs", (128, 8))
    salloc("oh8s", (8, 8))
    salloc("b0cS", (128, 40))
    salloc("b0cs", (128, 16))
    salloc("stats", (128, 6 * NT))
    for nm in ["rstd2", "nmr2", "rstd3", "nmr3", "sc3p", "nb3p"]:
        salloc(nm, (128, NT))
    salloc("lmsum", (128, NT * 16))
    salloc("bmean", (128, NT * 16))
    salloc("curve", (128, NT * 16))
    salloc("icurve", (128, NT * 16))
    salloc("totb", (16, 16))
    salloc("curvec", (128, NT * 16))
    salloc("asumC", (16, 16))
    salloc("cumdec", (16, 16))
    salloc("stateAll", (128, 512))
    salloc("comb", (128, NT * 8))
    salloc("combT", (8, T))
    salloc("cmcol", (128, 16))
    salloc("Sinh", (128, 512))
    salloc("scr", (128, D))
    salloc("scr2", (128, 64))
    salloc("cpool", (128, 16))

    pbig = nc.alloc_psum_tensor("pbig", [128, 4096], F32).ap()
    pb = [pbig[:, 512 * i:512 * (i + 1)] for i in range(8)]
    sb["PBIG"] = pbig

    with nc.Block() as block:
        b = B(nc, block, ap, sb, pb, debug)
        emit_preamble(b)
        emit_p0(b)
        emit_p1(b)
        emit_p2(b)
        if upto >= 4:
            emit_p3(b)
            emit_p4(b)
        if upto >= 5:
            emit_p5(b)
        if upto >= 6:
            emit_p6(b)
        emit_final(b, upto)
    return nc


def emit_preamble(b):
    nc, block, ap, sb = b.nc, b.block, b.ap, b.sb
    names = list(consts_arrays().keys())

    @block.sync
    def _(s):
        for k in names:
            b.SY.inc(s.dma_start(out=sb["k" + k], in_=ap["c_" + k][:, :]), 16)
        b.SY.inc(s.dma_start(out=sb["metaPs"], in_=ap["metaP"][:, :]), 16)
        b.SY.inc(s.dma_start(out=sb["oh8s"], in_=ap["oh8"][:, :]), 16)
        b.SY.inc(s.dma_start(out=sb["b0cS"], in_=ap["b0c"][:, :]), 16)

    @block.vector
    def _(v):
        b.SY.wait(v)
        v.memset(sb["stateAll"], 0.0)
        v.memset(sb["Sinh"], 0.0)
        for val, idx in B.CVALS.items():
            v.memset(sb["cpool"][:, idx:idx + 1], float(val))
        b.DV.inc(v.tensor_scalar(sb["b0cs"], sb["b0cS"][:, 0:16], 0.125, None, ALU.mult))


def emit_p0(b):
    nc, block, ap, sb, pb = b.nc, b.block, b.ap, b.sb, b.pb
    es = contextlib.ExitStack()
    xb = [es.enter_context(nc.sbuf_tensor(f"p0x{i}", [128, D], F32)).ap() for i in range(2)]
    xlnb = [es.enter_context(nc.sbuf_tensor(f"p0l{i}", [128, D], F32)).ap() for i in range(2)]
    xtb = [es.enter_context(nc.sbuf_tensor(f"p0t{i}", [128, D], F32)).ap() for i in range(2)]
    S0 = b.snap()
    st = sb["stats"]

    @block.sync
    def _(s):
        b.wait_snap(s, S0)
        for n in range(NT):
            if n >= 2:
                s.wait_ge(b.SA.h, S0["SA"] + 12 * (n - 2) + 4)
            b.SY.inc(s.dma_start(out=xb[n % 2],
                                 in_=ap["x_in"][128 * n:128 * (n + 1), :]), 16)

    @block.vector
    def _(v):
        b.wait_snap(v, S0)
        for n in range(NT):
            c0 = 6 * n
            v.wait_ge(b.SY.h, S0["SY"] + 16 * (n + 1))
            b.DV.inc(v.reduce_sum(st[:, c0:c0 + 1], xb[n % 2], axis=AX.X))
            v.drain()
            v.wait_ge(b.SA.h, S0["SA"] + 12 * n + 2)
            b.DV.inc(v.tensor_tensor(st[:, c0 + 3:c0 + 4], st[:, c0 + 1:c0 + 2],
                                     st[:, c0 + 2:c0 + 3], ALU.subtract))
            v.drain()
            v.wait_ge(b.SA.h, S0["SA"] + 12 * n + 3)
            b.DV.inc(v.scalar_tensor_tensor(st[:, c0 + 5:c0 + 6], st[:, c0:c0 + 1],
                                            b.cv(-1.0 / 1024.0), st[:, c0 + 4:c0 + 5],
                                            ALU.mult, ALU.mult))
            v.drain()

    @block.scalar
    def _(sc):
        b.wait_snap(sc, S0)
        for n in range(NT):
            c0 = 6 * n
            sc.wait_ge(b.SY.h, S0["SY"] + 16 * (n + 1))
            b.SA.inc(sc.activation(out=sb["scr"], in_=xb[n % 2], func=AF.Square,
                                   accum_out=st[:, c0 + 1:c0 + 2]))
            sc.wait_ge(b.DV.h, S0["DV"] + 3 * n + 1)
            b.SA.inc(sc.activation(out=st[:, c0 + 2:c0 + 3], in_=st[:, c0:c0 + 1],
                                   func=AF.Square, scale=b.cv(1.0 / 32.0)))
            sc.wait_ge(b.DV.h, S0["DV"] + 3 * n + 2)
            sc.activation(out=st[:, c0 + 4:c0 + 5], in_=st[:, c0 + 3:c0 + 4], func=AF.Ln,
                          bias=b.cv(1024.0 * EPS))
            b.SA.inc(sc.activation(out=st[:, c0 + 4:c0 + 5], in_=st[:, c0 + 4:c0 + 5],
                                   func=AF.Exp, scale=b.cv(-0.5), bias=b.cv(LN32)))
            sc.wait_ge(b.DV.h, S0["DV"] + 3 * n + 3)
            if n >= 2:
                sc.wait_ge(b.PE.h, S0["PE"] + 8 * (n - 1))
            b.SA.inc(sc.activation(out=xlnb[n % 2], in_=xb[n % 2], func=AF.Identity,
                                   scale=st[:, c0 + 4:c0 + 5],
                                   bias=st[:, c0 + 5:c0 + 6]))
            if n >= 2:
                sc.wait_ge(b.GP.h, S0["GP"] + 128 * (n - 1))
            for c in range(8):
                sc.wait_ge(b.PE.h, S0["PE"] + 8 * n + c + 1)
                b.SA.inc(sc.copy(out=xtb[n % 2][:, 128 * c:128 * (c + 1)],
                                 in_=pb[c % 2][:, 0:128]))

    @block.tensor
    def _(t):
        b.wait_snap(t, S0)
        for n in range(NT):
            t.wait_ge(b.SA.h, S0["SA"] + 12 * n + 4)
            for c in range(8):
                gi = 8 * n + c
                if gi >= 2:
                    pn, pc = (gi - 2) // 8, (gi - 2) % 8
                    t.wait_ge(b.SA.h, S0["SA"] + 12 * pn + 4 + pc + 1)
                b.PE.inc(t.transpose(pb[c % 2][:, 0:128],
                                     xlnb[n % 2][:, 128 * c:128 * (c + 1)],
                                     sb["kIDENT"]))

    @block.gpsimd
    def _(g):
        b.wait_snap(g, S0)
        for n in range(NT):
            g.wait_ge(b.SA.h, S0["SA"] + 12 * (n + 1))
            for c in range(8):
                b.GP.inc(g.dma_start(out=ap["xln0T_h"][n, c, :, :],
                                     in_=xtb[n % 2][:, 128 * c:128 * (c + 1)]), 16)
    es.close()


def emit_p1(b):
    nc, block, ap, sb, pb = b.nc, b.block, b.ap, b.sb, b.pb
    es = contextlib.ExitStack()
    Wa = es.enter_context(nc.sbuf_tensor("p1Wa", [128, 8 * D], F32)).ap()
    Wb = es.enter_context(nc.sbuf_tensor("p1Wb", [128, 8 * D], F32)).ap()
    xTb = [es.enter_context(nc.sbuf_tensor(f"p1xT{i}", [128, D], F32)).ap() for i in range(2)]
    acl = es.enter_context(nc.sbuf_tensor("p1acl", [128, D], F32)).ap()
    ebf = es.enter_context(nc.sbuf_tensor("p1ebf", [128, D], F32)).ap()
    spb = es.enter_context(nc.sbuf_tensor("p1spb", [128, D], F32)).ap()
    ab2 = es.enter_context(nc.sbuf_tensor("p1ab2", [128, D], F32)).ap()
    eb2 = es.enter_context(nc.sbuf_tensor("p1eb2", [128, D], F32)).ap()
    bbf = es.enter_context(nc.sbuf_tensor("p1bbf", [128, D], F32)).ap()
    b0one = es.enter_context(nc.sbuf_tensor("p1b0r", [1, 2 * D], F32)).ap()
    S0 = b.snap()

    @block.sync
    def _(s):
        b.wait_snap(s, S0)
        b.SY.inc(s.dma_start(out=Wa, in_=ap["WaT0"].rearrange("(c p) o -> p c o", p=128)), 16)
        b.SY.inc(s.dma_start(out=Wb, in_=ap["WbT0"].rearrange("(c p) o -> p c o", p=128)), 16)
        b.SY.inc(s.dma_start(out=b0one, in_=ap["b0r"][0:1, D:3 * D]), 16)
        for n in range(NT):
            if n >= 2:
                s.wait_ge(b.PE.h, S0["PE"] + 4 * (n - 1))
            for c in range(8):
                b.SY.inc(s.dma_start(out=xTb[n % 2][:, 128 * c:128 * (c + 1)],
                                     in_=ap["xln0T_h"][n, c, :, :]), 16)

    @block.tensor
    def _(t):
        b.wait_snap(t, S0)
        for n in range(NT):
            t.wait_ge(b.SY.h, S0["SY"] + 48 + 128 * (n + 1))
            for wi_, (Wm, brow) in enumerate(((Wa, 1), (Wb, 0))):
                if n >= 1:
                    t.wait_ge(b.DV.h, S0["DV"] + 4 * (n - 1) + (1 if wi_ == 0 else 3))
                for og in range(2):
                    pairs = [(xTb[n % 2][:, 128 * c:128 * (c + 1)],
                              Wm[:, D * c + 512 * og: D * c + 512 * og + 512])
                             for c in range(8)]
                    pairs.append((sb["kONESROW"][0:1, 0:128],
                                  b0one[0:1, D * brow + 512 * og: D * brow + 512 * og + 512]))
                    b.PE.inc(mm_accum(t, pb[2 * wi_ + og], pairs))

    @block.vector
    def _(v):
        b.wait_snap(v, S0)
        for n in range(NT):
            v.wait_ge(b.PE.h, S0["PE"] + 4 * n + 2)
            v.tensor_scalar(acl[:, 0:512], pb[0], 10.0, -10.0, ALU.min, ALU.max)
            b.DV.inc(v.tensor_scalar(acl[:, 512:1024], pb[1], 10.0, -10.0,
                                     ALU.min, ALU.max))
            v.wait_ge(b.SA.h, S0["SA"] + 2 * n + 1)
            b.DV.inc(v.tensor_reduce(sb["lmsum"][:, 16 * n:16 * (n + 1)],
                                     spb.rearrange("p (a d) -> p a d", d=64),
                                     axis=AX.X, op=ALU.add))
            v.wait_ge(b.SA.h, S0["SA"] + 2 * n + 2)
            if n >= 1:
                v.wait_ge(b.GP.h, S0["GP"] + 16 * n)
            v.scalar_tensor_tensor(bbf[:, 0:512], pb[2], b.cv(0.0), eb2[:, 0:512],
                                   ALU.max, ALU.add)
            b.DV.inc(v.scalar_tensor_tensor(bbf[:, 512:1024], pb[3], b.cv(0.0),
                                            eb2[:, 512:1024], ALU.max, ALU.add))
            v.drain()
            v.tensor_reduce(sb["scr2"][:, 0:16], bbf.rearrange("p (a d) -> p a d", d=64),
                            axis=AX.X, op=ALU.add)
            v.drain()
            b.DV.inc(v.tensor_scalar(sb["bmean"][:, 16 * n:16 * (n + 1)],
                                     sb["scr2"][:, 0:16], 1.0 / 64.0, None, ALU.mult))
            v.drain()

    @block.scalar
    def _(sc):
        b.wait_snap(sc, S0)
        for n in range(NT):
            sc.wait_ge(b.DV.h, S0["DV"] + 4 * n + 1)
            sc.activation(out=ebf, in_=acl, func=AF.Exp, scale=b.cv(-1.0))
            b.SA.inc(sc.activation(out=spb, in_=ebf, func=AF.Ln, bias=b.cv(1.0)))
            sc.wait_ge(b.PE.h, S0["PE"] + 4 * n + 4)
            sc.activation(out=ab2[:, 0:512], in_=pb[2], func=AF.Abs)
            sc.activation(out=ab2[:, 512:1024], in_=pb[3], func=AF.Abs)
            sc.activation(out=eb2, in_=ab2, func=AF.Exp, scale=b.cv(-1.0))
            b.SA.inc(sc.activation(out=eb2, in_=eb2, func=AF.Ln, bias=b.cv(1.0)))

    @block.gpsimd
    def _(g):
        b.wait_snap(g, S0)
        for n in range(NT):
            g.wait_ge(b.DV.h, S0["DV"] + 4 * n + 4)
            b.GP.inc(g.dma_start(out=ap["beta_h"][128 * n:128 * (n + 1), :], in_=bbf), 16)

    steps = []
    for n in range(NT):
        def mmf(t, n=n):
            return t.matmul(pb[0][:, 0:16], sb["kLINC64"],
                            sb["lmsum"][:, 16 * n:16 * (n + 1)], start=True, stop=True)
        steps.append(("PE", 1, mmf))
        steps.append(("SA", 1, lambda sc, n=n: sc.activation(
            out=sb["curve"][:, 16 * n:16 * (n + 1)], in_=pb[0][:, 0:16],
            func=AF.Exp, scale=b.cv(-1.0))))
        steps.append(("SA", 1, lambda sc, n=n: sc.activation(
            out=sb["icurve"][:, 16 * n:16 * (n + 1)], in_=pb[0][:, 0:16],
            func=AF.Exp)))
        steps.append(("SA", 1, lambda sc: sc.copy(out=sb["scr2"][:, 16:32],
                                                  in_=pb[0][:, 0:16])))
        steps.append(("PE", 1, lambda t: t.transpose(pb[1][0:16, 0:128],
                                                     sb["scr2"][:, 16:32], sb["kIDENT"])))
        steps.append(("SA", 1, lambda sc, n=n: sc.activation(
            out=sb["asumC"][:, n:n + 1], in_=pb[1][0:16, 127:128],
            func=AF.Exp, scale=b.cv(-1.0, 16))))
        steps.append(("SA", 1, lambda sc, n=n: sc.copy(
            out=sb["totb"][:, n:n + 1], in_=pb[1][0:16, 127:128])))
    steps.append(("PE", 1, lambda t: t.transpose(pb[2][0:16, 0:16], sb["totb"],
                                                 sb["kIDENT"][0:16, 0:16])))
    steps.append(("SA", 1, lambda sc: sc.copy(out=sb["scr2"][0:16, 32:48],
                                              in_=pb[2][0:16, 0:16])))
    steps.append(("PE", 1, lambda t: t.matmul(pb[3][0:16, 0:16],
                                              sb["kSTRICTLOW"][0:16, 0:16],
                                              sb["scr2"][0:16, 32:48],
                                              start=True, stop=True)))
    steps.append(("SA", 1, lambda sc: sc.copy(out=sb["scr2"][0:16, 48:64],
                                              in_=pb[3][0:16, 0:16])))
    steps.append(("PE", 1, lambda t: t.transpose(pb[2][0:16, 0:16],
                                                 sb["scr2"][0:16, 48:64],
                                                 sb["kIDENT"][0:16, 0:16])))
    steps.append(("SA", 1, lambda sc: sc.activation(out=sb["cumdec"], in_=pb[2][0:16, 0:16],
                                                    func=AF.Exp, scale=b.cv(-1.0, 16))))
    steps.append(("DV", 1, lambda v: v.tensor_tensor(
        sb["scr"][0:16, 0:256].rearrange("h (n hp) -> h n hp", hp=16),
        sb["kREPL16"].rearrange("h (n hp) -> h n hp", hp=16),
        sb["cumdec"].to_broadcast([16, 16, 16]), ALU.mult)))
    steps.append(("PE", 1, lambda t: t.matmul(pb[4][:, 0:256], sb["kONES16x128"],
                                              sb["scr"][0:16, 0:256], start=True, stop=True)))
    steps.append(("DV", 1, lambda v: v.tensor_tensor(sb["curvec"], sb["curve"],
                                                     pb[4][:, 0:256], ALU.mult)))
    b.serial(steps)
    es.close()


def emit_p2(b):
    nc, block, ap, sb, pb = b.nc, b.block, b.ap, b.sb, b.pb
    es = contextlib.ExitStack()
    Wq = es.enter_context(nc.sbuf_tensor("p2Wq", [128, 8 * D], F32)).ap()
    Wk = es.enter_context(nc.sbuf_tensor("p2Wk", [128, 8 * D], F32)).ap()
    Wv = es.enter_context(nc.sbuf_tensor("p2Wv", [128, 8 * D], F32)).ap()
    xTb = [es.enter_context(nc.sbuf_tensor(f"p2xT{i}", [128, D], F32)).ap() for i in range(2)]
    bb = [es.enter_context(nc.sbuf_tensor(f"p2bb{i}", [128, D], F32)).ap() for i in range(2)]
    xb2 = [es.enter_context(nc.sbuf_tensor(f"p2xb{i}", [128, D], F32)).ap() for i in range(2)]
    qTb = es.enter_context(nc.sbuf_tensor("p2qT", [128, D], F32)).ap()
    kTb = es.enter_context(nc.sbuf_tensor("p2kT", [128, D], F32)).ap()
    ktb = es.enter_context(nc.sbuf_tensor("p2kt", [128, D], F32)).ap()
    vb = es.enter_context(nc.sbuf_tensor("p2vb", [128, D], F32)).ap()
    vbb = es.enter_context(nc.sbuf_tensor("p2vbb", [128, D], F32)).ap()
    kbb = es.enter_context(nc.sbuf_tensor("p2kbb", [128, D], F32)).ap()
    PTb = es.enter_context(nc.sbuf_tensor("p2PT", [128, 2048], F32)).ap()
    ycb = es.enter_context(nc.sbuf_tensor("p2ycb", [128, 512], F32)).ap()
    dbuf = es.enter_context(nc.sbuf_tensor("p2db", [128, 512], F32)).ap()
    x1b = es.enter_context(nc.sbuf_tensor("p2x1", [128, D], F32)).ap()
    bvr = es.enter_context(nc.sbuf_tensor("p2bvr", [1, D], F32)).ap()
    S0 = b.snap()
    PEC, SAC, DVC = 29, 27, 4

    @block.sync
    def _(s):
        b.wait_snap(s, S0)
        b.SY.inc(s.dma_start(out=Wq, in_=ap["WqT0"].rearrange("(c p) o -> p c o", p=128)), 16)
        b.SY.inc(s.dma_start(out=Wk, in_=ap["WkT0"].rearrange("(c p) o -> p c o", p=128)), 16)
        b.SY.inc(s.dma_start(out=Wv, in_=ap["WvT0"].rearrange("(c p) o -> p c o", p=128)), 16)
        b.SY.inc(s.dma_start(out=bvr, in_=ap["b0r"][0:1, 0:D]), 16)
        for n in range(NT):
            if n >= 2:
                s.wait_ge(b.PE.h, S0["PE"] + PEC * (n - 1))
                s.wait_ge(b.DV.h, S0["DV"] + DVC * (n - 1))
            for c in range(8):
                b.SY.inc(s.dma_start(out=xTb[n % 2][:, 128 * c:128 * (c + 1)],
                                     in_=ap["xln0T_h"][n, c, :, :]), 16)
            b.SY.inc(s.dma_start(out=bb[n % 2], in_=ap["beta_h"][128 * n:128 * (n + 1), :]), 16)
            b.SY.inc(s.dma_start(out=xb2[n % 2], in_=ap["x_in"][128 * n:128 * (n + 1), :]), 16)

    @block.tensor
    def _(t):
        b.wait_snap(t, S0)
        t.wait_ge(b.SY.h, S0["SY"] + 64)
        for n in range(NT):
            t.wait_ge(b.SY.h, S0["SY"] + 64 + 160 * (n + 1))
            if n >= 1:
                t.wait_ge(b.DV.h, S0["DV"] + DVC * n)
                t.wait_ge(b.SA.h, S0["SA"] + SAC * n)
            for wmi, Wm in enumerate((Wq, Wk)):
                for co in range(8):
                    g = 8 * wmi + co
                    if g >= 2:
                        t.wait_ge(b.SA.h, S0["SA"] + SAC * n + g - 1)
                    pairs = [(Wm[:, D * c + 128 * co: D * c + 128 * co + 128],
                              xTb[n % 2][:, 128 * c:128 * (c + 1)])
                             for c in range(8)]
                    b.PE.inc(mm_accum(t, pb[g % 2][:, 0:128], pairs))
            for og in range(2):
                g = 16 + og
                t.wait_ge(b.SA.h, S0["SA"] + SAC * n + g - 1)
                pairs = [(xTb[n % 2][:, 128 * c:128 * (c + 1)],
                          Wv[:, D * c + 512 * og: D * c + 512 * og + 512])
                         for c in range(8)]
                pairs.append((sb["kONESROW"][0:1, 0:128],
                              bvr[0:1, 512 * og: 512 * og + 512]))
                b.PE.inc(mm_accum(t, pb[g % 2], pairs))
            for i in range(8):
                if i >= 2:
                    t.wait_ge(b.SA.h, S0["SA"] + SAC * n + 18 + i - 1)
                b.PE.inc(t.transpose(pb[2 + i % 2][:, 0:128],
                                     kTb[:, 128 * i:128 * (i + 1)], sb["kIDENT"]))
            t.wait_ge(b.SA.h, S0["SA"] + SAC * n + 26)
            ins = None
            for j in range(16):
                h = PERM[j]
                bank, reg = divmod(j, 4)
                half = slice(0, 64) if h % 2 == 0 else slice(64, 128)
                c = h // 2
                ins = t.matmul(pb[bank][:, 128 * reg:128 * reg + 128],
                               kTb[half, 128 * c:128 * (c + 1)],
                               qTb[half, 128 * c:128 * (c + 1)],
                               start=True, stop=True)
            b.PE.inc(ins)
            t.wait_ge(b.DV.h, S0["DV"] + DVC * n + 3)
            ins = None
            for j in range(16):
                h = PERM[j]
                bank = pb[0] if h % 2 == 0 else pb[1]
                col = 64 * (h // 2)
                ins = t.matmul(bank[:, col:col + 64], PTb[:, 128 * j:128 * (j + 1)],
                               vbb[:, 64 * h:64 * h + 64],
                               start=True, stop=True)
            b.PE.inc(ins)
            t.wait_ge(b.DV.h, S0["DV"] + DVC * n + 2)
            ins = None
            for h in range(16):
                rows = slice(0, 64) if h % 2 == 0 else slice(64, 128)
                col = 64 * (h // 2)
                ins = t.matmul(pb[4][rows, col:col + 64], kbb[:, 64 * h:64 * h + 64],
                               vb[:, 64 * h:64 * h + 64], start=True, stop=True)
            b.PE.inc(ins)

    @block.scalar
    def _(sc):
        b.wait_snap(sc, S0)
        for n in range(NT):
            for co in range(8):
                sc.wait_ge(b.PE.h, S0["PE"] + PEC * n + co + 1)
                if n >= 1 and co == 0:
                    sc.wait_ge(b.PE.h, S0["PE"] + PEC * n)
                    sc.wait_ge(b.GP.h, S0["GP"] + 48 * n)
                b.SA.inc(sc.activation(out=qTb[:, 128 * co:128 * (co + 1)],
                                       in_=pb[co % 2][:, 0:128], func=AF.Identity,
                                       scale=b.cv(0.125), bias=sb["b0cs"][:, co:co + 1]))
            for co in range(8):
                sc.wait_ge(b.PE.h, S0["PE"] + PEC * n + 8 + co + 1)
                b.SA.inc(sc.activation(out=kTb[:, 128 * co:128 * (co + 1)],
                                       in_=pb[co % 2][:, 0:128], func=AF.Identity,
                                       scale=b.cv(0.125), bias=sb["b0cs"][:, 8 + co:8 + co + 1]))
            for og in range(2):
                sc.wait_ge(b.PE.h, S0["PE"] + PEC * n + 16 + og + 1)
                b.SA.inc(sc.copy(out=vb[:, 512 * og:512 * og + 512], in_=pb[(16 + og) % 2]))
            for i in range(8):
                sc.wait_ge(b.PE.h, S0["PE"] + PEC * n + 18 + i + 1)
                if n >= 1 and i == 0:
                    sc.wait_ge(b.DV.h, S0["DV"] + DVC * (n - 1) + 2)
                b.SA.inc(sc.copy(out=ktb[:, 128 * i:128 * (i + 1)],
                                 in_=pb[2 + i % 2][:, 0:128]))
            sc.wait_ge(b.PE.h, S0["PE"] + PEC * n + 29)
            if n >= 1:
                sc.wait_ge(b.GP.h, S0["GP"] + 48 * n)
            b.SA.inc(sc.copy(out=dbuf, in_=pb[4]))

    @block.vector
    def _(v):
        b.wait_snap(v, S0)
        for n in range(NT):
            v.wait_ge(b.SA.h, S0["SA"] + SAC * n + 18)
            if n >= 1:
                v.wait_ge(b.PE.h, S0["PE"] + PEC * (n - 1) + 29)
            b.DV.inc(v.tensor_tensor(vbb, vb, bb[n % 2], ALU.mult))
            v.wait_ge(b.SA.h, S0["SA"] + SAC * n + 26)
            b.DV.inc(v.tensor_tensor(
                kbb.rearrange("p (a d) -> p a d", d=64),
                ktb.rearrange("p (a d) -> p a d", d=64),
                sb["bmean"][:, 16 * n:16 * (n + 1)].to_broadcast([128, 16, 64]),
                ALU.mult))
            v.wait_ge(b.PE.h, S0["PE"] + PEC * n + 27)
            ins = None
            for j in range(16):
                h = PERM[j]
                bank, reg = divmod(j, 4)
                ins = v.scalar_tensor_tensor(
                    PTb[:, 128 * j:128 * (j + 1)],
                    pb[bank][:, 128 * reg:128 * reg + 128],
                    sb["icurve"][:, 16 * n + h:16 * n + h + 1],
                    sb["kTRIL"], ALU.mult, ALU.mult)
            b.DV.inc(ins)
            v.wait_ge(b.PE.h, S0["PE"] + PEC * n + 28)
            if n >= 1:
                v.wait_ge(b.GP.h, S0["GP"] + 48 * n)
            ins = None
            for par in range(2):
                yv = pb[par].rearrange("p (a d) -> p a d", d=64)
                cb = sb["curve"][:, 16 * n:16 * (n + 1)] \
                    .rearrange("p (a two) -> p a two", two=2)[:, :, par] \
                    .to_broadcast([128, 8, 64])
                xv = xb2[n % 2].rearrange("p (a two d) -> p two a d", two=2, d=64)[:, par]
                ov = x1b.rearrange("p (a two d) -> p two a d", two=2, d=64)[:, par]
                v.tensor_tensor(ycb.rearrange("p (a d) -> p a d", d=64), yv, cb,
                                ALU.mult)
                v.drain()
                ins = v.tensor_tensor(ov, ycb.rearrange("p (a d) -> p a d", d=64),
                                      xv, ALU.add)
                v.drain()
            b.DV.inc(ins)

    @block.gpsimd
    def _(g):
        b.wait_snap(g, S0)
        for n in range(NT):
            g.wait_ge(b.DV.h, S0["DV"] + DVC * (n + 1))
            g.wait_ge(b.SA.h, S0["SA"] + SAC * (n + 1))
            b.GP.inc(g.dma_start(out=ap["qspill_h"][n, :, :], in_=qTb), 16)
            b.GP.inc(g.dma_start(out=ap["delta_h"][n, :, :], in_=dbuf), 16)
            b.GP.inc(g.dma_start(out=ap["x1_h"][128 * n:128 * (n + 1), :], in_=x1b), 16)

    es.close()
    es = contextlib.ExitStack()
    qc = es.enter_context(nc.sbuf_tensor("p2qc", [128, D], F32)).ap()
    dc2 = es.enter_context(nc.sbuf_tensor("p2dc2", [128, 512], F32)).ap()
    mcb = es.enter_context(nc.sbuf_tensor("p2mc", [128, D], F32)).ap()
    st2 = es.enter_context(nc.sbuf_tensor("p2st2", [128, 512], F32)).ap()
    sA = es.enter_context(nc.sbuf_tensor("p2sA", [128, 512], F32)).ap()
    sBv = es.enter_context(nc.sbuf_tensor("p2sB", [128, 512], F32)).ap()

    def chain_pass(state_src, curve_name):
        steps = []
        for n in range(NT):
            steps.append(("SY", 16, lambda s, n=n: s.dma_start(
                out=qc, in_=ap["qspill_h"][n, :, :])))
            steps.append(("SY", 16, lambda s, n=n: s.dma_start(
                out=dc2, in_=ap["delta_h"][n, :, :])))

            def mems(t, n=n):
                ins = None
                for h in range(16):
                    half = slice(0, 64) if h % 2 == 0 else slice(64, 128)
                    bank = pb[0] if h % 2 == 0 else pb[1]
                    col = 64 * (h // 2)
                    ins = t.matmul(bank[:, col:col + 64],
                                   qc[half, 128 * (h // 2):128 * (h // 2) + 128],
                                   state_src[half, col:col + 64],
                                   start=True, stop=True)
                return ins
            steps.append(("PE", 1, mems))

            def mcf0(v, n=n):
                cv = sb[curve_name][:, 16 * n:16 * (n + 1)] \
                    .rearrange("p (a two) -> p a two", two=2)[:, :, 0] \
                    .to_broadcast([128, 8, 64])
                return v.tensor_tensor(mcb[:, 0:512].rearrange("p (a d) -> p a d", d=64),
                                       pb[0].rearrange("p (a d) -> p a d", d=64),
                                       cv, ALU.mult)
            steps.append(("DV", 1, mcf0))

            def mcf1(v, n=n):
                cv = sb[curve_name][:, 16 * n:16 * (n + 1)] \
                    .rearrange("p (a two) -> p a two", two=2)[:, :, 1] \
                    .to_broadcast([128, 8, 64])
                return v.tensor_tensor(mcb[:, 512:1024].rearrange("p (a d) -> p a d", d=64),
                                       pb[1].rearrange("p (a d) -> p a d", d=64),
                                       cv, ALU.mult)
            steps.append(("DV", 1, mcf1))

            if state_src is sb["stateAll"]:
                def asd(v, n=n):
                    return v.tensor_scalar(sb["scr2"][0:16, 0:8], sb["kDIAGC"],
                                           sb["asumC"][:, n:n + 1], None, ALU.mult)
                steps.append(("DV", 1, asd))
                steps.append(("PE", 1, lambda t: t.matmul(
                    pb[2][:, 0:8], sb["kSELPAR"], sb["scr2"][0:16, 0:8],
                    start=True, stop=True)))
                steps.append(("DV", 1, lambda v: v.tensor_tensor(
                    st2.rearrange("p (a d) -> p a d", d=64),
                    sb["stateAll"].rearrange("p (a d) -> p a d", d=64),
                    pb[2][:, 0:8].to_broadcast([128, 8, 64]), ALU.mult)))
                steps.append(("DV", 1, lambda v: v.tensor_tensor(
                    sb["stateAll"], st2, dc2, ALU.add)))

            def acc0(g, n=n):
                dst = ap["x1_h"][128 * n:128 * (n + 1), :] \
                    .rearrange("t (a two d) -> t two a d", two=2, d=64)[:, 0]
                return g.dma_start(out=dst,
                                   in_=mcb[:, 0:512].rearrange("p (a d) -> p a d", d=64),
                                   accum_op=ALU.add)
            steps.append(("GP", 16, acc0))

            def acc1(g, n=n):
                dst = ap["x1_h"][128 * n:128 * (n + 1), :] \
                    .rearrange("t (a two d) -> t two a d", two=2, d=64)[:, 1]
                return g.dma_start(out=dst,
                                   in_=mcb[:, 512:1024].rearrange("p (a d) -> p a d", d=64),
                                   accum_op=ALU.add)
            steps.append(("GP", 16, acc1))
        return steps

    b.serial(chain_pass(sb["stateAll"], "curve"))

    steps = [
        ("GP", 16, lambda g: g.dma_start(
            out=ap["cc_st_in"].rearrange("(p f) -> p f", p=128), in_=sb["stateAll"])),
        ("CC", 1, lambda g: g.collective_compute(
            "AllGather", ALU.bypass,
            replica_groups=[[0, 1], [2, 3], [4, 5], [6, 7]],
            ins=[ap["cc_st_in"][:]], outs=[ap["cc_st_out"][:]])),
        ("SY", 16, lambda s: s.dma_start(
            out=sA, in_=ap["cc_st_out"][0:65536].rearrange("(p f) -> p f", p=128))),
        ("SY", 16, lambda s: s.dma_start(
            out=sBv, in_=ap["cc_st_out"][65536:131072].rearrange("(p f) -> p f", p=128))),
        ("DV", 1, lambda v: v.tensor_scalar(st2, sA, sb["metaPs"][:, 2:3], None, ALU.mult)),
        ("DV", 1, lambda v: v.scalar_tensor_tensor(sb["Sinh"], sBv, sb["metaPs"][:, 3:4],
                                                   st2, ALU.mult, ALU.add)),
    ]
    b.serial(steps)
    b.serial(chain_pass(sb["Sinh"], "curvec"))
    es.close()


def emit_p3(b):
    nc, block, ap, sb, pb = b.nc, b.block, b.ap, b.sb, b.pb
    es = contextlib.ExitStack()
    xb = [es.enter_context(nc.sbuf_tensor(f"p3x{i}", [128, D], F32)).ap() for i in range(2)]
    xlnb = [es.enter_context(nc.sbuf_tensor(f"p3l{i}", [128, D], F32)).ap() for i in range(2)]
    xtb = [es.enter_context(nc.sbuf_tensor(f"p3t{i}", [128, D], F32)).ap() for i in range(2)]
    gws = es.enter_context(nc.sbuf_tensor("p3gw", [128, 64], F32)).ap()
    lg = es.enter_context(nc.sbuf_tensor("p3lg", [128, 128], F32)).ap()
    g16 = es.enter_context(nc.sbuf_tensor("p3g16", [128, 128], F32)).ap()
    E8 = es.enter_context(nc.sbuf_tensor("p3E8", [128, 128], F32)).ap()
    E2 = es.enter_context(nc.sbuf_tensor("p3E2", [128, 128], F32)).ap()
    MS = es.enter_context(nc.sbuf_tensor("p3MS", [128, 128], F32)).ap()
    M2B = es.enter_context(nc.sbuf_tensor("p3M2B", [128, 128], F32)).ap()
    S0 = b.snap()
    st = sb["stats"]
    DVC, SAC, PEC = 3, 13, 9

    @block.sync
    def _(s):
        b.wait_snap(s, S0)
        b.SY.inc(s.dma_start(out=gws.rearrange("p (c e) -> p c e", e=8),
                             in_=ap["gwT0"].rearrange("(c p) e -> p c e", p=128)), 16)
        for n in range(NT):
            if n >= 2:
                s.wait_ge(b.SA.h, S0["SA"] + SAC * (n - 2) + 4)
            b.SY.inc(s.dma_start(out=xb[n % 2],
                                 in_=ap["x1_h"][128 * n:128 * (n + 1), :]), 16)

    @block.vector
    def _(v):
        b.wait_snap(v, S0)
        for n in range(NT):
            c0 = 6 * n
            v.wait_ge(b.SY.h, S0["SY"] + 16 + 16 * (n + 1))
            b.DV.inc(v.reduce_sum(st[:, c0:c0 + 1], xb[n % 2], axis=AX.X))
            v.drain()
            v.wait_ge(b.SA.h, S0["SA"] + SAC * n + 2)
            b.DV.inc(v.tensor_tensor(st[:, c0 + 3:c0 + 4], st[:, c0 + 1:c0 + 2],
                                     st[:, c0 + 2:c0 + 3], ALU.subtract))
            v.drain()
            v.wait_ge(b.SA.h, S0["SA"] + SAC * n + 3)
            b.DV.inc(v.scalar_tensor_tensor(st[:, c0 + 5:c0 + 6], st[:, c0:c0 + 1],
                                            b.cv(-1.0 / 1024.0), st[:, c0 + 4:c0 + 5],
                                            ALU.mult, ALU.mult))
            v.drain()

    @block.scalar
    def _(sc):
        b.wait_snap(sc, S0)
        for n in range(NT):
            c0 = 6 * n
            sc.wait_ge(b.SY.h, S0["SY"] + 16 + 16 * (n + 1))
            b.SA.inc(sc.activation(out=sb["scr"], in_=xb[n % 2], func=AF.Square,
                                   accum_out=st[:, c0 + 1:c0 + 2]))
            sc.wait_ge(b.DV.h, S0["DV"] + DVC * n + 1)
            b.SA.inc(sc.activation(out=st[:, c0 + 2:c0 + 3], in_=st[:, c0:c0 + 1],
                                   func=AF.Square, scale=b.cv(1.0 / 32.0)))
            sc.wait_ge(b.DV.h, S0["DV"] + DVC * n + 2)
            sc.activation(out=st[:, c0 + 4:c0 + 5], in_=st[:, c0 + 3:c0 + 4], func=AF.Ln,
                          bias=b.cv(1024.0 * EPS))
            b.SA.inc(sc.activation(out=st[:, c0 + 4:c0 + 5], in_=st[:, c0 + 4:c0 + 5],
                                   func=AF.Exp, scale=b.cv(-0.5), bias=b.cv(LN32)))
            sc.wait_ge(b.DV.h, S0["DV"] + DVC * n + 3)
            if n >= 2:
                sc.wait_ge(b.PE.h, S0["PE"] + PEC * (n - 1))
            b.SA.inc(sc.activation(out=xlnb[n % 2], in_=xb[n % 2], func=AF.Identity,
                                   scale=st[:, c0 + 4:c0 + 5],
                                   bias=st[:, c0 + 5:c0 + 6]))
            if n >= 2:
                sc.wait_ge(b.GP.h, S0["GP"] + 128 * (n - 1))
            for c in range(8):
                sc.wait_ge(b.PE.h, S0["PE"] + PEC * n + c + 1)
                b.SA.inc(sc.copy(out=xtb[n % 2][:, 128 * c:128 * (c + 1)],
                                 in_=pb[c % 2][:, 0:128]))
            sc.wait_ge(b.PE.h, S0["PE"] + PEC * n + 9)
            b.SA.inc(sc.copy(out=lg[:, 8 * n:8 * n + 8], in_=pb[2][:, 0:8]))

    @block.tensor
    def _(t):
        b.wait_snap(t, S0)
        for n in range(NT):
            t.wait_ge(b.SA.h, S0["SA"] + SAC * n + 4)
            for c in range(8):
                Tg = 8 * n + c
                if Tg >= 2:
                    pn, pc = (Tg - 2) // 8, (Tg - 2) % 8
                    t.wait_ge(b.SA.h, S0["SA"] + SAC * pn + 4 + pc + 1)
                b.PE.inc(t.transpose(pb[c % 2][:, 0:128],
                                     xlnb[n % 2][:, 128 * c:128 * (c + 1)],
                                     sb["kIDENT"]))
            t.wait_ge(b.SA.h, S0["SA"] + SAC * n + 12)
            if n >= 1:
                t.wait_ge(b.SA.h, S0["SA"] + SAC * (n - 1) + 13)
            pairs = [(xtb[n % 2][:, 128 * c:128 * (c + 1)], gws[:, 8 * c:8 * c + 8])
                     for c in range(8)]
            b.PE.inc(mm_accum(t, pb[2][:, 0:8], pairs))

    @block.gpsimd
    def _(g):
        b.wait_snap(g, S0)
        for n in range(NT):
            g.wait_ge(b.SA.h, S0["SA"] + SAC * n + 12)
            for c in range(8):
                b.GP.inc(g.dma_start(out=ap["xln1T_h"][n, c, :, :],
                                     in_=xtb[n % 2][:, 128 * c:128 * (c + 1)]), 16)

    L3 = lg.rearrange("p (n e) -> p n e", e=8)
    steps = [
        ("DV", 1, lambda v: v.tensor_reduce(g16[:, 0:16], L3, axis=AX.X, op=ALU.max)),
        ("DV", 1, lambda v: v.tensor_tensor(
            MS.rearrange("p (n e) -> p n e", e=8), L3,
            g16[:, 0:16].to_broadcast([128, 16, 8]), ALU.subtract)),
        ("SA", 1, lambda sc: sc.activation(out=E8, in_=MS, func=AF.Exp)),
        ("DV", 1, lambda v: v.tensor_reduce(g16[:, 16:32],
                                            E8.rearrange("p (n e) -> p n e", e=8),
                                            axis=AX.X, op=ALU.add)),
        ("DV", 1, lambda v: v.tensor_reduce(g16[:, 32:48],
                                            E8.rearrange("p (n e) -> p n e", e=8),
                                            axis=AX.X, op=ALU.max)),
        ("DV", 1, lambda v: v.tensor_tensor(
            MS.rearrange("p (n e) -> p n e", e=8),
            E8.rearrange("p (n e) -> p n e", e=8),
            g16[:, 32:48].to_broadcast([128, 16, 8]), ALU.is_ge)),
        ("DV", 1, lambda v: v.scalar_tensor_tensor(E2, MS, b.cv(-10.0), E8,
                                                   ALU.mult, ALU.add)),
        ("DV", 1, lambda v: v.tensor_reduce(g16[:, 48:64],
                                            E2.rearrange("p (n e) -> p n e", e=8),
                                            axis=AX.X, op=ALU.max)),
        ("DV", 1, lambda v: v.tensor_tensor(
            M2B.rearrange("p (n e) -> p n e", e=8),
            E2.rearrange("p (n e) -> p n e", e=8),
            g16[:, 48:64].to_broadcast([128, 16, 8]), ALU.is_ge)),
        ("DV", 1, lambda v: v.tensor_tensor(g16[:, 64:80], g16[:, 32:48],
                                            g16[:, 48:64], ALU.add)),
        ("DV", 1, lambda v: v.scalar_tensor_tensor(g16[:, 80:96], g16[:, 16:32],
                                                   b.cv(1e-20), g16[:, 64:80],
                                                   ALU.mult, ALU.add)),
        ("DV", 1, lambda v: v.reciprocal(g16[:, 96:112], g16[:, 80:96])),
        ("DV", 1, lambda v: v.tensor_tensor(lg, MS, M2B, ALU.add)),
        ("DV", 1, lambda v: v.tensor_tensor(MS, lg, E8, ALU.mult)),
        ("DV", 1, lambda v: v.tensor_tensor(
            sb["comb"].rearrange("p (n e) -> p n e", e=8),
            MS.rearrange("p (n e) -> p n e", e=8),
            g16[:, 96:112].to_broadcast([128, 16, 8]), ALU.mult)),
    ]
    for n in range(NT):
        steps.append(("PE", 1, lambda t, n=n: t.transpose(
            pb[3][0:8, 0:128], sb["comb"][:, 8 * n:8 * n + 8], sb["kIDENT"])))
        steps.append(("SA", 1, lambda sc, n=n: sc.copy(
            out=sb["combT"][:, 128 * n:128 * (n + 1)], in_=pb[3][0:8, 0:128])))
    b.serial(steps)
    es.close()


def emit_p4(b):
    nc, block, ap, sb, pb = b.nc, b.block, b.ap, b.sb, b.pb
    es = contextlib.ExitStack()
    Wu = es.enter_context(nc.sbuf_tensor("p4Wu", [128, 8 * D], F32)).ap()
    Wd = es.enter_context(nc.sbuf_tensor("p4Wd", [128, 8 * D], F32)).ap()
    ubc = es.enter_context(nc.sbuf_tensor("p4ubc", [128, 8], F32)).ap()
    db8 = es.enter_context(nc.sbuf_tensor("p4db8", [8, D], F32)).ap()
    sdbr = es.enter_context(nc.sbuf_tensor("p4sdbr", [1, D], F32)).ap()
    xs = [es.enter_context(nc.sbuf_tensor(f"p4xs{i}", [128, 4096], F32)).ap() for i in range(2)]
    hb = es.enter_context(nc.sbuf_tensor("p4hb", [128, 4096], F32)).ap()
    yt = es.enter_context(nc.sbuf_tensor("p4yt", [128, 512], F32)).ap()
    S0 = b.snap()
    NE = 9
    NS = 4
    PEC, SAC = 16, 16

    def eidx(e, s):
        return e * NS + s

    @block.sync
    def _(s_):
        b.wait_snap(s_, S0)
        for e in range(NE):
            if e >= 1:
                s_.wait_ge(b.PE.h, S0["PE"] + PEC * NS * e)
                s_.wait_ge(b.SA.h, S0["SA"] + SAC * NS * e)
            if e < 8:
                b.SY.inc(s_.dma_start(out=Wu, in_=ap["up_w"][e].rearrange(
                    "(c p) h -> p c h", p=128)), 16)
                b.SY.inc(s_.dma_start(out=Wd, in_=ap["down_w"][e].rearrange(
                    "(c p) h -> p c h", p=128)), 16)
                b.SY.inc(s_.dma_start(out=ubc[:, 0:8],
                                      in_=ap["up_bc"][:, 8 * e:8 * e + 8]), 16)
            else:
                b.SY.inc(s_.dma_start(out=Wu, in_=ap["sh_up_w"].rearrange(
                    "(c p) h -> p c h", p=128)), 16)
                b.SY.inc(s_.dma_start(out=Wd, in_=ap["sh_down_w"].rearrange(
                    "(c p) h -> p c h", p=128)), 16)
                b.SY.inc(s_.dma_start(out=ubc[:, 0:8], in_=ap["sh_up_bc"][:, :]), 16)
                b.SY.inc(s_.dma_start(out=db8, in_=ap["down_br"].rearrange(
                    "one (e o) -> (one e) o", e=8)), 16)
                b.SY.inc(s_.dma_start(out=sdbr, in_=ap["sh_down_br"][:, :]), 16)
            for s in range(NS):
                i = eidx(e, s)
                if i >= 2:
                    s_.wait_ge(b.PE.h, S0["PE"] + PEC * (i - 1))
                for c in range(8):
                    b.SY.inc(s_.dma_start(
                        out=xs[s % 2][:, 512 * c:512 * (c + 1)],
                        in_=ap["xln1T_h"][4 * s:4 * s + 4, c].rearrange("n p t -> p n t")), 16)

    @block.tensor
    def _(t):
        b.wait_snap(t, S0)
        for e in range(NE):
            nw = 3 * min(e + 1, 8) + (5 if e == 8 else 0)
            for s in range(NS):
                i = eidx(e, s)
                t.wait_ge(b.SY.h, S0["SY"] + 16 * (nw + 8 * (i + 1)))
                for hc in range(8):
                    g = PEC * i + hc
                    if g >= 2:
                        pn, pc = (g - 2) // PEC, (g - 2) % PEC
                        t.wait_ge(b.SA.h, S0["SA"] + PEC * pn + pc + 1)
                    pairs = [(Wu[:, D * c + 128 * hc: D * c + 128 * hc + 128],
                              xs[s % 2][:, 512 * c:512 * (c + 1)])
                             for c in range(8)]
                    b.PE.inc(mm_accum(t, pb[hc % 2], pairs))
                for g2 in range(8):
                    tt, og = divmod(g2, 2)
                    g = PEC * i + 8 + g2
                    if g >= 2:
                        pn, pc = (g - 2) // PEC, (g - 2) % PEC
                        t.wait_ge(b.SA.h, S0["SA"] + PEC * pn + pc + 1)
                    t.wait_ge(b.SA.h, S0["SA"] + PEC * i + 8)
                    pairs = [(hb[:, 512 * hc + 128 * tt: 512 * hc + 128 * tt + 128],
                              Wd[:, D * hc + 512 * og: D * hc + 512 * og + 512])
                             for hc in range(8)]
                    if e == 8:
                        pairs.append((sb["kONESROW"][0:1, 0:128],
                                      sdbr[0:1, 512 * og:512 * og + 512]))
                        pairs.append((sb["combT"][0:8, 128 * (4 * s + tt):
                                                  128 * (4 * s + tt) + 128],
                                      db8[:, 512 * og:512 * og + 512]))
                    b.PE.inc(mm_accum(t, pb[2 + g2 % 2], pairs))

    @block.scalar
    def _(sc):
        b.wait_snap(sc, S0)
        for e in range(NE):
            for s in range(NS):
                i = eidx(e, s)
                for hc in range(8):
                    sc.wait_ge(b.PE.h, S0["PE"] + PEC * i + hc + 1)
                    b.SA.inc(sc.activation(out=hb[:, 512 * hc:512 * (hc + 1)],
                                           in_=pb[hc % 2], func=AF.Silu,
                                           bias=ubc[:, hc:hc + 1]))
                for g2 in range(8):
                    tt, og = divmod(g2, 2)
                    sc.wait_ge(b.PE.h, S0["PE"] + PEC * i + 8 + g2 + 1)
                    if i >= 1:
                        sc.wait_ge(b.GP.h, S0["GP"] + 8 * 16 * i)
                    n_ = 4 * s + tt
                    if e < 8:
                        b.SA.inc(sc.activation(out=yt, in_=pb[2 + g2 % 2],
                                               func=AF.Identity,
                                               scale=sb["comb"][:, 8 * n_ + e:8 * n_ + e + 1]))
                    else:
                        b.SA.inc(sc.copy(out=yt, in_=pb[2 + g2 % 2]))

    @block.gpsimd
    def _(g):
        b.wait_snap(g, S0)
        for e in range(NE):
            for s in range(NS):
                i = eidx(e, s)
                for g2 in range(8):
                    tt, og = divmod(g2, 2)
                    g.wait_ge(b.SA.h, S0["SA"] + PEC * i + 8 + g2 + 1)
                    n_ = 4 * s + tt
                    dst = ap["yacc_h"][128 * n_:128 * (n_ + 1), 512 * og:512 * (og + 1)]
                    b.GP.inc(g.dma_start(out=dst, in_=yt,
                                         accum_op=(ALU.add if e > 0 else ALU.bypass)), 16)

    es.close()
    es = contextlib.ExitStack()
    xa = es.enter_context(nc.sbuf_tensor("p4xa", [128, D], F32)).ap()
    ya = es.enter_context(nc.sbuf_tensor("p4ya", [128, D], F32)).ap()
    x2b = es.enter_context(nc.sbuf_tensor("p4x2", [128, D], F32)).ap()
    steps = []
    for n in range(NT):
        steps.append(("SY", 16, lambda s, n=n: s.dma_start(
            out=xa, in_=ap["x1_h"][128 * n:128 * (n + 1), :])))
        steps.append(("SY", 16, lambda s, n=n: s.dma_start(
            out=ya, in_=ap["yacc_h"][128 * n:128 * (n + 1), :])))
        steps.append(("DV", 1, lambda v: v.tensor_tensor(x2b, xa, ya, ALU.add)))
        steps.append(("GP", 16, lambda g, n=n: g.dma_start(
            out=ap["x2_h"][128 * n:128 * (n + 1), :], in_=x2b)))
        steps.append(("DV", 1, lambda v, n=n: v.reduce_sum(
            sb["scr2"][:, 0:1], x2b, axis=AX.X)))
        steps.append(("SA", 1, lambda sc, n=n: sc.activation(
            out=sb["scr"], in_=x2b, func=AF.Square, accum_out=sb["scr2"][:, 1:2])))
        steps.append(("SA", 1, lambda sc: sc.activation(
            out=sb["scr2"][:, 2:3], in_=sb["scr2"][:, 0:1], func=AF.Square,
            scale=b.cv(1.0 / 32.0))))
        steps.append(("DV", 1, lambda v: v.tensor_tensor(
            sb["scr2"][:, 3:4], sb["scr2"][:, 1:2], sb["scr2"][:, 2:3], ALU.subtract)))
        steps.append(("SA", 1, lambda sc: sc.activation(
            out=sb["scr2"][:, 4:5], in_=sb["scr2"][:, 3:4], func=AF.Ln,
            bias=b.cv(1024.0 * EPS))))
        steps.append(("SA", 1, lambda sc, n=n: sc.activation(
            out=sb["rstd2"][:, n:n + 1], in_=sb["scr2"][:, 4:5],
            func=AF.Exp, scale=b.cv(-0.5), bias=b.cv(LN32))))
        steps.append(("DV", 1, lambda v, n=n: v.scalar_tensor_tensor(
            sb["nmr2"][:, n:n + 1], sb["scr2"][:, 0:1], b.cv(-1.0 / 1024.0),
            sb["rstd2"][:, n:n + 1], ALU.mult, ALU.mult)))
    b.serial(steps)
    es.close()


def emit_p5(b):
    nc, block, ap, sb, pb = b.nc, b.block, b.ap, b.sb, b.pb
    esw = contextlib.ExitStack()
    gmat = esw.enter_context(nc.sbuf_tensor("p5gmat", [128, D], F32)).ap()
    bmat = esw.enter_context(nc.sbuf_tensor("p5bmat", [128, D], F32)).ap()

    es = contextlib.ExitStack()
    Wq = es.enter_context(nc.sbuf_tensor("p5Wq", [128, 8 * D], F32)).ap()
    Wk = es.enter_context(nc.sbuf_tensor("p5Wk", [128, 8 * D], F32)).ap()
    Wv = es.enter_context(nc.sbuf_tensor("p5Wv", [128, 8 * D], F32)).ap()
    xb = [es.enter_context(nc.sbuf_tensor(f"p5x{i}", [128, D], F32)).ap() for i in range(2)]
    xlnb = [es.enter_context(nc.sbuf_tensor(f"p5l{i}", [128, D], F32)).ap() for i in range(2)]
    xtb = [es.enter_context(nc.sbuf_tensor(f"p5t{i}", [128, D], F32)).ap() for i in range(2)]
    qob = es.enter_context(nc.sbuf_tensor("p5qo", [128, D], F32)).ap()
    kob = es.enter_context(nc.sbuf_tensor("p5ko", [128, D], F32)).ap()
    vob = es.enter_context(nc.sbuf_tensor("p5vo", [128, D], F32)).ap()
    b1cs = es.enter_context(nc.sbuf_tensor("p5b1c", [128, 32], F32)).ap()
    b1vr = es.enter_context(nc.sbuf_tensor("p5b1vr", [1, D], F32)).ap()

    steps = []
    steps.append(("SY", 16, lambda s: s.dma_start(out=b1cs, in_=ap["b1c"][:, :])))
    steps.append(("SY", 16, lambda s: s.dma_start(out=b1vr, in_=ap["b1r"][0:1, 2 * D:3 * D])))
    steps.append(("SY", 16, lambda s: s.dma_start(out=Wq, in_=ap["WqT1"].rearrange(
        "(c p) o -> p c o", p=128))))
    steps.append(("SY", 16, lambda s: s.dma_start(out=Wk, in_=ap["WkT1"].rearrange(
        "(c p) o -> p c o", p=128))))
    steps.append(("SY", 16, lambda s: s.dma_start(out=Wv, in_=ap["WvT1"].rearrange(
        "(c p) o -> p c o", p=128))))
    b.serial(steps)

    S0 = b.snap()
    SAC = 27
    PEC = 26
    GPC = 18 * 16

    @block.sync
    def _(s):
        b.wait_snap(s, S0)
        for n in range(NT):
            if n >= 2:
                s.wait_ge(b.SA.h, S0["SA"] + SAC * (n - 2) + 1)
            b.SY.inc(s.dma_start(out=xb[n % 2],
                                 in_=ap["x2_h"][128 * n:128 * (n + 1), :]), 16)

    @block.scalar
    def _(sc):
        b.wait_snap(sc, S0)
        for n in range(NT):
            sc.wait_ge(b.SY.h, S0["SY"] + 16 * (n + 1))
            if n >= 2:
                sc.wait_ge(b.PE.h, S0["PE"] + PEC * (n - 1))
            b.SA.inc(sc.activation(out=xlnb[n % 2], in_=xb[n % 2], func=AF.Identity,
                                   scale=sb["rstd2"][:, n:n + 1],
                                   bias=sb["nmr2"][:, n:n + 1]))
            for c in range(8):
                sc.wait_ge(b.PE.h, S0["PE"] + PEC * n + c + 1)
                b.SA.inc(sc.copy(out=xtb[n % 2][:, 128 * c:128 * (c + 1)],
                                 in_=pb[c % 2][:, 0:128]))
            for co in range(8):
                sc.wait_ge(b.PE.h, S0["PE"] + PEC * n + 8 + co + 1)
                if n >= 1 and co == 0:
                    sc.wait_ge(b.GP.h, S0["GP"] + GPC * n)
                b.SA.inc(sc.activation(out=qob[:, 128 * co:128 * (co + 1)],
                                       in_=pb[2 + co % 2][:, 0:128], func=AF.Identity,
                                       bias=b1cs[:, co:co + 1]))
            for co in range(8):
                sc.wait_ge(b.PE.h, S0["PE"] + PEC * n + 16 + co + 1)
                b.SA.inc(sc.activation(out=kob[:, 128 * co:128 * (co + 1)],
                                       in_=pb[4 + co % 2][:, 0:128], func=AF.Identity,
                                       bias=b1cs[:, 8 + co:8 + co + 1]))
            for og in range(2):
                sc.wait_ge(b.PE.h, S0["PE"] + PEC * n + 24 + og + 1)
                b.SA.inc(sc.copy(out=vob[:, 512 * og:512 * og + 512], in_=pb[6 + og]))

    @block.tensor
    def _(t):
        b.wait_snap(t, S0)
        for n in range(NT):
            t.wait_ge(b.SA.h, S0["SA"] + SAC * n + 1)
            for c in range(8):
                Tg = 8 * n + c
                if Tg >= 2:
                    pn, pc = (Tg - 2) // 8, (Tg - 2) % 8
                    t.wait_ge(b.SA.h, S0["SA"] + SAC * pn + 1 + pc + 1)
                b.PE.inc(t.transpose(pb[c % 2][:, 0:128],
                                     xlnb[n % 2][:, 128 * c:128 * (c + 1)],
                                     sb["kIDENT"]))
            t.wait_ge(b.SA.h, S0["SA"] + SAC * n + 9)
            for co in range(8):
                Tg = 8 * n + co
                if Tg >= 2:
                    pn, pc = (Tg - 2) // 8, (Tg - 2) % 8
                    t.wait_ge(b.SA.h, S0["SA"] + SAC * pn + 9 + pc + 1)
                pairs = [(Wq[:, D * c + 128 * co: D * c + 128 * co + 128],
                          xtb[n % 2][:, 128 * c:128 * (c + 1)]) for c in range(8)]
                b.PE.inc(mm_accum(t, pb[2 + co % 2][:, 0:128], pairs))
            for co in range(8):
                Tg = 8 * n + co
                if Tg >= 2:
                    pn, pc = (Tg - 2) // 8, (Tg - 2) % 8
                    t.wait_ge(b.SA.h, S0["SA"] + SAC * pn + 17 + pc + 1)
                pairs = [(Wk[:, D * c + 128 * co: D * c + 128 * co + 128],
                          xtb[n % 2][:, 128 * c:128 * (c + 1)]) for c in range(8)]
                b.PE.inc(mm_accum(t, pb[4 + co % 2][:, 0:128], pairs))
            for og in range(2):
                if n >= 1:
                    t.wait_ge(b.SA.h, S0["SA"] + SAC * (n - 1) + 25 + og + 1)
                pairs = [(xtb[n % 2][:, 128 * c:128 * (c + 1)],
                          Wv[:, D * c + 512 * og: D * c + 512 * og + 512])
                         for c in range(8)]
                pairs.append((sb["kONESROW"][0:1, 0:128],
                              b1vr[0:1, 512 * og: 512 * og + 512]))
                b.PE.inc(mm_accum(t, pb[6 + og], pairs))

    @block.gpsimd
    def _(g):
        b.wait_snap(g, S0)
        kin = ap["cc_k_in"].rearrange("(c p t) -> c p t", c=8, p=128)
        vin = ap["ccvin"].rearrange("(t d) -> t d", d=D)
        for n in range(NT):
            g.wait_ge(b.SA.h, S0["SA"] + SAC * (n + 1))
            for co in range(8):
                b.GP.inc(g.dma_start(out=ap["q1_h"][co, :, 128 * n:128 * (n + 1)],
                                     in_=qob[:, 128 * co:128 * (co + 1)]), 16)
            for co in range(8):
                b.GP.inc(g.dma_start(out=kin[co, :, 128 * n:128 * (n + 1)],
                                     in_=kob[:, 128 * co:128 * (co + 1)]), 16)
            for og in range(2):
                b.GP.inc(g.dma_start(
                    out=vin[128 * n:128 * (n + 1), 512 * og:512 * (og + 1)],
                    in_=vob[:, 512 * og:512 * og + 512]), 16)
    es.close()

    es2 = contextlib.ExitStack()
    bandb = es2.enter_context(nc.sbuf_tensor("p5band", [1, 2 * T], F32)).ap()
    bscr = es2.enter_context(nc.sbuf_tensor("p5bscr", [1, 2 * T], F32)).ap()
    grows = es2.enter_context(nc.sbuf_tensor("p5grows", [1, 2 * D], F32)).ap()
    steps = []
    steps.append(("CC", 1, lambda g: g.collective_compute(
        "AllGather", ALU.bypass, replica_groups=[[0, 1], [2, 3], [4, 5], [6, 7]],
        ins=[ap["cc_k_in"][:]], outs=[ap["cc_k_out"][:]])))
    steps.append(("CC", 1, lambda g: g.collective_compute(
        "AllGather", ALU.bypass, replica_groups=[[0, 1], [2, 3], [4, 5], [6, 7]],
        ins=[ap["ccvin"][:]], outs=[ap["ccvout"][:]])))
    steps.append(("SY", 16, lambda s: s.dma_start(out=bandb[0:1, 0:T], in_=ap["band"][:, :])))
    steps.append(("SY", 16, lambda s: s.dma_start(out=grows[0:1, 0:D], in_=ap["g1r"][:, :])))
    steps.append(("SY", 16, lambda s: s.dma_start(out=grows[0:1, D:2 * D], in_=ap["be1r"][:, :])))
    for og in range(2):
        steps.append(("PE", 1, lambda t, og=og: t.matmul(
            pb[0], sb["kONESROW"][0:1, 0:128], grows[0:1, 512 * og:512 * og + 512],
            start=True, stop=True)))
        steps.append(("SA", 1, lambda sc, og=og: sc.copy(
            out=gmat[:, 512 * og:512 * og + 512], in_=pb[0])))
        steps.append(("PE", 1, lambda t, og=og: t.matmul(
            pb[0], sb["kONESROW"][0:1, 0:128],
            grows[0:1, D + 512 * og:D + 512 * og + 512], start=True, stop=True)))
        steps.append(("SA", 1, lambda sc, og=og: sc.copy(
            out=bmat[:, 512 * og:512 * og + 512], in_=pb[0])))

    def band_mask(v):
        B0 = bandb[0:1, 0:T]
        s1 = bscr[0:1, T:T + 1]
        tmp = bscr[0:1, 0:T]
        msk = bandb[0:1, T:2 * T]
        v.tensor_copy(tmp, B0)
        v.drain()
        for it in range(3):
            v.reduce_max(s1, tmp, axis=AX.X)
            v.drain()
            v.tensor_scalar(msk, tmp, s1, None, ALU.is_ge)
            v.drain()
            v.scalar_tensor_tensor(tmp, msk, b.cv(-1e9, 1), tmp, ALU.mult, ALU.add)
            v.drain()
        ins = v.tensor_scalar(msk, tmp, -1e8, None, ALU.is_gt)
        return ins
    steps.append(("DV", 1, band_mask))
    steps.append(("GP", 16, lambda g: g.dma_start(
        out=ap["cc_rs_out"][0:T], in_=bandb[0:1, T:2 * T])))
    steps.append(("SY", 16, lambda s: s.dma_start(
        out=sb["cmcol"], in_=ap["cc_rs_out"][0:T].rearrange("(t p) -> p t", p=128),
        allow_slow_non_contiguous=True)))
    b.serial(steps)
    es2.close()

    es = contextlib.ExitStack()
    kThp = es.enter_context(nc.sbuf_tensor("p5kT", [128, 2 * T], F32)).ap()
    qThp = es.enter_context(nc.sbuf_tensor("p5qT", [128, T], F32)).ap()
    vvt = es.enter_context(nc.sbuf_tensor("p5vvt", [128, 32 * 130], F32)).ap()
    PT = es.enter_context(nc.sbuf_tensor("p5PT", [128, 2 * 3072], F32)).ap()
    rd64 = es.enter_context(nc.sbuf_tensor("p5rd", [128, 512], F32)).ap()
    rdsb = es.enter_context(nc.sbuf_tensor("p5rdsb", [128, 512], F32)).ap()
    ctxe = es.enter_context(nc.sbuf_tensor("p5ctxe", [128, 512], F32)).ap()
    kof = ap["cc_k_out"].rearrange("(bl c p t) -> bl c p t", bl=2, c=8, p=128)
    vof = ap["ccvout"].rearrange("(bl t d) -> bl t d", bl=2, d=D)

    for hp in range(8):
        steps = []
        for half in range(2):
            steps.append(("SY", 16, (lambda s, hp=hp, half=half: s.dma_start(
                out=kThp[:, T * half:T * (half + 1)], in_=kof[half, hp])), []))
        steps.append(("SY", 16, (lambda s, hp=hp: s.dma_start(
            out=qThp, in_=ap["q1_h"][hp])), []))
        i_ones = len(steps)
        steps.append(("DV", 1, (lambda v: v.memset(
            vvt.rearrange("p (j two x) -> p j two x", two=2, x=65)[:, :, :, 64], 1.0)), []))
        for j in range(32):
            def vload(s, hp=hp, j=j):
                dst = vvt[:, 130 * j:130 * (j + 1)] \
                    .rearrange("p (two x) -> p two x", x=65)[:, :, 0:64]
                return s.dma_start(out=dst,
                                   in_=vof[j // 16, 128 * (j % 16):128 * (j % 16) + 128,
                                           128 * hp:128 * hp + 128]
                                   .rearrange("t (two d) -> t two d", two=2))
            steps.append(("SY", 16, vload, [i_ones]))
            if j < 16:
                def vmask(v, j=j):
                    return v.tensor_scalar(vvt[:, 130 * j:130 * (j + 1)],
                                           vvt[:, 130 * j:130 * (j + 1)],
                                           sb["cmcol"][:, j:j + 1], None, ALU.mult)
                steps.append(("DV", 1, vmask, [len(steps) - 1, i_ones]))
        b.sched(steps)

        groups = [list(range(3 * gi, min(3 * gi + 3, 32))) for gi in range(11)]
        for qg in range(4):
            steps = []
            idx_exp = {}
            idx_ctx = {}
            for gi, grp in enumerate(groups):
                def stmg(t, hp=hp, qg=qg, grp=grp):
                    ins = None
                    for j in grp:
                        bk = 2 * (j % 3)
                        t.matmul(pb[bk], kThp[0:64, 128 * j:128 * (j + 1)],
                                 qThp[0:64, 512 * qg:512 * (qg + 1)],
                                 start=True, stop=True)
                        ins = t.matmul(pb[bk + 1], kThp[64:128, 128 * j:128 * (j + 1)],
                                       qThp[64:128, 512 * qg:512 * (qg + 1)],
                                       start=True, stop=True)
                    return ins
                deps = [idx_exp[gi - 1]] if gi >= 1 else []
                steps.append(("PE", 1, stmg, deps))
                i_st = len(steps) - 1

                def expg(sc, gi=gi, grp=grp):
                    w = 1024 * len(grp)
                    return sc.activation(out=PT[:, 3072 * (gi % 2):3072 * (gi % 2) + w],
                                         in_=sb["PBIG"][:, 0:w],
                                         func=AF.Exp, scale=b.cv(0.125))
                deps = [i_st]
                if gi >= 2:
                    deps.append(idx_ctx[gi - 2])
                steps.append(("SA", 1, expg, deps))
                idx_exp[gi] = len(steps) - 1

                if gi >= 1:
                    def ctxg(t, gi=gi - 1, grp=groups[gi - 1]):
                        ins = None
                        for j in grp:
                            c0 = 3072 * (gi % 2) + 1024 * (j % 3)
                            t.matmul(pb[6][0:65, :], vvt[:, 130 * j:130 * j + 65],
                                     PT[:, c0:c0 + 512],
                                     start=(j == 0), stop=(j == 31))
                            ins = t.matmul(pb[7][0:65, :],
                                           vvt[:, 130 * j + 65:130 * (j + 1)],
                                           PT[:, c0 + 512:c0 + 1024],
                                           start=(j == 0), stop=(j == 31))
                        return ins
                    steps.append(("PE", 1, ctxg, [idx_exp[gi - 1]]))
                    idx_ctx[gi - 1] = len(steps) - 1
            gi_last = len(groups) - 1
            def ctxg_last(t, gi=gi_last, grp=groups[gi_last]):
                ins = None
                for j in grp:
                    c0 = 3072 * (gi % 2) + 1024 * (j % 3)
                    t.matmul(pb[6][0:65, :], vvt[:, 130 * j:130 * j + 65],
                             PT[:, c0:c0 + 512], start=(j == 0), stop=(j == 31))
                    ins = t.matmul(pb[7][0:65, :], vvt[:, 130 * j + 65:130 * (j + 1)],
                                   PT[:, c0 + 512:c0 + 1024],
                                   start=(j == 0), stop=(j == 31))
                return ins
            steps.append(("PE", 1, ctxg_last, [idx_exp[gi_last]]))
            idx_ctx[gi_last] = len(steps) - 1
            prev = {}
            for h2 in range(2):
                deps = [idx_ctx[gi_last]]
                if h2 == 1:
                    deps.append(prev["rdm"])
                steps.append(("DV", 1, (lambda v, h2=h2: v.reciprocal(
                    rd64[64:65, 0:512], pb[6 + h2][64:65, 0:512])), deps))
                i_rec = len(steps) - 1
                deps2 = [i_rec]
                if h2 == 1:
                    deps2.append(prev["rde"])
                steps.append(("PE", 1, (lambda t: t.matmul(
                    pb[2][0:64, 0:512], sb["kONESCOL"][64:65, 0:64],
                    rd64[64:65, 0:512], start=True, stop=True)), deps2))
                i_rdm = len(steps) - 1
                deps = [i_rdm]
                if h2 == 1:
                    deps.append(prev["cn"])
                steps.append(("SA", 1, (lambda sc: sc.copy(
                    out=rdsb[0:64, 0:512], in_=pb[2][0:64, 0:512])), deps))
                i_rde = len(steps) - 1
                deps = [i_rde]
                if h2 == 1:
                    deps.append(prev["store"])
                steps.append(("DV", 1, (lambda v, h2=h2: v.tensor_tensor(
                    ctxe[0:64, 0:512], pb[6 + h2][0:64, 0:512],
                    rdsb[0:64, 0:512], ALU.mult)), deps))
                i_cn = len(steps) - 1
                steps.append(("GP", 16, (lambda g, hp=hp, qg=qg, h2=h2: g.dma_start(
                    out=ap["ctxT_h"][2 * hp + h2, :, 512 * qg:512 * (qg + 1)],
                    in_=ctxe[0:64, 0:512])), [i_cn]))
                prev = {"rdm": i_rdm, "cn": i_cn, "rde": i_rde, "store": len(steps) - 1}
            b.sched(steps)
    es.close()

    es = contextlib.ExitStack()
    Wo = es.enter_context(nc.sbuf_tensor("p5Wo", [128, 8 * D], F32)).ap()
    cth = es.enter_context(nc.sbuf_tensor("p5cth", [128, 8 * T], F32)).ap()
    bor = es.enter_context(nc.sbuf_tensor("p5bor", [1, D], F32)).ap()
    x2c = es.enter_context(nc.sbuf_tensor("p5x2c", [128, D], F32)).ap()
    yb = es.enter_context(nc.sbuf_tensor("p5yb", [128, D], F32)).ap()
    tb = es.enter_context(nc.sbuf_tensor("p5tb", [128, D], F32)).ap()
    x3b = es.enter_context(nc.sbuf_tensor("p5x3b", [128, D], F32)).ap()
    steps = [("SY", 16, lambda s: s.dma_start(
        out=Wo, in_=ap["WoT1"].rearrange("(c p) o -> p c o", p=128))),
        ("SY", 16, lambda s: s.dma_start(
            out=bor, in_=ap["b1r"][0:1, 3 * D:4 * D]))]
    for hp in range(8):
        for h2 in range(2):
            def ldc(s, hp=hp, h2=h2):
                return s.dma_start(out=cth[64 * h2:64 * (h2 + 1),
                                           2048 * hp:2048 * (hp + 1)],
                                   in_=ap["ctxT_h"][2 * hp + h2])
            steps.append(("SY", 16, ldc))
    b.serial(steps)

    steps = []
    prev = {}
    for n in range(NT):
        i_mm = []
        for og in range(2):
            def womm(t, n=n, og=og):
                pairs = [(cth[:, 2048 * c + 128 * n: 2048 * c + 128 * n + 128],
                          Wo[:, D * c + 512 * og: D * c + 512 * og + 512])
                         for c in range(8)]
                pairs.append((sb["kONESROW"][0:1, 0:128],
                              bor[0:1, 512 * og:512 * og + 512]))
                return mm_accum(t, pb[og], pairs)
            deps = [prev[f"yadd{og}"]] if n >= 1 else []
            steps.append(("PE", 1, womm, deps))
            i_mm.append(len(steps) - 1)
        deps = [prev["x3add"]] if n >= 1 else []
        steps.append(("SY", 16, (lambda s, n=n: s.dma_start(
            out=x2c, in_=ap["x2_h"][128 * n:128 * (n + 1), :])), deps))
        i_x2 = len(steps) - 1
        deps = [i_x2] + ([prev["tread"]] if n >= 1 else [])
        steps.append(("SA", 1, (lambda sc, n=n: sc.activation(
            out=yb, in_=x2c, func=AF.Identity, scale=sb["rstd2"][:, n:n + 1],
            bias=sb["nmr2"][:, n:n + 1])), deps))
        i_ln = len(steps) - 1
        steps.append(("DV", 1, (lambda v: v.tensor_tensor(
            yb[:, 0:512], yb[:, 0:512], pb[0], ALU.add)), [i_mm[0], i_ln]))
        yadd0 = len(steps) - 1
        steps.append(("DV", 1, (lambda v: v.tensor_tensor(
            yb[:, 512:1024], yb[:, 512:1024], pb[1], ALU.add)), [i_mm[1], i_ln]))
        yadd1 = len(steps) - 1
        i_y = yadd1
        steps.append(("DV", 1, (lambda v: v.reduce_sum(
            sb["scr2"][:, 0:1], yb, axis=AX.X)), [i_y]))
        i_rs = len(steps) - 1
        steps.append(("SA", 1, (lambda sc: sc.activation(
            out=sb["scr"], in_=yb, func=AF.Square, accum_out=sb["scr2"][:, 1:2])), [i_y]))
        i_sq = len(steps) - 1
        steps.append(("SA", 1, (lambda sc: sc.activation(
            out=sb["scr2"][:, 2:3], in_=sb["scr2"][:, 0:1], func=AF.Square,
            scale=b.cv(1.0 / 32.0))), [i_rs]))
        i_t1 = len(steps) - 1
        steps.append(("DV", 1, (lambda v: v.tensor_tensor(
            sb["scr2"][:, 3:4], sb["scr2"][:, 1:2], sb["scr2"][:, 2:3],
            ALU.subtract)), [i_sq, i_t1]))
        i_t2 = len(steps) - 1
        steps.append(("SA", 1, (lambda sc: sc.activation(
            out=sb["scr2"][:, 4:5], in_=sb["scr2"][:, 3:4], func=AF.Ln,
            bias=b.cv(1024.0 * EPS))), [i_t2]))
        steps.append(("SA", 1, (lambda sc: sc.activation(
            out=sb["scr2"][:, 4:5], in_=sb["scr2"][:, 4:5], func=AF.Exp,
            scale=b.cv(-0.5), bias=b.cv(LN32))), [len(steps) - 1]))
        i_r = len(steps) - 1
        steps.append(("DV", 1, (lambda v: v.scalar_tensor_tensor(
            sb["scr2"][:, 5:6], sb["scr2"][:, 0:1], b.cv(-1.0 / 1024.0),
            sb["scr2"][:, 4:5], ALU.mult, ALU.mult)), [i_r]))
        i_nmr = len(steps) - 1
        steps.append(("SA", 1, (lambda sc: sc.activation(
            out=tb, in_=yb, func=AF.Identity, scale=sb["scr2"][:, 4:5],
            bias=sb["scr2"][:, 5:6])), [i_nmr]))
        i_t = len(steps) - 1
        prev["tread"] = i_t
        steps.append(("DV", 1, (lambda v: v.tensor_tensor(tb, tb, gmat, ALU.mult)),
                      [i_t]))
        steps.append(("DV", 1, (lambda v: v.tensor_tensor(tb, tb, bmat, ALU.add)),
                      [len(steps) - 1]))
        steps.append(("DV", 1, (lambda v: v.tensor_tensor(x3b, tb, x2c, ALU.add)),
                      [len(steps) - 1]))
        i_x3 = len(steps) - 1
        prev["x3add"] = i_x3
        prev["yadd0"] = yadd0
        prev["yadd1"] = yadd1
        steps.append(("GP", 16, (lambda g, n=n: g.dma_start(
            out=ap["x3_h"][128 * n:128 * (n + 1), :], in_=x3b)), [i_x3]))
        i_st3 = len(steps) - 1
        steps.append(("DV", 1, (lambda v: v.reduce_sum(
            sb["scr2"][:, 8:9], x3b, axis=AX.X)), [i_x3]))
        i_rs3 = len(steps) - 1
        steps.append(("SA", 1, (lambda sc: sc.activation(
            out=sb["scr"], in_=x3b, func=AF.Square, accum_out=sb["scr2"][:, 9:10])),
            [i_x3]))
        i_sq3 = len(steps) - 1
        steps.append(("SA", 1, (lambda sc: sc.activation(
            out=sb["scr2"][:, 10:11], in_=sb["scr2"][:, 8:9], func=AF.Square,
            scale=b.cv(1.0 / 32.0))), [i_rs3]))
        steps.append(("DV", 1, (lambda v: v.tensor_tensor(
            sb["scr2"][:, 11:12], sb["scr2"][:, 9:10], sb["scr2"][:, 10:11],
            ALU.subtract)), [i_sq3, len(steps) - 1]))
        i_t23 = len(steps) - 1
        steps.append(("SA", 1, (lambda sc: sc.activation(
            out=sb["scr2"][:, 12:13], in_=sb["scr2"][:, 11:12], func=AF.Ln,
            bias=b.cv(1024.0 * EPS))), [i_t23]))
        steps.append(("SA", 1, (lambda sc, n=n: sc.activation(
            out=sb["rstd3"][:, n:n + 1], in_=sb["scr2"][:, 12:13], func=AF.Exp,
            scale=b.cv(-0.5), bias=b.cv(LN32))), [len(steps) - 1]))
        i_r3 = len(steps) - 1
        steps.append(("DV", 1, (lambda v, n=n: v.scalar_tensor_tensor(
            sb["nmr3"][:, n:n + 1], sb["scr2"][:, 8:9], b.cv(-1.0 / 1024.0),
            sb["rstd3"][:, n:n + 1], ALU.mult, ALU.mult)), [i_r3]))
        i_nm3 = len(steps) - 1
        steps.append(("DV", 1, (lambda v: v.tensor_scalar(
            sb["scr2"][:, 13:14], sb["scr2"][:, 11:12], 1024.0 * EPS, None, ALU.add)),
            [i_t23]))
        steps.append(("DV", 1, (lambda v: v.reciprocal(
            sb["scr2"][:, 14:15], sb["scr2"][:, 13:14])), [len(steps) - 1]))
        steps.append(("DV", 1, (lambda v: v.tensor_tensor(
            sb["scr2"][:, 15:16], sb["scr2"][:, 11:12], sb["scr2"][:, 14:15],
            ALU.mult)), [len(steps) - 1]))
        steps.append(("DV", 1, (lambda v: v.tensor_scalar(
            sb["scr2"][:, 16:17], sb["scr2"][:, 15:16], EPS, None, ALU.add)),
            [len(steps) - 1]))
        steps.append(("SA", 1, (lambda sc: sc.activation(
            out=sb["scr2"][:, 17:18], in_=sb["scr2"][:, 16:17], func=AF.Ln)),
            [len(steps) - 1]))
        steps.append(("SA", 1, (lambda sc: sc.activation(
            out=sb["scr2"][:, 17:18], in_=sb["scr2"][:, 17:18], func=AF.Exp,
            scale=b.cv(-0.5))), [len(steps) - 1]))
        i_s3 = len(steps) - 1
        steps.append(("DV", 1, (lambda v, n=n: v.tensor_tensor(
            sb["sc3p"][:, n:n + 1], sb["rstd3"][:, n:n + 1], sb["scr2"][:, 17:18],
            ALU.mult)), [i_s3, i_r3]))
        steps.append(("DV", 1, (lambda v, n=n: v.tensor_tensor(
            sb["nb3p"][:, n:n + 1], sb["nmr3"][:, n:n + 1], sb["scr2"][:, 17:18],
            ALU.mult)), [i_s3, i_nm3]))
    b.sched(steps)
    es.close()
    esw.close()


def emit_p6(b):
    nc, block, ap, sb, pb = b.nc, b.block, b.ap, b.sb, b.pb
    es0 = contextlib.ExitStack()
    lngS = es0.enter_context(nc.sbuf_tensor("p6lng", [128, 8], F32)).ap()
    lnbS = es0.enter_context(nc.sbuf_tensor("p6lnb", [128, 8], F32)).ap()
    wc2 = es0.enter_context(nc.sbuf_tensor("p6wc2", [128, 16], F32)).ap()
    sI = es0.enter_context(nc.sbuf_tensor("p6sI", [128, 16], I32)).ap()
    tkf = es0.enter_context(nc.sbuf_tensor("p6tkf", [128, 16], F32)).ap()
    dspin = ap["cc_disp_in"].rearrange("(r c) -> r c", c=1032)
    dout8 = ap["cc_disp_out"].rearrange("(r c) -> r c", c=1032)
    rsin = ap["cc_rs_in"].rearrange("(r c) -> r c", c=D)

    esz = contextlib.ExitStack()
    zrow2 = esz.enter_context(nc.sbuf_tensor("p6zr", [128, D], F32)).ap()
    steps = [("DV", 1, (lambda v: v.memset(zrow2, 0.0)), [])]
    for z in range(129):
        r0 = 128 * z
        rows = min(128, 8 * T + 1 - r0)
        steps.append(("GP", 16, (lambda g, r0=r0, rows=rows: g.dma_start(
            out=rsin[r0:r0 + rows, :], in_=zrow2[0:rows, :])), [0]))
    b.sched(steps)

    es = contextlib.ExitStack()
    zb = [es.enter_context(nc.sbuf_tensor(f"p6z{i}", [128, D], F32)).ap() for i in range(2)]
    ztb = [es.enter_context(nc.sbuf_tensor(f"p6t{i}", [128, D], F32)).ap() for i in range(2)]
    x3c = [es.enter_context(nc.sbuf_tensor(f"p6x{i}", [128, D], F32)).ap() for i in range(2)]
    gws1 = es.enter_context(nc.sbuf_tensor("p6gw", [128, 64], F32)).ap()
    blg = es.enter_context(nc.sbuf_tensor("p6blg", [1, 8], F32)).ap()
    LG = es.enter_context(nc.sbuf_tensor("p6LG", [128, 128], F32)).ap()
    OH = es.enter_context(nc.sbuf_tensor("p6OH", [128, 128], F32)).ap()
    RK = es.enter_context(nc.sbuf_tensor("p6RK", [128, 128], F32)).ap()
    g16 = es.enter_context(nc.sbuf_tensor("p6g16", [128, 64], F32)).ap()
    wcol = es.enter_context(nc.sbuf_tensor("p6w", [128, 16], F32)).ap()
    bas = [es.enter_context(nc.sbuf_tensor(f"p6b{i}", [1, 8], F32)).ap() for i in range(2)]
    rowb = es.enter_context(nc.sbuf_tensor("p6row", [8, 2], F32)).ap()
    send = es.enter_context(nc.sbuf_tensor("p6send", [128, 1032], F32)).ap()
    crow = es.enter_context(nc.sbuf_tensor("p6crow", [1, 1032], F32)).ap()
    slotf = es.enter_context(nc.sbuf_tensor("p6sf", [128, 1], F32)).ap()
    sloti = es.enter_context(nc.sbuf_tensor("p6si", [128, 16], I32)).ap()

    b.serial([
        ("SY", 16, lambda s: s.dma_start(
            out=gws1.rearrange("p (c e) -> p c e", e=8),
            in_=ap["gwT1"].rearrange("(c p) e -> p c e", p=128))),
        ("SY", 16, lambda s: s.dma_start(out=lngS, in_=ap["lngc"][:, :])),
        ("SY", 16, lambda s: s.dma_start(out=lnbS, in_=ap["lnbc"][:, :])),
        ("PE", 1, lambda t: mm_accum(
            t, pb[0][0:1, 0:8],
            [(lnbS[:, c:c + 1], gws1[:, 8 * c:8 * c + 8]) for c in range(8)])),
        ("SA", 1, lambda sc: sc.copy(out=blg, in_=pb[0][0:1, 0:8])),
        ("DV", 1, lambda v: v.tensor_tensor(
            gws1.rearrange("p (c e) -> p c e", e=8),
            gws1.rearrange("p (c e) -> p c e", e=8),
            lngS.to_broadcast([128, 8, 8]), ALU.mult)),
        ("DV", 1, lambda v: v.memset(bas[0], 0.0)),
        ("DV", 1, lambda v: v.memset(send[:, 1026:1032], 0.0)),
        ("DV", 1, lambda v: v.memset(crow, 0.0)),
    ])

    S0 = b.snap()
    SAC, PEC = 10, 9

    @block.sync
    def _(s):
        b.wait_snap(s, S0)
        for n in range(NT):
            if n >= 2:
                s.wait_ge(b.SA.h, S0["SA"] + SAC * (n - 2) + 1)
            b.SY.inc(s.dma_start(out=x3c[n % 2],
                                 in_=ap["x3_h"][128 * n:128 * (n + 1), :]), 16)

    @block.scalar
    def _(sc):
        b.wait_snap(sc, S0)
        for n in range(NT):
            sc.wait_ge(b.SY.h, S0["SY"] + 16 * (n + 1))
            if n >= 2:
                sc.wait_ge(b.PE.h, S0["PE"] + PEC * (n - 1))
            b.SA.inc(sc.activation(out=zb[n % 2], in_=x3c[n % 2], func=AF.Identity,
                                   scale=sb["sc3p"][:, n:n + 1],
                                   bias=sb["nb3p"][:, n:n + 1]))
            for c in range(8):
                sc.wait_ge(b.PE.h, S0["PE"] + PEC * n + c + 1)
                b.SA.inc(sc.copy(out=ztb[n % 2][:, 128 * c:128 * (c + 1)],
                                 in_=pb[c % 2][:, 0:128]))
            sc.wait_ge(b.PE.h, S0["PE"] + PEC * n + 9)
            b.SA.inc(sc.copy(out=LG[:, 8 * n:8 * n + 8], in_=pb[2][:, 0:8]))

    @block.tensor
    def _(t):
        b.wait_snap(t, S0)
        for n in range(NT):
            t.wait_ge(b.SA.h, S0["SA"] + SAC * n + 1)
            for c in range(8):
                Tg = 8 * n + c
                if Tg >= 2:
                    pn, pc = (Tg - 2) // 8, (Tg - 2) % 8
                    t.wait_ge(b.SA.h, S0["SA"] + SAC * pn + 1 + pc + 1)
                b.PE.inc(t.transpose(pb[c % 2][:, 0:128],
                                     zb[n % 2][:, 128 * c:128 * (c + 1)],
                                     sb["kIDENT"]))
            t.wait_ge(b.SA.h, S0["SA"] + SAC * n + 9)
            if n >= 1:
                t.wait_ge(b.SA.h, S0["SA"] + SAC * (n - 1) + 10)
            pairs = [(ztb[n % 2][:, 128 * c:128 * (c + 1)], gws1[:, 8 * c:8 * c + 8])
                     for c in range(8)]
            pairs.append((sb["kONESROW"][0:1, 0:128], blg[0:1, 0:8]))
            b.PE.inc(mm_accum(t, pb[2][:, 0:8], pairs))

    L3 = LG.rearrange("p (n e) -> p n e", e=8)
    steps = [
        ("DV", 1, lambda v: v.tensor_reduce(g16[:, 0:16], L3, axis=AX.X, op=ALU.max)),
        ("DV", 1, lambda v: v.tensor_tensor(
            OH.rearrange("p (n e) -> p n e", e=8), L3,
            g16[:, 0:16].to_broadcast([128, 16, 8]), ALU.is_ge)),
        ("DV", 1, lambda v: v.tensor_tensor(
            RK.rearrange("p (n e) -> p n e", e=8), L3,
            g16[:, 0:16].to_broadcast([128, 16, 8]), ALU.subtract)),
        ("SA", 1, lambda sc: sc.activation(out=RK, in_=RK, func=AF.Exp)),
        ("DV", 1, lambda v: v.tensor_reduce(g16[:, 16:32],
                                            RK.rearrange("p (n e) -> p n e", e=8),
                                            axis=AX.X, op=ALU.add)),
        ("DV", 1, lambda v: v.reciprocal(wcol, g16[:, 16:32])),
    ]
    for n in range(NT):
        pa = n % 2
        def cummm(t, n=n, pa=pa):
            t.matmul(pb[4][:, 0:8], sb["kSTRICTLOW"], OH[:, 8 * n:8 * n + 8],
                     start=True, stop=False)
            return t.matmul(pb[4][:, 0:8], sb["kONESROW"][0:1, 0:128],
                            bas[pa][0:1, 0:8], start=False, stop=True)
        steps.append(("PE", 1, cummm))
        steps.append(("SA", 1, (lambda sc, n=n: sc.copy(
            out=RK[:, 8 * n:8 * n + 8], in_=pb[4][:, 0:8]))))
        def totmm(t, n=n):
            return t.matmul(pb[5][0:1, 0:8], sb["kONESCOL"][:, 0:1],
                            OH[:, 8 * n:8 * n + 8], start=True, stop=True)
        steps.append(("PE", 1, totmm))
        steps.append(("DV", 1, (lambda v, pa=pa: v.scalar_tensor_tensor(
            bas[1 - pa][0:1, 0:8], pb[5][0:1, 0:8], b.cv(1.0, 1),
            bas[pa][0:1, 0:8], ALU.mult, ALU.add))))
    basf = bas[NT % 2]
    steps += [
        ("PE", 1, (lambda t: t.transpose(pb[6][0:8, 0:1], basf[0:1, 0:8],
                                         sb["kIDENT"][0:1, 0:1]))),
        ("SA", 1, (lambda sc: sc.copy(out=rowb[:, 0:1], in_=pb[6][0:8, 0:1]))),
        ("PE", 1, (lambda t: t.matmul(pb[6][0:8, 8:9], sb["kSTRICTLOW8"],
                                      rowb[:, 0:1], start=True, stop=True))),
        ("SA", 1, (lambda sc: sc.copy(out=rowb[:, 1:2], in_=pb[6][0:8, 8:9]))),
        ("PE", 1, (lambda t: t.transpose(pb[6][0:1, 16:24], rowb[:, 1:2],
                                         sb["kIDENT"][0:8, 0:8]))),
        ("SA", 1, (lambda sc: sc.copy(out=crow[0:1, 8:16], in_=pb[6][0:1, 16:24]))),
        ("PE", 1, (lambda t: t.matmul(pb[6][:, 24:32], sb["kONESROW"][0:1, 0:128],
                                      crow[0:1, 8:16], start=True, stop=True))),
        ("SA", 1, (lambda sc: sc.copy(out=g16[:, 32:40], in_=pb[6][:, 24:32]))),
        ("DV", 1, (lambda v: v.tensor_copy(crow[0:1, 0:8], basf[0:1, 0:8]))),
        ("DV", 1, (lambda v: v.memset(crow[0:1, 1025:1026], 16384.0))),
        ("GP", 16, (lambda g: g.dma_start(out=dspin[T:T + 1, :], in_=crow))),
    ]
    for n in range(NT):
        steps.append(("DV", 1, (lambda v, n=n: v.tensor_tensor(
            g16[:, 40:48], RK[:, 8 * n:8 * n + 8], g16[:, 32:40], ALU.add))))
        steps.append(("DV", 1, (lambda v, n=n: v.tensor_tensor(
            g16[:, 48:56], g16[:, 40:48], OH[:, 8 * n:8 * n + 8], ALU.mult))))
        steps.append(("DV", 1, (lambda v: v.reduce_sum(
            slotf, g16[:, 48:56], axis=AX.X))))
        steps.append(("DV", 1, (lambda v, n=n: v.tensor_copy(
            sloti[:, n:n + 1], slotf))))
        steps.append(("SY", 16, (lambda s, n=n: s.dma_start(
            out=send[:, 0:1024], in_=ap["x3_h"][128 * n:128 * (n + 1), :]))))
        steps.append(("SA", 1, (lambda sc, n=n: sc.activation(
            out=send[:, 0:1024], in_=send[:, 0:1024], func=AF.Identity,
            scale=sb["sc3p"][:, n:n + 1], bias=sb["nb3p"][:, n:n + 1]))))
        steps.append(("DV", 1, (lambda v, n=n: v.tensor_copy(
            send[:, 1024:1025], wcol[:, n:n + 1]))))
        steps.append(("DV", 1, (lambda v, n=n: v.tensor_scalar(
            send[:, 1025:1026], sb["kSLOTF"][:, n:n + 1], sb["metaPs"][:, 0:1],
            None, ALU.add))))
        steps.append(("GP", 16, (lambda g, n=n: g.indirect_dma_start(
            dspin, IndirectOffsetOnAxis(ap=sloti[:, n:n + 1], axis=0),
            send, None))))
    if b.debug:
        for nm, src_ap in [("d_LG", LG), ("d_OH", OH), ("d_RK", RK),
                           ("d_wcol", wcol), ("d_crow", crow)]:
            steps.append(("GP", 16, (lambda g, nm=nm, src_ap=src_ap: g.dma_start(
                out=ap[nm][:, :], in_=src_ap))))
    b.serial(steps)
    es.close()

    b.serial([("CC", 1, lambda g: g.collective_compute(
        "AllGather", ALU.bypass, replica_groups=[list(range(8))],
        ins=[ap["cc_disp_in"][:]], outs=[ap["cc_disp_out"][:]]))])

    es = contextlib.ExitStack()
    cnts = es.enter_context(nc.sbuf_tensor("p6cn", [8, 32], F32)).ap()
    colv = es.enter_context(nc.sbuf_tensor("p6cv", [8, 16], F32)).ap()
    bmats = es.enter_context(nc.sbuf_tensor("p6bm", [128, 24], F32)).ap()
    ixf = [es.enter_context(nc.sbuf_tensor(f"p6ix{i}", [128, 16], F32)).ap()
           for i in range(2)]
    gidx = es.enter_context(nc.sbuf_tensor("p6gx", [128, 16], I32)).ap()
    msk = es.enter_context(nc.sbuf_tensor("p6mk", [128, 16], F32)).ap()

    steps = [
        ("SY", 16, lambda s: s.dma_start(
            out=cnts[:, 0:16],
            in_=ap["cc_disp_out"].rearrange("(r x) -> r x", x=2049 * 1032)
            [:, 2048 * 1032:2048 * 1032 + 16])),
        ("DV", 1, lambda v: v.tensor_tensor(cnts[:, 16:24], cnts[:, 0:8],
                                            sb["oh8s"], ALU.mult)),
        ("DV", 1, lambda v: v.reduce_sum(colv[:, 0:1], cnts[:, 16:24], axis=AX.X)),
        ("DV", 1, lambda v: v.tensor_tensor(cnts[:, 16:24], cnts[:, 8:16],
                                            sb["oh8s"], ALU.mult)),
        ("DV", 1, lambda v: v.reduce_sum(colv[:, 1:2], cnts[:, 16:24], axis=AX.X)),
        ("PE", 1, (lambda t: t.matmul(pb[0][0:8, 0:1], sb["kSTRICTLOW8"],
                                      colv[:, 0:1], start=True, stop=True))),
        ("SA", 1, (lambda sc: sc.copy(out=colv[:, 2:3], in_=pb[0][0:8, 0:1]))),
        ("DV", 1, (lambda v: v.tensor_tensor(colv[:, 3:4], colv[:, 2:3],
                                             colv[:, 0:1], ALU.add))),
        ("DV", 1, (lambda v: v.tensor_tensor(colv[:, 4:5], sb["kR2049"],
                                             colv[:, 1:2], ALU.add))),
        ("DV", 1, (lambda v: v.tensor_tensor(colv[:, 5:6], colv[:, 4:5],
                                             colv[:, 2:3], ALU.subtract))),
        ("SY", 16, (lambda s: s.dma_start(out=ap["cc_rs_out"][0:8],
                                          in_=colv[:, 5:6]))),
        ("SY", 16, (lambda s: s.dma_start(out=colv[0:7, 6:7],
                                          in_=ap["cc_rs_out"][1:8],
                                          allow_slow_non_contiguous=True))),
        ("DV", 1, (lambda v: v.tensor_tensor(colv[0:7, 7:8], colv[0:7, 6:7],
                                             colv[0:7, 5:6], ALU.subtract))),
        ("DV", 1, (lambda v: v.tensor_scalar(cnts[:, 24:32], sb["kDIAG8"],
                                             colv[:, 3:4], None, ALU.mult))),
        ("PE", 1, (lambda t: t.matmul(pb[0][:, 0:8], sb["kONES16x128"][0:8, :],
                                      cnts[:, 24:32], start=True, stop=True))),
        ("SA", 1, (lambda sc: sc.copy(out=bmats[:, 0:8], in_=pb[0][:, 0:8]))),
        ("DV", 1, (lambda v: v.tensor_scalar(cnts[:, 24:32], sb["kDIAG8"],
                                             colv[0:8, 7:8], None, ALU.mult))),
        ("PE", 1, (lambda t: t.matmul(pb[0][:, 8:16], sb["kONES16x128"][0:8, :],
                                      cnts[:, 24:32], start=True, stop=True))),
        ("SA", 1, (lambda sc: sc.copy(out=bmats[:, 8:16], in_=pb[0][:, 8:16]))),
        ("DV", 1, (lambda v: v.tensor_scalar(cnts[:, 24:32], sb["kDIAG8"],
                                             colv[:, 5:6], None, ALU.mult))),
        ("PE", 1, (lambda t: t.matmul(pb[0][:, 16:24], sb["kONES16x128"][0:8, :],
                                      cnts[:, 24:32], start=True, stop=True))),
        ("SA", 1, (lambda sc: sc.copy(out=bmats[:, 16:24], in_=pb[0][:, 16:24]))),
        ("DV", 1, (lambda v: v.tensor_scalar(ixf[0], sb["kSLOTF"],
                                             bmats[:, 16:17], None, ALU.add))),
    ]
    cur = 0
    for r in range(7):
        steps.append(("DV", 1, (lambda v, r=r: v.tensor_scalar(
            msk, sb["kSLOTF"], bmats[:, r:r + 1], None, ALU.is_ge))))
        steps.append(("DV", 1, (lambda v, r=r: v.tensor_scalar(
            msk, msk, bmats[:, 8 + r:8 + r + 1], None, ALU.mult))))
        steps.append(("DV", 1, (lambda v, cur=cur: v.tensor_tensor(
            ixf[1 - cur], ixf[cur], msk, ALU.add))))
        cur = 1 - cur
    steps += [
        ("DV", 1, (lambda v: v.tensor_scalar(
            msk, sb["kSLOTF"], bmats[:, 7:8], None, ALU.is_ge))),
        ("DV", 1, (lambda v, cur=cur: v.tensor_tensor(
            ixf[1 - cur], msk, ixf[cur], ALU.mult))),
        ("DV", 1, (lambda v, cur=cur: v.tensor_tensor(
            ixf[1 - cur], ixf[cur], ixf[1 - cur], ALU.subtract))),
        ("DV", 1, (lambda v, cur=cur: v.scalar_tensor_tensor(
            ixf[cur], msk, b.cv(2048.0), ixf[1 - cur], ALU.mult, ALU.add))),
        ("DV", 1, (lambda v, cur=cur: v.tensor_copy(gidx, ixf[cur]))),
    ]
    if b.debug:
        steps.append(("GP", 16, (lambda g, cur=cur: g.dma_start(
            out=ap["d_ixf"][:, :], in_=ixf[cur]))))
    b.serial(steps)

    dT = es.enter_context(nc.sbuf_tensor("p6dT", [128, 8 * T], F32)).ap()
    dsp = es.enter_context(nc.sbuf_tensor("p6dsp", [128, 1032], F32)).ap()
    steps = []
    prev_g = None
    for n in range(NT):
        deps = [prev_g] if prev_g is not None else []
        steps.append(("GP", 16, (lambda g, n=n: g.indirect_dma_start(
            dsp, None, dout8,
            IndirectOffsetOnAxis(ap=gidx[:, n:n + 1], axis=0))), deps))
        i_ga = len(steps) - 1
        steps.append(("DV", 1, (lambda v, n=n: v.tensor_copy(
            wc2[:, n:n + 1], dsp[:, 1024:1025])), [i_ga]))
        steps.append(("DV", 1, (lambda v, n=n: v.tensor_copy(
            sI[:, n:n + 1], dsp[:, 1025:1026])), [i_ga]))
        if b.debug:
            steps.append(("DV", 1, (lambda v, n=n: v.tensor_copy(
                tkf[:, n:n + 1], dsp[:, 1025:1026])), [i_ga]))
        for c in range(8):
            deps = [i_ga]
            if c >= 2:
                deps.append(len(steps) - 3)
            steps.append(("PE", 1, (lambda t, c=c: t.transpose(
                pb[c % 2][:, 0:128], dsp[:, 128 * c:128 * (c + 1)],
                sb["kIDENT"])), deps))
            steps.append(("SA", 1, (lambda sc, n=n, c=c: sc.activation(
                out=dT[:, 2048 * c + 128 * n: 2048 * c + 128 * n + 128],
                in_=pb[c % 2][:, 0:128], func=AF.Identity,
                scale=lngS[:, c:c + 1], bias=lnbS[:, c:c + 1])),
                [len(steps) - 1]))
        prev_g = len(steps) - 1
    b.sched(steps)
    if b.debug:
        b.serial([
            ("GP", 16, lambda g: g.dma_start(out=ap["d_wc2"][:, :], in_=wc2)),
            ("GP", 16, lambda g: g.dma_start(out=ap["d_tkf"][:, :], in_=tkf)),
            ("GP", 16, lambda g: g.dma_start(out=ap["d_dT"][:, :], in_=dT)),
        ])

    wib = [es.enter_context(nc.sbuf_tensor(f"p6wi{i}", [128, 1024], F32)).ap()
           for i in range(2)]
    gb = es.enter_context(nc.sbuf_tensor("p6gb", [128, 2048], F32)).ap()
    steps = []
    last_store = {}
    last_evac = {}
    last_mm = {}
    for io in range(32):
        deps = [last_store[io - 2]] if io >= 2 else []
        def wload(s, io=io):
            return s.dma_start(
                out=wib[io % 2].rearrange("p (a q) -> p a q", q=128),
                in_=ap["wi_e"].rearrange("(c p) (a q) -> p c a q", p=128, q=128)
                [:, :, io, :])
        steps.append(("SY", 16, wload, deps))
        i_w = len(steps) - 1
        for tg in range(4):
            def gmm(t, io=io, tg=tg):
                pairs = [(wib[io % 2][:, 128 * c:128 * (c + 1)],
                          dT[:, 2048 * c + 512 * tg: 2048 * c + 512 * tg + 512])
                         for c in range(8)]
                return mm_accum(t, pb[4 + tg], pairs)
            deps = [i_w]
            if (io - 1, tg) in last_evac:
                deps.append(last_evac[(io - 1, tg)])
            steps.append(("PE", 1, gmm, deps))
            last_mm[(io, tg)] = len(steps) - 1
            deps = [len(steps) - 1]
            if (io - 1, tg) in last_store:
                deps.append(last_store[(io - 1, tg)])
            steps.append(("SA", 1, (lambda sc, tg=tg: sc.activation(
                out=gb[:, 512 * tg:512 * (tg + 1)], in_=pb[4 + tg],
                func=AF.Gelu)), deps))
            last_evac[(io, tg)] = len(steps) - 1
            steps.append(("GP", 16, (lambda g, io=io, tg=tg: g.dma_start(
                out=ap["g_h"][io, :, 512 * tg:512 * (tg + 1)],
                in_=gb[:, 512 * tg:512 * (tg + 1)])), [len(steps) - 1]))
            last_store[(io, tg)] = len(steps) - 1
            last_store[io] = len(steps) - 1
    b.sched(steps)
    es.close()

    es = contextlib.ExitStack()
    Wo2 = es.enter_context(nc.sbuf_tensor("p6Wo", [128, 32 * D], F32)).ap()
    gcb = [es.enter_context(nc.sbuf_tensor(f"p6gc{i}", [128, 4096], F32)).ap()
           for i in range(2)]
    ot = es.enter_context(nc.sbuf_tensor("p6ot", [128, D], F32)).ap()
    b.serial([("SY", 16, lambda s: s.dma_start(
        out=Wo2.rearrange("p (c o) -> p c o", o=D),
        in_=ap["wo_e"].rearrange("(c p) o -> p c o", p=128)))])
    steps = []
    prev = {}
    for n in range(NT):
        deps = [prev["sc", n - 2]] if n >= 2 else []
        steps.append(("SY", 16, (lambda s, n=n: s.dma_start(
            out=gcb[n % 2].rearrange("p (c q) -> p c q", q=128),
            in_=ap["g_h"][:, :, 128 * n:128 * (n + 1)]
            .rearrange("c p q -> p c q"))), deps))
        i_ld = len(steps) - 1
        i_mm = []
        for og in range(2):
            def omm(t, n=n, og=og):
                pairs = [(gcb[n % 2][:, 128 * c:128 * (c + 1)],
                          Wo2[:, D * c + 512 * og: D * c + 512 * og + 512])
                         for c in range(32)]
                return mm_accum(t, pb[og], pairs)
            deps = [i_ld]
            if n >= 1:
                deps.append(prev["ev", n - 1])
            steps.append(("PE", 1, omm, deps))
            i_mm.append(len(steps) - 1)
        def oev(sc, n=n):
            sc.activation(out=ot[:, 0:512], in_=pb[0], func=AF.Identity,
                          scale=wc2[:, n:n + 1])
            return sc.activation(out=ot[:, 512:1024], in_=pb[1],
                                 func=AF.Identity, scale=wc2[:, n:n + 1])
        deps = list(i_mm)
        if n >= 1:
            deps.append(prev["sc", n - 1])
        steps.append(("SA", 2, oev, deps))
        prev["ev", n] = len(steps) - 1
        steps.append(("GP", 16, (lambda g, n=n: g.indirect_dma_start(
            rsin, IndirectOffsetOnAxis(ap=sI[:, n:n + 1], axis=0),
            ot, None)), [len(steps) - 1]))
        prev["sc", n] = len(steps) - 1
    b.sched(steps)
    es.close()

    b.serial([("CC", 1, lambda g: g.collective_compute(
        "ReduceScatter", ALU.add, replica_groups=[list(range(8))],
        ins=[ap["cc_rs_in"][0:8 * T * D]], outs=[ap["cc_rs_out"][:]]))])

    esf = contextlib.ExitStack()
    xa = [esf.enter_context(nc.sbuf_tensor(f"p6xa{i}", [128, D], F32)).ap()
          for i in range(2)]
    ra = [esf.enter_context(nc.sbuf_tensor(f"p6ra{i}", [128, D], F32)).ap()
          for i in range(2)]
    za = [esf.enter_context(nc.sbuf_tensor(f"p6za{i}", [128, D], F32)).ap()
          for i in range(2)]
    rso = ap["cc_rs_out"].rearrange("(t d) -> t d", d=D)
    steps = []
    prev = {}
    for n in range(NT):
        deps = [prev[n - 2]] if n >= 2 else []
        steps.append(("SY", 16, (lambda s, n=n: s.dma_start(
            out=xa[n % 2], in_=ap["x3_h"][128 * n:128 * (n + 1), :])), deps))
        i_x = len(steps) - 1
        steps.append(("SY", 16, (lambda s, n=n: s.dma_start(
            out=ra[n % 2], in_=rso[128 * n:128 * (n + 1), :])), deps))
        i_r = len(steps) - 1
        steps.append(("SA", 1, (lambda sc, n=n: sc.activation(
            out=za[n % 2], in_=xa[n % 2], func=AF.Identity,
            scale=sb["rstd3"][:, n:n + 1], bias=sb["nmr3"][:, n:n + 1])), [i_x]))
        steps.append(("DV", 1, (lambda v, n=n: v.tensor_tensor(
            za[n % 2], za[n % 2], xa[n % 2], ALU.add)), [len(steps) - 1]))
        steps.append(("DV", 1, (lambda v, n=n: v.tensor_tensor(
            za[n % 2], za[n % 2], ra[n % 2], ALU.add)), [len(steps) - 1, i_r]))
        steps.append(("GP", 16, (lambda g, n=n: g.dma_start(
            out=ap["x4_out"][128 * n:128 * (n + 1), :], in_=za[n % 2])),
            [len(steps) - 1]))
        prev[n] = len(steps) - 1
    b.sched(steps)
    esf.close()
    esz.close()
    es0.close()


def emit_final(b, upto):
    nc, block, ap = b.nc, b.block, b.ap
    S0 = b.snap()
    src = {2: "x1_h", 4: "x2_h", 5: "x3_h"}.get(upto, "x1_h")

    @block.gpsimd
    def _(g):
        b.wait_snap(g, S0)
        if upto < 6:
            b.GP.inc(g.dma_start(out=ap["x4_out"][:, :], in_=ap[src][:, :]), 16)
        if b.debug:
            b.GP.inc(g.dma_start(out=ap["d_x1"][:, :], in_=ap["x1_h"][:, :]), 16)
            b.GP.inc(g.dma_start(out=ap["d_xln0T"][:, :, :, :], in_=ap["xln0T_h"][:, :, :, :]), 16)
            b.GP.inc(g.dma_start(out=ap["d_beta"][:, :], in_=ap["beta_h"][:, :]), 16)
            b.GP.inc(g.dma_start(out=ap["d_qspill"][:, :, :], in_=ap["qspill_h"][:, :, :]), 16)
            b.GP.inc(g.dma_start(out=ap["d_delta"][:, :, :], in_=ap["delta_h"][:, :, :]), 16)
            for nm, key in [("lmsum", "d_lmsum"), ("bmean", "d_bmean"),
                            ("curve", "d_curve"), ("icurve", "d_icurve"),
                            ("stateAll", "d_state"), ("Sinh", "d_sinh"),
                            ("cumdec", "d_cumdec"), ("stats", "d_stats")]:
                b.GP.inc(g.dma_start(out=ap[key][:, :] if ap[key].shape[0] != NT else ap[key],
                                     in_=b.sb[nm]), 16)
            if upto >= 4:
                b.GP.inc(g.dma_start(out=ap["d_x2"][:, :], in_=ap["x2_h"][:, :]), 16)
            if upto >= 5:
                b.GP.inc(g.dma_start(out=ap["d_x3"][:, :], in_=ap["x3_h"][:, :]), 16)
            if upto >= 6:
                b.GP.inc(g.dma_start(out=ap["d_rs"][:, :],
                                     in_=ap["cc_rs_out"].rearrange("(t d) -> t d", d=D)), 16)
        g.wait_ge(b.GP.h, b.GP.n)


def prep_inputs(inputs, core):
    z = {k: np.ascontiguousarray(np.asarray(v, dtype=np.float32)) for k, v in inputs.items()}
    d = {}
    x = z["L0_attn_x"].reshape(8, T, D)
    d["x_in"] = x[core]
    d["WqT0"] = z["L0_attn_Wq"].T
    d["WkT0"] = z["L0_attn_Wk"].T
    d["WvT0"] = z["L0_attn_Wv"].T
    d["WbT0"] = z["L0_attn_Wbeta"].T
    d["WaT0"] = z["L0_attn_Walpha"].T
    b0 = np.stack([z["L0_attn_bq"], z["L0_attn_bk"], z["L0_attn_bv"],
                   z["L0_attn_bbeta"], z["L0_attn_balpha"]], 0)
    d["b0r"] = b0[2:5].reshape(1, 3 * D)
    d["b0c"] = b0.reshape(5, 8, 128).transpose(2, 0, 1).reshape(128, 40)
    d["gwT0"] = z["L0_ffn_gate_w"].T
    d["up_w"] = z["L0_ffn_up_w"]
    d["up_bc"] = z["L0_ffn_up_b"].reshape(8, 8, 128).transpose(2, 0, 1).reshape(128, 64)
    d["down_w"] = z["L0_ffn_down_w"]
    d["down_br"] = z["L0_ffn_down_b"].reshape(1, 8 * D)
    d["sh_up_w"] = z["L0_ffn_sh_up_w"]
    d["sh_up_bc"] = z["L0_ffn_sh_up_b"].reshape(8, 128).T.copy()
    d["sh_down_w"] = z["L0_ffn_sh_down_w"]
    d["sh_down_br"] = z["L0_ffn_sh_down_b"].reshape(1, D)
    d["band"] = z["L1_attn_band_importance"]
    d["WqT1"] = z["L1_attn_Wq"].T
    d["WkT1"] = z["L1_attn_Wk"].T
    d["WvT1"] = z["L1_attn_Wv"].T
    d["WoT1"] = z["L1_attn_Wo"].T
    b1 = np.stack([z["L1_attn_bq"], z["L1_attn_bk"], z["L1_attn_bv"], z["L1_attn_bo"]], 0)
    d["b1r"] = b1.reshape(1, 4 * D)
    d["b1c"] = b1.reshape(4, 8, 128).transpose(2, 0, 1).reshape(128, 32)
    d["g1r"] = z["L1_attn_gamma"].reshape(1, D)
    d["be1r"] = z["L1_attn_beta"].reshape(1, D)
    d["gwT1"] = z["L1_ffn_gate_weight"].T
    d["lngc"] = z["L1_ffn_ln_gamma"].reshape(8, 128).T.copy()
    d["lnbc"] = z["L1_ffn_ln_beta"].reshape(8, 128).T.copy()
    d["wi_e"] = z["L1_ffn_wi"][core]
    d["wo_e"] = z["L1_ffn_wo"][core]
    meta = np.zeros((128, 8), np.float32)
    meta[:, 0] = core * T
    meta[:, 1] = core % 2
    meta[:, 2] = 1.0 if (core % 2 == 1) else 0.0
    meta[:, 3] = 0.0
    d["metaP"] = meta
    oh = np.zeros((8, 8), np.float32)
    oh[:, core] = 1.0
    d["oh8"] = oh
    return d


def kernel(**inputs):
    nc = build_program(debug=False, upto=BUILD_UPTO)
    in_maps = [prep_inputs(inputs, c) for c in range(8)]
    res = run_bass_kernel_spmd(nc, in_maps, core_ids=list(range(8)))
    out = np.concatenate([res.results[c]["x4_out"] for c in range(8)], axis=0)
    return out.reshape(4, 4096, 1024)
